# version 09
# baseline (speedup 1.0000x reference)
import numpy as np
from contextlib import ExitStack
import concourse.bass as bass
import concourse.mybir as mybir
from concourse.bass_utils import run_bass_kernel_spmd

F32 = mybir.dt.float32
BF16 = mybir.dt.bfloat16
I32 = mybir.dt.int32
AF = mybir.ActivationFunctionType
ALU = mybir.AluOpType
AX = mybir.AxisListType.X

D = 1024
TC = 2048
S = 8192
B = 2
H = 16
DH = 64
E = 8
FF = 4096
HID = 3072
DLAT = 256
NCH = D // 128



def _decl_l0_moe(nc):
    DD = {}
    NP = FF // 128
    DD["xn"] = nc.dram_tensor("xn", [D, TC], BF16, kind="ExternalInput").ap()
    DD["xnf"] = nc.dram_tensor("xnf", [D, TC], F32, kind="ExternalInput").ap()
    DD["w1"] = nc.dram_tensor("w1", [E, D, FF], F32, kind="ExternalInput").ap()
    DD["w2"] = nc.dram_tensor("w2", [E, FF, D], F32, kind="ExternalInput").ap()
    DD["sw1"] = nc.dram_tensor("sw1", [D, FF], F32, kind="ExternalInput").ap()
    DD["sw2"] = nc.dram_tensor("sw2", [FF, D], F32, kind="ExternalInput").ap()
    DD["b1t"] = nc.dram_tensor("b1t", [E, 128, NP], F32, kind="ExternalInput").ap()
    DD["sb1t"] = nc.dram_tensor("sb1t", [128, NP], F32, kind="ExternalInput").ap()
    DD["b2"] = nc.dram_tensor("b2", [E, D], F32, kind="ExternalInput").ap()
    DD["sb2"] = nc.dram_tensor("sb2", [1, D], F32, kind="ExternalInput").ap()
    DD["gwt"] = nc.dram_tensor("gwt", [D, E], F32, kind="ExternalInput").ap()
    DD["gb"] = nc.dram_tensor("gb", [E, 1], F32, kind="ExternalInput").ap()
    DD["g"] = nc.dram_tensor("g", [D, TC], BF16, kind="ExternalOutput").ap()
    return DD


def emit_l0_moe(nc, DD, n_tt=4, n_e=9):
    TT = 512
    NP = FF // 128
    NMISC = n_e + NCH
    xn_d = DD["xn"]
    xnf_d = DD["xnf"]
    w1_d = DD["w1"]
    w2_d = DD["w2"]
    sw1_d = DD["sw1"]
    sw2_d = DD["sw2"]
    b1t_d = DD["b1t"]
    sb1t_d = DD["sb1t"]
    b2_d = DD["b2"]
    sb2_d = DD["sb2"]
    gwt_d = DD["gwt"]
    gb_d = DD["gb"]
    g_d = DD["g"]
    with ExitStack() as ctx:
        ent = ctx.enter_context
        xnsb = ent(nc.sbuf_tensor("xnsb", [128, 2 * NCH * 512], BF16))
        w1b = ent(nc.sbuf_tensor("w1b", [128, 2 * NCH * 512], BF16))
        w2b = ent(nc.sbuf_tensor("w2b", [128, 3 * 16384], BF16))
        hdn = ent(nc.sbuf_tensor("hdn", [128, NP * TT], BF16))
        _rt_off = nc.sbuf_base
        routed = ent(nc.sbuf_tensor("routed", [128, NCH * TT], F32))
        tmpy = ent(nc.sbuf_tensor("tmpy", [128, TT], F32))
        combbc = ent(nc.sbuf_tensor("combbc", [128, 8 * TT], BF16))
        gstg = ent(nc.sbuf_tensor("gstg", [128, TT], BF16))
        comb9 = ent(nc.sbuf_tensor("comb9", [9, TT], F32))
        scf = ent(nc.sbuf_tensor("scf", [8, n_tt * TT], F32))
        xnf2 = nc.alloc_sbuf_tensor_at("xnf2", [128, TC], F32, offset=_rt_off)
        gwtf = ent(nc.sbuf_tensor("gwtf", [128, NCH * E], F32))
        sctm = ent(nc.sbuf_tensor("sctm", [128, 8], F32))
        m1 = ent(nc.sbuf_tensor("m1", [128, 1], F32))
        m2 = ent(nc.sbuf_tensor("m2", [128, 1], F32))
        srec = ent(nc.sbuf_tensor("srec", [128, 1], F32))
        ns1 = ent(nc.sbuf_tensor("ns1", [128, 1], F32))
        ns2 = ent(nc.sbuf_tensor("ns2", [128, 1], F32))
        eq1 = ent(nc.sbuf_tensor("eq1", [128, 8], F32))
        eq2 = ent(nc.sbuf_tensor("eq2", [128, 8], F32))
        msk = ent(nc.sbuf_tensor("msk", [128, 8], F32))
        combtm = ent(nc.sbuf_tensor("combtm", [128, 8], F32))
        b29 = ent(nc.sbuf_tensor("b29", [9, D], F32))
        b1sb = ent(nc.sbuf_tensor("b1sb", [128, 9 * NP], F32))
        gb = ent(nc.sbuf_tensor("gbsb", [E, 1], F32))
        sel = ent(nc.sbuf_tensor("sel", [9, 9 * 128], F32))
        idf = ent(nc.sbuf_tensor("idf", [128, 128], F32))
        ps = ent(nc.psum_tensor("ps", [128, 4096], F32))
        if "w1scr" in DD:
            w1scr, w2scr = DD["w1scr"], DD["w2scr"]
        else:
            w1scr = nc.dram_tensor("moe_w1scr", [n_e * 8, 128, NCH * 512], BF16,
                                   kind="Internal").ap()
            w2scr = nc.dram_tensor("moe_w2scr", [n_e, 2, 128, 32 * 512], BF16,
                                   kind="Internal").ap()
        s_init = ent(nc.semaphore("s_init"))
        s_w1wr = ent(nc.semaphore("s_w1wr"))
        s_w2wr = ent(nc.semaphore("s_w2wr"))
        s_w1s = [ent(nc.semaphore(f"s_w1s{i}")) for i in range(2)]
        s_w1f = ent(nc.semaphore("s_w1f"))
        s_w2sl = [ent(nc.semaphore(f"s_w2sl{i}")) for i in range(3)]
        s_w2f = ent(nc.semaphore("s_w2f"))
        s_sig = ent(nc.semaphore("s_sig"))
        s_tr = ent(nc.semaphore("s_tr"))
        s_trf = ent(nc.semaphore("s_trf"))
        s_comb = ent(nc.semaphore("s_comb"))
        s_misc = ent(nc.semaphore("s_misc"))
        s_miscf = ent(nc.semaphore("s_miscf"))
        s_hps = ent(nc.semaphore("s_hps"))
        s_gelu = ent(nc.semaphore("s_gelu"))
        s_y = ent(nc.semaphore("s_y"))
        s_yf_hd = ent(nc.semaphore("s_yf_hd"))
        s_yf = ent(nc.semaphore("s_yf"))
        s_silu = ent(nc.semaphore("s_silu"))
        s_gsts = [ent(nc.semaphore(f"s_gsts{i}")) for i in range(2)]
        s_vfl = ent(nc.semaphore("s_vfl"))
        s_gate = ent(nc.semaphore("s_gate"))
        s_xnfs = [ent(nc.semaphore(f"s_xnfs{i}")) for i in range(2)]
        s_xbs = [ent(nc.semaphore(f"s_xbs{i}")) for i in range(2)]
        s_xnff = ent(nc.semaphore("s_xnff"))
        s_gfl = ent(nc.semaphore("s_gfl"))
        block = ent(nc.Block())

        Y = lambda oc: ps[:, oc * 512:(oc + 1) * 512]
        HSL = lambda sl: ps[:, 3072 + sl * 512: 3072 + (sl + 1) * 512] if sl < 2 \
            else ps[:, (sl - 2) * 512: (sl - 1) * 512]
        GATE = ps[0:8, 0:TT]
        TRA = ps[:, 512:520]
        TRB = ps[0:8, 1024:1152]
        MISC = lambda r: ps[:, 1536 + (r % 3) * 512: 1536 + (r % 3 + 1) * 512]

        w2thr = {}
        _cum = [0, 0, 0]
        for _h in range(2 * n_tt * n_e):
            _cum[_h % 3] += 16 if _h < 2 * n_e else 64
            w2thr[_h] = _cum[_h % 3]

        INIT_TOT = 16 * 6 + 4

        @block.gpsimd
        def _(g_):
            g_.memset(idf[:, :], 1.0).then_inc(s_gfl, 1)
            g_.wait_ge(s_gfl, 1)
            g_.affine_select(idf[:, :], idf[:, :], [[-1, 128]], ALU.is_equal, 0.0,
                             channel_multiplier=1).then_inc(s_init, 1)
            g_.memset(sel[:, :], 1.0).then_inc(s_gfl, 1)
            g_.wait_ge(s_gfl, 2)
            g_.affine_select(sel[:, :], sel[:, :], [[1, 9 * 128]], ALU.is_ge, 0.0,
                             channel_multiplier=-128).then_inc(s_gfl, 1)
            g_.wait_ge(s_gfl, 3)
            g_.affine_select(sel[:, :], sel[:, :], [[-1, 9 * 128]], ALU.is_ge, 0.0,
                             base=127, channel_multiplier=128).then_inc(s_init, 1)
            g_.nop().then_inc(s_init, 1)
            g_.memset(comb9[:, :], 0.1).then_inc(s_init, 1)

            for e in range(n_e):
                eidx = e
                for hc in range(8):
                    idx = eidx * 8 + hc
                    slot = idx % 2
                    if idx >= 2:
                        g_.wait_ge(s_w1f, idx - 1)
                        g_.wait_ge(s_w1wr, 16 * idx)
                    dst = w1b[:, slot * NCH * 512:(slot + 1) * NCH * 512] \
                        .rearrange("p (c f) -> p c f", f=512)
                    dst2 = w1b[:, slot * NCH * 512:(slot + 1) * NCH * 512]
                    src = w1_d[e, :, hc * 512:(hc + 1) * 512] if e < E \
                        else sw1_d[:, hc * 512:(hc + 1) * 512]
                    g_.dma_start(out=dst, in_=src.rearrange("(c p) f -> p c f", p=128)
                                 ).then_inc(s_w1s[slot], 16)
                    g_.wait_ge(s_w1s[slot], 16 * (idx // 2 + 1))
                    g_.dma_start(out=w1scr[idx], in_=dst2).then_inc(s_w1wr, 16)
                if eidx >= 1:
                    g_.wait_ge(s_w2wr, 16 * 2 * eidx)
                w2src = w2_d[e] if e < E else sw2_d
                for oh in range(2):
                    h = 2 * eidx + oh
                    sl = h % 3
                    if h >= 3:
                        g_.wait_ge(s_w2f, eidx - 1 + oh)
                    g_.dma_start(
                        out=w2b[:, sl * 16384:(sl + 1) * 16384]
                        .rearrange("p (c f) -> p c f", f=512),
                        in_=w2src.rearrange("(c p) o -> p c o", p=128)
                        [:, :, oh * 512:(oh + 1) * 512]).then_inc(s_w2sl[sl], 16)
                for oh in range(2):
                    h = 2 * eidx + oh
                    sl = h % 3
                    g_.wait_ge(s_w2sl[sl], w2thr[h])
                    g_.dma_start(out=w2scr[e, oh],
                                 in_=w2b[:, sl * 16384:(sl + 1) * 16384]
                                 ).then_inc(s_w2wr, 16)
            g_.wait_ge(s_w1wr, 16 * n_e * 8)
            g_.wait_ge(s_w2wr, 16 * 2 * n_e)
            for tt in range(1, n_tt):
                if 2 <= tt + 1 < n_tt:
                    nb = tt + 1
                    g_.wait_ge(s_w2f, (nb - 1) * n_e)
                    g_.dma_start(out=xnsb[:, (nb % 2) * 4096:(nb % 2 + 1) * 4096]
                                 .rearrange("p (c t) -> p c t", t=512),
                                 in_=xn_d.rearrange("(c p) t -> p c t", p=128)
                                 [:, :, nb * TT:(nb + 1) * TT]).then_inc(s_xbs[nb % 2], 16)
                for e in range(n_e):
                    eidx = tt * n_e + e
                    for hc in range(8):
                        idx = eidx * 8 + hc
                        slot = idx % 2
                        g_.wait_ge(s_w1f, idx - 1)
                        g_.dma_start(out=w1b[:, slot * NCH * 512:(slot + 1) * NCH * 512],
                                     in_=w1scr[e * 8 + hc]).then_inc(s_w1s[slot], 16)
                        oh, q = hc // 4, hc % 4
                        h = 2 * eidx + oh
                        sl = h % 3
                        if q == 0:
                            g_.wait_ge(s_w2f, eidx - 1 + oh)
                        g_.dma_start(
                            out=w2b[:, sl * 16384 + q * 4096: sl * 16384 + (q + 1) * 4096],
                            in_=w2scr[e, oh][:, q * 4096:(q + 1) * 4096]
                            ).then_inc(s_w2sl[sl], 16)

        @block.tensor
        def _(t):
            t.wait_ge(s_init, INIT_TOT)
            for c in range(NCH):
                t.wait_ge(s_xnfs[0], 16 * (c + 1))
                for tt in range(n_tt):
                    mm = t.matmul(ps[0:8, tt * TT:(tt + 1) * TT],
                                  gwtf[:, c * E:(c + 1) * E],
                                  xnf2[:, tt * TT: tt * TT + TT],
                                  start=(c == 0), stop=(c == NCH - 1))
                    if c == NCH - 1:
                        mm.then_inc(s_gate, 1)
                t.drain().then_inc(s_xnff, 1)
            nmisc = 0
            for tt in range(n_tt):
                t0 = tt * TT
                t.wait_ge(s_xbs[tt % 2], 16 * (tt // 2 + 1))
                if tt > 0:
                    t.wait_ge(s_yf, tt * n_e)
                for st in range(4):
                    t.wait_ge(s_sig, n_tt)
                    if tt > 0 or st > 0:
                        t.wait_ge(s_trf, tt * 4 + st)
                    t.transpose(TRA, scf[:, tt * TT + st * 128: tt * TT + (st + 1) * 128], idf[0:8, 0:8]) \
                        .then_inc(s_tr, 1)
                    t.wait_ge(s_comb, tt * 4 + st + 1)
                    t.transpose(TRB, combtm[:, :], idf[:, :]).then_inc(s_tr, 1)
                t.wait_ge(s_trf, (tt + 1) * 4)
                for e in range(n_e):
                    if nmisc >= 3:
                        t.wait_ge(s_miscf, nmisc - 2)
                    t.matmul(MISC(nmisc), sel[:, e * 128:(e + 1) * 128],
                             comb9[:, :], start=True, stop=True).then_inc(s_misc, 1)
                    nmisc += 1
                for oc in range(NCH):
                    if nmisc >= 3:
                        t.wait_ge(s_miscf, nmisc - 2)
                    t.matmul(MISC(nmisc), b29[:, oc * 128:(oc + 1) * 128],
                             comb9[:, :], start=True, stop=True).then_inc(s_misc, 1)
                    nmisc += 1
                for e in range(n_e):
                    eidx = tt * n_e + e
                    for p in range(NP):
                        hc, pl = p // 4, p % 4
                        w1idx = eidx * 8 + hc
                        gidx = eidx * NP + p
                        if pl == 0:
                            t.wait_ge(s_w1s[w1idx % 2], 16 * (w1idx // 2 + 1))
                        if gidx >= 4:
                            t.wait_ge(s_gelu, gidx - 3)
                        if p == 0 and eidx >= 1:
                            t.wait_ge(s_yf_hd, 2 * eidx - 1)
                        if p == 2 and eidx >= 1:
                            t.wait_ge(s_yf_hd, 2 * eidx)
                        if p == 2 and e == 0:
                            t.wait_ge(s_trf, (tt + 1) * 4)
                        base = (w1idx % 2) * NCH * 512 + pl * 128
                        for c in range(NCH):
                            mm = t.matmul(HSL(gidx % 4), w1b[:, base + c * 512: base + c * 512 + 128],
                                          xnsb[:, (tt % 2) * 4096 + c * TT:
                                               (tt % 2) * 4096 + c * TT + TT],
                                          start=(c == 0), stop=(c == NCH - 1))
                        mm.then_inc(s_hps, 1)
                        if pl == 3:
                            t.drain().then_inc(s_w1f, 1)
                    t.wait_ge(s_gelu, eidx * NP + NP)
                    for oh in range(2):
                        h = 2 * eidx + oh
                        t.wait_ge(s_w2sl[h % 3], w2thr[h])
                    if e == 0:
                        t.wait_ge(s_miscf, (tt + 1) * (n_e + NCH))
                    if eidx >= 1:
                        t.wait_ge(s_yf, eidx)
                    for oc in range(NCH):
                        sb = ((2 * eidx + oc // 4) % 3) * 16384 + (oc % 4) * 128
                        for p in range(NP):
                            mm = t.matmul(Y(oc), w2b[:, sb + p * 512: sb + p * 512 + 128],
                                          hdn[:, p * TT:(p + 1) * TT],
                                          start=(p == 0), stop=(p == NP - 1))
                        if oc == NCH - 1:
                            mm.then_inc(s_y, 1)
                            t.drain().then_inc(s_w2f, 1)

        @block.scalar
        def _(s):
            s.wait_ge(s_init, INIT_TOT)
            for tt in range(n_tt):
                s.wait_ge(s_gate, tt + 1)
                s.activation(scf[:, tt * TT:(tt + 1) * TT], ps[0:8, tt * TT:(tt + 1) * TT],
                             AF.Sigmoid, bias=gb[:, :]).then_inc(s_sig, 1)
            for tt in range(n_tt):
                for e in range(n_e):
                    eidx = tt * n_e + e
                    for p in range(NP):
                        gidx = eidx * NP + p
                        s.wait_ge(s_hps, gidx + 1)
                        s.activation(hdn[:, p * TT:(p + 1) * TT], HSL(gidx % 4),
                                     AF.Gelu, bias=b1sb[:, e * NP + p: e * NP + p + 1]) \
                            .then_inc(s_gelu, 1)
                s.wait_ge(s_yf, (tt + 1) * n_e)
                for oc in range(NCH):
                    gi = tt * NCH + oc
                    if gi >= 1:
                        s.wait_ge(s_gsts[0], 16 * gi)
                    s.activation(gstg[:, 0:TT],
                                 routed[:, oc * TT:(oc + 1) * TT], AF.Silu) \
                        .then_inc(s_silu, 1)

        @block.vector
        def _(v):
            v.wait_ge(s_init, INIT_TOT)
            nmisc_v = 0
            vfl = [0]
            def VF(inst):
                vfl[0] += 1
                inst.then_inc(s_vfl, 1)
                v.wait_ge(s_vfl, vfl[0])
                return inst
            for tt in range(n_tt):
                for st in range(4):
                    v.wait_ge(s_tr, tt * 8 + 2 * st + 1)
                    VF(v.tensor_copy(sctm[:, :], TRA))
                    VF(v.reduce_max(m1[:, :], sctm[:, :], axis=AX))
                    VF(v.tensor_scalar(eq1[:, :], sctm[:, :], m1[:, :], None, ALU.is_equal))
                    VF(v.scalar_tensor_tensor(msk[:, :], eq1[:, :], -2.0, sctm[:, :],
                                              ALU.mult, ALU.add))
                    VF(v.reduce_max(m2[:, :], msk[:, :], axis=AX))
                    VF(v.tensor_scalar(eq2[:, :], msk[:, :], m2[:, :], None, ALU.is_equal))
                    VF(v.tensor_tensor(srec[:, :], m1[:, :], m2[:, :], ALU.add))
                    VF(v.tensor_scalar(srec[:, :], srec[:, :], 1e-6, None, ALU.add))
                    VF(v.reciprocal(srec[:, :], srec[:, :]))
                    VF(v.tensor_tensor(ns1[:, :], m1[:, :], srec[:, :], ALU.mult))
                    VF(v.tensor_tensor(ns2[:, :], m2[:, :], srec[:, :], ALU.mult))
                    VF(v.tensor_scalar(eq1[:, :], eq1[:, :], ns1[:, :], None, ALU.mult))
                    VF(v.tensor_scalar(eq2[:, :], eq2[:, :], ns2[:, :], None, ALU.mult))
                    v.tensor_tensor(combtm[:, :], eq1[:, :], eq2[:, :], ALU.add) \
                        .then_inc(s_comb, 1)
                    v.wait_ge(s_tr, tt * 8 + 2 * st + 2)
                    v.tensor_copy(comb9[0:8, st * 128:(st + 1) * 128], TRB) \
                        .then_inc(s_trf, 1)
                v.wait_ge(s_silu, NCH * tt)
                for e in range(n_e):
                    v.wait_ge(s_misc, tt * NMISC + e + 1)
                    if e < E:
                        v.tensor_copy(combbc[:, e * TT:(e + 1) * TT], MISC(nmisc_v)) \
                            .then_inc(s_miscf, 1)
                    else:
                        v.nop().then_inc(s_miscf, 1)
                    nmisc_v += 1
                for oc in range(NCH):
                    v.wait_ge(s_misc, tt * NMISC + n_e + oc + 1)
                    v.tensor_copy(routed[:, oc * TT:(oc + 1) * TT], MISC(nmisc_v)) \
                        .then_inc(s_miscf, 1)
                    nmisc_v += 1
                for e in range(n_e):
                    eidx = tt * n_e + e
                    v.wait_ge(s_y, eidx + 1)
                    for k, oc in enumerate([6, 7, 0, 1, 2, 3, 4, 5]):
                        if e < E:
                            VF(v.tensor_tensor(tmpy[:, :], Y(oc),
                                               combbc[:, e * TT:(e + 1) * TT], ALU.mult))
                        else:
                            VF(v.tensor_scalar(tmpy[:, :], Y(oc), 0.1, None, ALU.mult))
                        inst = v.tensor_tensor(routed[:, oc * TT:(oc + 1) * TT],
                                               routed[:, oc * TT:(oc + 1) * TT],
                                               tmpy[:, :], ALU.add)
                        if k in (1, 3):
                            inst.then_inc(s_yf_hd, 1)
                    inst.then_inc(s_yf, 1)

        @block.sync
        def _(sy):
            sy.dma_start(out=b29[0:8, :], in_=b2_d[:, :]).then_inc(s_init, 16)
            sy.dma_start(out=b29[8:9, :], in_=sb2_d[:, :]).then_inc(s_init, 16)
            sy.dma_start(out=b1sb[:, 0:8 * NP],
                         in_=b1t_d.rearrange("e p f -> p e f")).then_inc(s_init, 16)
            sy.dma_start(out=b1sb[:, 8 * NP:9 * NP], in_=sb1t_d[:, :]).then_inc(s_init, 16)
            sy.dma_start(out=gwtf[:, :],
                         in_=gwt_d.rearrange("(c p) e -> p c e", p=128)).then_inc(s_init, 16)
            sy.dma_start(out=gb[:, :], in_=gb_d[:, :]).then_inc(s_init, 16)
            for tt in range(min(2, n_tt)):
                sy.dma_start(out=xnsb[:, tt * 4096:(tt + 1) * 4096]
                             .rearrange("p (c t) -> p c t", t=512),
                             in_=xn_d.rearrange("(c p) t -> p c t", p=128)
                             [:, :, tt * TT:(tt + 1) * TT]).then_inc(s_xbs[tt % 2], 16)
            for c in range(NCH):
                if c >= 1:
                    sy.wait_ge(s_xnff, c)
                sy.dma_start(out=xnf2[:, :],
                             in_=xnf_d[c * 128:(c + 1) * 128, :]).then_inc(s_xnfs[0], 16)
            for tt in range(n_tt):
                for oc in range(NCH):
                    gi = tt * NCH + oc
                    sy.wait_ge(s_silu, gi + 1)
                    sy.dma_start(out=g_d.rearrange("(c p) t -> p c t", p=128)
                                 [:, oc, tt * TT:(tt + 1) * TT],
                                 in_=gstg[:, 0:TT]).then_inc(s_gsts[0], 16)
            sy.wait_ge(s_gsts[0], 16 * n_tt * NCH)


def build_l0_moe(n_tt=4, n_e=9):
    nc = bass.Bass("TRN2", target_bir_lowering=False, debug=False)
    DD = _decl_l0_moe(nc)
    emit_l0_moe(nc, DD, n_tt, n_e)
    return nc


def _host_moe_inputs(xn_fm_bf16, W):
    NP = FF // 128
    return {
        "xn": xn_fm_bf16,
        "w1": W["w1"], "w2": W["w2"], "sw1": W["sw1"], "sw2": W["sw2"],
        "b1t": np.ascontiguousarray(W["b1"].reshape(E, NP, 128).transpose(0, 2, 1)),
        "sb1t": np.ascontiguousarray(W["sb1"].reshape(NP, 128).T),
        "b2": W["b2"], "sb2": W["sb2"].reshape(1, D),
        "gwt": np.ascontiguousarray(W["gate_w"].T), "gb": W["gate_b"].reshape(E, 1),
    }





def _decl_ln(nc):
    DD = {}
    DD["x"] = nc.dram_tensor("x", [D, TC], F32, kind="ExternalInput").ap()
    DD["xn"] = nc.dram_tensor("xn", [D, TC], BF16, kind="ExternalOutput").ap()
    DD["xnf"] = nc.dram_tensor("xnf", [D, TC], F32, kind="ExternalOutput").ap()
    return DD

def emit_ln(nc, DD):
    TT = 512
    NT = TC // TT
    x_d = DD["x"]
    xn_d = DD["xn"]
    xnf_d = DD["xnf"]
    with ExitStack() as ctx:
        ent = ctx.enter_context
        xt = ent(nc.sbuf_tensor("xt", [128, 2 * NCH * TT], F32))
        xsq = ent(nc.sbuf_tensor("xsq", [128, NCH * TT], F32))
        ones = ent(nc.sbuf_tensor("ones", [128, 128], F32))
        mrow = ent(nc.sbuf_tensor("mrow", [1, TT], F32))
        e2row = ent(nc.sbuf_tensor("e2row", [1, TT], F32))
        vrow = ent(nc.sbuf_tensor("vrow", [1, TT], F32))
        srow = ent(nc.sbuf_tensor("srow", [1, TT], F32))
        rrow = ent(nc.sbuf_tensor("rrow", [1, TT], F32))
        mrrow = ent(nc.sbuf_tensor("mrrow", [1, TT], F32))
        xnf_s = ent(nc.sbuf_tensor("xnf_s", [128, 2 * NCH * TT], F32))
        xn_s = ent(nc.sbuf_tensor("xn_s", [128, 2 * NCH * TT], BF16))
        ps = ent(nc.psum_tensor("lnps", [128, 4096], F32))
        s_xs = [ent(nc.semaphore(f"s_xs{i}")) for i in range(2)]
        s_xf = ent(nc.semaphore("s_xf"))
        s_init = ent(nc.semaphore("s_init"))
        s_sq = ent(nc.semaphore("s_sq"))
        s_sum = ent(nc.semaphore("s_sum"))
        s_st = ent(nc.semaphore("s_st"))
        s_sqr = ent(nc.semaphore("s_sqr"))
        s_rr = ent(nc.semaphore("s_rr"))
        s_bc = ent(nc.semaphore("s_bc"))
        s_xn = ent(nc.semaphore("s_xn"))
        s_os = [ent(nc.semaphore(f"s_os{i}")) for i in range(2)]
        s_vfl = ent(nc.semaphore("s_vfl"))
        block = ent(nc.Block())

        def S1(p): return ps[0:1, p * 2048: p * 2048 + TT]
        def S2(p): return ps[0:1, p * 2048 + TT: p * 2048 + 2 * TT]
        def RB(p): return ps[:, p * 2048 + 1024: p * 2048 + 1024 + TT]
        def MRB(p): return ps[:, p * 2048 + 1536: p * 2048 + 1536 + TT]

        @block.gpsimd
        def _(g_):
            g_.memset(ones[:, :], 1.0).then_inc(s_init, 1)

        @block.sync
        def _(sy):
            for i in range(NT):
                if i >= 2:
                    sy.wait_ge(s_xf, i - 1)
                sy.dma_start(out=xt.rearrange("p (s c t) -> p s c t", s=2, t=TT)[:, i % 2],
                             in_=x_d.rearrange("(c p) t -> p c t", p=128)[:, :, i * TT:(i + 1) * TT])                     .then_inc(s_xs[i % 2], 16)
            for i in range(NT):
                for half, sem in ((0, s_os[0]), (1, s_os[1])):
                    sy.wait_ge(s_xn, 2 * i + half + 1)
                    sl = slice((i % 2) * NCH * TT + half * 4 * TT,
                               (i % 2) * NCH * TT + (half + 1) * 4 * TT)
                    off = half * 4
                    sy.dma_start(out=xnf_d.rearrange("(c p) t -> p c t", p=128)
                                 [:, off:off + 4, i * TT:(i + 1) * TT],
                                 in_=xnf_s.rearrange("p (s c t) -> p s c t", s=2, t=TT)
                                 [:, i % 2, off:off + 4]).then_inc(sem, 16)
                    sy.dma_start(out=xn_d.rearrange("(c p) t -> p c t", p=128)
                                 [:, off:off + 4, i * TT:(i + 1) * TT],
                                 in_=xn_s.rearrange("p (s c t) -> p s c t", s=2, t=TT)
                                 [:, i % 2, off:off + 4]).then_inc(sem, 16)

        @block.tensor
        def _(t):
            t.wait_ge(s_init, 1)
            for i in range(NT):
                p = i % 2
                t.wait_ge(s_xs[p], 16 * (i // 2 + 1))
                if i >= 2:
                    t.wait_ge(s_xn, 2 * (i - 1))
                for c in range(NCH):
                    t.matmul(S1(p), ones[:, 0:1],
                             xt[:, p * NCH * TT + c * TT:(p * NCH * TT) + (c + 1) * TT],
                             start=(c == 0), stop=(c == NCH - 1))
                t.wait_ge(s_sq, i * NCH + NCH)
                for c in range(NCH):
                    mm = t.matmul(S2(p), ones[:, 0:1], xsq[:, c * TT:(c + 1) * TT],
                                  start=(c == 0), stop=(c == NCH - 1))
                mm.then_inc(s_sum, 1)
                t.wait_ge(s_rr, i + 1)
                t.matmul(RB(p), ones[0:1, :], rrow[:, :], start=True, stop=True)
                t.matmul(MRB(p), ones[0:1, :], mrrow[:, :], start=True, stop=True)                     .then_inc(s_bc, 1)

        @block.scalar
        def _(s):
            for i in range(NT):
                s.wait_ge(s_st, i + 1)
                s.activation(srow[:, :], vrow[:, :], AF.Sqrt).then_inc(s_sqr, 1)

        @block.vector
        def _(v):
            vfl = [0]
            def VF(inst):
                vfl[0] += 1
                inst.then_inc(s_vfl, 1)
                v.wait_ge(s_vfl, vfl[0])
                return inst
            for i in range(NT):
                p = i % 2
                v.wait_ge(s_xs[p], 16 * (i // 2 + 1))
                for c in range(NCH):
                    src = xt[:, p * NCH * TT + c * TT: p * NCH * TT + (c + 1) * TT]
                    v.tensor_tensor(xsq[:, c * TT:(c + 1) * TT], src, src, ALU.mult)                         .then_inc(s_sq, 1)
                v.wait_ge(s_sum, i + 1)
                VF(v.tensor_scalar(mrow[:, :], S1(p), 1.0 / D, None, ALU.mult))
                VF(v.tensor_scalar(e2row[:, :], S2(p), 1.0 / D, None, ALU.mult))
                VF(v.tensor_tensor(vrow[:, :], mrow[:, :], mrow[:, :], ALU.mult))
                VF(v.tensor_tensor(vrow[:, :], e2row[:, :], vrow[:, :], ALU.subtract))
                v.tensor_scalar(vrow[:, :], vrow[:, :], 1e-5, None, ALU.add) \
                    .then_inc(s_st, 1)
                v.wait_ge(s_sqr, i + 1)
                VF(v.reciprocal(rrow[:, :], srow[:, :]))
                v.tensor_tensor(mrrow[:, :], mrow[:, :], rrow[:, :], ALU.mult) \
                    .then_inc(s_rr, 1)
                v.wait_ge(s_bc, i + 1)
                v.wait_ge(s_os[0], 32 * (i // 2))
                v.wait_ge(s_os[1], 32 * (i // 2))
                for c in range(NCH):
                    src = xt[:, p * NCH * TT + c * TT: p * NCH * TT + (c + 1) * TT]
                    dstf = xnf_s[:, p * NCH * TT + c * TT: p * NCH * TT + (c + 1) * TT]
                    dstb = xn_s[:, p * NCH * TT + c * TT: p * NCH * TT + (c + 1) * TT]
                    VF(v.tensor_tensor(dstf, src, RB(p), ALU.mult))
                    VF(v.tensor_tensor(dstf, dstf, MRB(p), ALU.subtract))
                    inst = v.tensor_copy(dstb, dstf)
                    if c == 3 or c == NCH - 1:
                        inst.then_inc(s_xn, 1)
                v.drain().then_inc(s_xf, 1)

def build_ln():
    nc = bass.Bass("TRN2", target_bir_lowering=False, debug=False)
    DD = _decl_ln(nc)
    emit_ln(nc, DD)
    return nc




def _decl_l0_qkv(nc):
    DD = {}
    DD["xn"] = nc.dram_tensor("xn", [D, TC], BF16, kind="ExternalInput").ap()
    DD["wqt"] = nc.dram_tensor("wqt", [D, D], F32, kind="ExternalInput").ap()
    DD["wkt"] = nc.dram_tensor("wkt", [D, D], F32, kind="ExternalInput").ap()
    DD["wvt"] = nc.dram_tensor("wvt", [D, D], F32, kind="ExternalInput").ap()
    DD["qf"] = nc.dram_tensor("qf", [D, TC], BF16, kind="ExternalOutput").ap()
    DD["kf"] = nc.dram_tensor("kf", [D, TC], BF16, kind="ExternalOutput").ap()
    DD["ktm"] = nc.dram_tensor("ktm", [TC, D], BF16, kind="ExternalOutput").ap()
    DD["vtm"] = nc.dram_tensor("vtm", [TC, H * 65], BF16, kind="ExternalOutput").ap()
    DD["stot"] = nc.dram_tensor("stot", [H * 64, 65], F32, kind="ExternalOutput").ap()

    return DD

def emit_l0_qkv(nc, DD):
    NCT = TC // 128
    xn_d = DD["xn"]
    wqt_d = DD["wqt"]
    wkt_d = DD["wkt"]
    wvt_d = DD["wvt"]
    qf_d = DD["qf"]
    kf_d = DD["kf"]
    ktm_d = DD["ktm"]
    vtm_d = DD["vtm"]
    stot_d = DD["stot"]
    with ExitStack() as ctx:
        ent = ctx.enter_context
        xnsb = ent(nc.sbuf_tensor("xnsb", [128, NCH * TC], BF16))
        wq = ent(nc.sbuf_tensor("wq", [128, NCH * D], BF16))
        wk = ent(nc.sbuf_tensor("wk", [128, NCH * D], BF16))
        wv = ent(nc.sbuf_tensor("wv", [128, NCH * D], BF16))
        kf = ent(nc.sbuf_tensor("kfsb", [128, NCH * TC], BF16))
        el1 = ent(nc.sbuf_tensor("el1", [128, 2 * 512], F32))
        el3 = ent(nc.sbuf_tensor("el3", [128, 2 * 512], F32))
        qstg = ent(nc.sbuf_tensor("qstg", [128, 2 * 512], BF16))
        vstg = ent(nc.sbuf_tensor("vstg", [128, 2 * 1040], BF16))
        kts = ent(nc.sbuf_tensor("kts", [128, 2 * 1024], BF16))
        sts = ent(nc.sbuf_tensor("sts", [64, 16 * 65], F32))
        identb = ent(nc.sbuf_tensor("identb", [128, 128], BF16))
        ps = ent(nc.psum_tensor("ps", [128, 3584], F32))
        psb = ent(nc.psum_tensor("psb", [128, 256], BF16))
        s_init = ent(nc.semaphore("s_init"))
        s_proj = ent(nc.semaphore("s_proj"))
        s_emn = ent(nc.semaphore("s_emn"))
        s_eex = ent(nc.semaphore("s_eex"))
        s_eev = ent(nc.semaphore("s_eev"))
        s_ktr = ent(nc.semaphore("s_ktr"))
        s_ktv = ent(nc.semaphore("s_ktv"))
        s_stm = ent(nc.semaphore("s_stm"))
        s_stv = ent(nc.semaphore("s_stv"))
        s_vfl = ent(nc.semaphore("s_vfl"))
        s_gfl = ent(nc.semaphore("s_gfl"))
        s_qst = [ent(nc.semaphore(f"s_qst{i}")) for i in range(2)]
        s_kst = ent(nc.semaphore("s_kst"))
        s_tst = [ent(nc.semaphore(f"s_tst{i}")) for i in range(2)]
        s_vst = [ent(nc.semaphore(f"s_vst{i}")) for i in range(2)]
        s_sst = ent(nc.semaphore("s_sst"))
        block = ent(nc.Block())

        PRJ = lambda r: ps[:, (r % 3) * 512:(r % 3) * 512 + 512]
        VPS = lambda r: ps[:, (r % 2) * 1024:(r % 2) * 1024 + 1024]
        KTR = psb[:, 0:128]
        SKV = lambda h: ps[0:64, (5 + h // 8) * 512 + (h % 8) * 64:
                           (5 + h // 8) * 512 + (h % 8) * 64 + 64]
        SKS = lambda h: ps[0:64, 4 * 512 + h: 4 * 512 + h + 1]

        @block.gpsimd
        def _(g_):
            for w_sb, w_dd in ((wq, wqt_d), (wk, wkt_d), (wv, wvt_d)):
                g_.dma_start(out=w_sb.rearrange("p (c f) -> p c f", f=D),
                             in_=w_dd.rearrange("(c p) f -> p c f", p=128)).then_inc(s_init, 16)
            g_.memset(identb[:, :], 1.0).then_inc(s_gfl, 1)
            g_.wait_ge(s_gfl, 1)
            g_.affine_select(identb[:, :], identb[:, :], [[-1, 128]], ALU.is_equal, 0.0,
                             channel_multiplier=1).then_inc(s_init, 1)

        @block.sync
        def _(sy):
            sy.dma_start(out=xnsb.rearrange("p (c t) -> p c t", t=TC),
                         in_=xn_d.rearrange("(c p) t -> p c t", p=128)).then_inc(s_init, 16)
            for i in range(32):
                oc, ti = i // 4, i % 4
                sy.wait_ge(s_eev, i + 1)
                sy.dma_start(out=qf_d[oc * 128:(oc + 1) * 128, ti * 512:(ti + 1) * 512],
                             in_=qstg[:, (i % 2) * 512:(i % 2) * 512 + 512]) \
                    .then_inc(s_qst[i % 2], 16)
            for i in range(32):
                oc, ti = i // 4, i % 4
                sy.wait_ge(s_eev, 33 + i)
                sy.dma_start(out=kf_d[oc * 128:(oc + 1) * 128, ti * 512:(ti + 1) * 512],
                             in_=kf[:, oc * TC + ti * 512: oc * TC + ti * 512 + 512]) \
                    .then_inc(s_kst, 16)
            for ct in range(NCT):
                sy.wait_ge(s_ktv, ct * 8 + 8)
                sy.dma_start(out=ktm_d[ct * 128:(ct + 1) * 128, :],
                             in_=kts[:, (ct % 2) * 1024:(ct % 2) * 1024 + 1024]) \
                    .then_inc(s_tst[ct % 2], 16)
                sy.wait_ge(s_eev, 64 + ct + 1)
                sy.dma_start(out=vtm_d[ct * 128:(ct + 1) * 128, :],
                             in_=vstg[:, (ct % 2) * 1040:(ct % 2) * 1040 + 1040]) \
                    .then_inc(s_vst[ct % 2], 16)
            sy.wait_ge(s_stv, 1)
            sy.dma_start(out=stot_d.rearrange("(h p) e -> p h e", p=64),
                         in_=sts.rearrange("p (h e) -> p h e", e=65)).then_inc(s_sst, 16)
            sy.wait_ge(s_qst[0], 16 * 16)
            sy.wait_ge(s_qst[1], 16 * 16)
            sy.wait_ge(s_kst, 16 * 32)
            sy.wait_ge(s_tst[0], 16 * 8)
            sy.wait_ge(s_tst[1], 16 * 8)
            sy.wait_ge(s_vst[0], 16 * 8)
            sy.wait_ge(s_vst[1], 16 * 8)
            sy.wait_ge(s_sst, 16)

        @block.tensor
        def _(t):
            t.wait_ge(s_init, 65)
            nprj = 0
            for w_sb in (wq, wk):
                for oc in range(NCH):
                    for ti in range(4):
                        if nprj >= 3:
                            t.wait_ge(s_eev, nprj - 2)
                        for ic in range(NCH):
                            mm = t.matmul(PRJ(nprj),
                                          w_sb[:, ic * D + oc * 128: ic * D + oc * 128 + 128],
                                          xnsb[:, ic * TC + ti * 512: ic * TC + ti * 512 + 512],
                                          start=(ic == 0), stop=(ic == NCH - 1))
                        mm.then_inc(s_proj, 1)
                        nprj += 1
            t.wait_ge(s_eev, 64)
            nktr = 0
            for ct in range(NCT):
                if ct >= 2:
                    t.wait_ge(s_eev, 64 + ct - 1)
                for ic in range(NCH):
                    t.matmul(VPS(ct)[:, 0:512],
                             xnsb[:, ic * TC + ct * 128: ic * TC + ct * 128 + 128],
                             wv[:, ic * D: ic * D + 512],
                             start=(ic == 0), stop=(ic == NCH - 1))
                    mm = t.matmul(VPS(ct)[:, 512:1024],
                                  xnsb[:, ic * TC + ct * 128: ic * TC + ct * 128 + 128],
                                  wv[:, ic * D + 512: ic * D + 1024],
                                  start=(ic == 0), stop=(ic == NCH - 1))
                mm.then_inc(s_proj, 1)
                for fc in range(NCH):
                    if nktr >= 1:
                        t.wait_ge(s_ktv, nktr)
                    t.transpose(KTR, kf[:, fc * TC + ct * 128: fc * TC + ct * 128 + 128],
                                identb[:, :]).then_inc(s_ktr, 1)
                    nktr += 1
                t.wait_ge(s_ktv, ct * 8 + 8)
                t.wait_ge(s_eev, 64 + ct + 1)
                for h in range(H):
                    off = (h // 2) * 128 + (h % 2) * 64
                    kslice = kts[:, (ct % 2) * 1024 + off: (ct % 2) * 1024 + off + 64]
                    t.matmul(SKV(h), kslice,
                             vstg[:, (ct % 2) * 1040 + h * 65: (ct % 2) * 1040 + h * 65 + 64],
                             start=(ct == 0 and h % 8 == 0), stop=(ct == NCT - 1))
                    mm = t.matmul(SKS(h), kslice,
                                  vstg[:, (ct % 2) * 1040 + h * 65 + 64:
                                       (ct % 2) * 1040 + h * 65 + 65],
                                  start=(ct == 0 and h == 0), stop=(ct == NCT - 1))
                if ct == NCT - 1:
                    mm.then_inc(s_stm, 1)

        @block.scalar
        def _(s):
            for i in range(64):
                s.wait_ge(s_emn, i + 1)
                s.activation(el1[:, (i % 2) * 512:(i % 2) * 512 + 512],
                             el1[:, (i % 2) * 512:(i % 2) * 512 + 512], AF.Exp) \
                    .then_inc(s_eex, 1)

        @block.vector
        def _(v):
            vfl = [0]
            def VF(inst):
                vfl[0] += 1
                inst.then_inc(s_vfl, 1)
                v.wait_ge(s_vfl, vfl[0])
                return inst
            v.memset(vstg[:, :], 1.0)
            for phase in range(2):
                for i in range(32):
                    sl = i % 2
                    gidx = phase * 32 + i
                    v.wait_ge(s_proj, gidx + 1)
                    v.tensor_scalar_min(el1[:, sl * 512:(sl + 1) * 512], PRJ(gidx), 0.0) \
                        .then_inc(s_emn, 1)
                    VF(v.tensor_scalar_max(el3[:, sl * 512:(sl + 1) * 512], PRJ(gidx), 0.0))
                    v.wait_ge(s_eex, gidx + 1)
                    oc, ti = i // 4, i % 4
                    if phase == 0:
                        if i >= 2:
                            v.wait_ge(s_qst[sl], 16 * (i // 2))
                        dst = qstg[:, sl * 512:(sl + 1) * 512]
                    else:
                        dst = kf[:, oc * TC + ti * 512: oc * TC + ti * 512 + 512]
                    v.tensor_tensor(dst, el1[:, sl * 512:(sl + 1) * 512],
                                    el3[:, sl * 512:(sl + 1) * 512], ALU.add).then_inc(s_eev, 1)
            nktr = 0
            for ct in range(NCT):
                sl = ct % 2
                for fc in range(NCH):
                    v.wait_ge(s_ktr, nktr + 1)
                    if fc == 0 and ct >= 2:
                        v.wait_ge(s_tst[sl], 16 * (ct // 2))
                    v.tensor_copy(kts[:, sl * 1024 + fc * 128: sl * 1024 + fc * 128 + 128],
                                  KTR).then_inc(s_ktv, 1)
                    nktr += 1
                v.wait_ge(s_proj, 64 + ct + 1)
                if ct >= 2:
                    v.wait_ge(s_vst[sl], 16 * (ct // 2))
                v.tensor_copy(vstg.rearrange("p (s h e) -> p s h e", s=2, e=65)[:, sl, :, 0:64],
                              VPS(ct).rearrange("p (h e) -> p h e", e=64)) \
                    .then_inc(s_eev, 1)
            v.wait_ge(s_stm, 1)
            for h in range(H):
                v.tensor_copy(sts[:, h * 65: h * 65 + 64], SKV(h))
                v.tensor_copy(sts[:, h * 65 + 64: h * 65 + 65], SKS(h))
            v.drain().then_inc(s_stv, 1)

def build_l0_qkv():
    nc = bass.Bass("TRN2", target_bir_lowering=False, debug=False)
    DD = _decl_l0_qkv(nc)
    emit_l0_qkv(nc, DD)
    return nc




def _decl_l0_scan(nc):
    DD = {}
    DD["qf"] = nc.dram_tensor("qf", [D, TC], BF16, kind="ExternalInput").ap()
    DD["kf"] = nc.dram_tensor("kf", [D, TC], BF16, kind="ExternalInput").ap()
    DD["ktm"] = nc.dram_tensor("ktm", [TC, D], BF16, kind="ExternalInput").ap()
    DD["vtm"] = nc.dram_tensor("vtm", [TC, H * 65], BF16, kind="ExternalInput").ap()
    DD["stall"] = nc.dram_tensor("stall", [4 * H * 64, 65], F32, kind="ExternalInput").ap()
    DD["pm64"] = nc.dram_tensor("pm64", [128, 4], F32, kind="ExternalInput").ap()
    DD["g"] = nc.dram_tensor("g", [D, TC], BF16, kind="ExternalInput").ap()
    DD["x"] = nc.dram_tensor("x", [D, TC], F32, kind="ExternalInput").ap()
    DD["wot"] = nc.dram_tensor("wot", [D, D], F32, kind="ExternalInput").ap()
    DD["x1"] = nc.dram_tensor("x1", [D, TC], F32, kind="ExternalOutput").ap()
    DD["s_out"] = nc.dram_tensor("s_out", [D, TC], BF16, kind="ExternalOutput").ap()

    return DD

def emit_l0_scan(nc, DD):
    NCT = TC // 128
    HPH = 8
    qf_d = DD["qf"]
    kf_d = DD["kf"]
    ktm_d = DD["ktm"]
    vtm_d = DD["vtm"]
    stall_d = DD["stall"]
    pm_d = DD["pm64"]
    g_d = DD["g"]
    x_d = DD["x"]
    wot_d = DD["wot"]
    x1_d = DD["x1"]
    sout_d = DD["s_out"]
    with ExitStack() as ctx:
        ent = ctx.enter_context
        s_sb = ent(nc.sbuf_tensor("s_sb", [128, NCH * TC], BF16))
        trim = ent(nc.sbuf_tensor("trim", [128, 128], F32))
        pm = ent(nc.sbuf_tensor("pm", [128, 4], F32))
        onesf = ent(nc.sbuf_tensor("onesf", [128, 128], F32))
        wot_sb = ent(nc.sbuf_tensor("wot_sb", [128, NCH * D], BF16))
        s_init = ent(nc.semaphore("s_init"))
        s_ld = [ent(nc.semaphore(f"s_ld{i}")) for i in range(2)]
        s_int = ent(nc.semaphore("s_int"))
        s_pt = ent(nc.semaphore("s_pt"))
        s_o = ent(nc.semaphore("s_o"))
        s_up = ent(nc.semaphore("s_up"))
        s_rd = ent(nc.semaphore("s_rd"))
        s_bdA = [ent(nc.semaphore(f"s_bdA{i}")) for i in range(4)]
        s_bdB = [ent(nc.semaphore(f"s_bdB{i}")) for i in range(4)]
        s_sv = ent(nc.semaphore("s_sv"))
        s_stb = ent(nc.semaphore("s_stb"))
        s_bas = ent(nc.semaphore("s_bas"))
        s_vfl = ent(nc.semaphore("s_vfl"))
        s_gfl = ent(nc.semaphore("s_gfl"))
        s_wo = ent(nc.semaphore("s_wo"))
        s_x1 = ent(nc.semaphore("s_x1"))
        s_gx = [ent(nc.semaphore(f"s_gx{i}")) for i in range(2)]
        s_ost = [ent(nc.semaphore(f"s_ost{i}")) for i in range(2)]
        s_sst = ent(nc.semaphore("s_sst"))

        vfl = [0]
        with ExitStack() as ctx1:
            en1 = ctx1.enter_context
            qfh = en1(nc.sbuf_tensor("qfh", [128, 4 * TC], BF16))
            kfh = en1(nc.sbuf_tensor("kfh", [128, 4 * TC], BF16))
            ktmh = en1(nc.sbuf_tensor("ktmh", [128, NCT * 512], BF16))
            vtmh = en1(nc.sbuf_tensor("vtmh", [128, NCT * 520], BF16))
            sgth = en1(nc.sbuf_tensor("sgth", [128, 4 * (HPH // 2) * 65], F32))
            stF = en1(nc.sbuf_tensor("stF", [128, (HPH // 2) * 65], F32))
            stB = en1(nc.sbuf_tensor("stB", [128, (HPH // 2) * 65], BF16))
            ptile = en1(nc.sbuf_tensor("ptile", [128, 4 * 128], BF16))
            drow = en1(nc.sbuf_tensor("drow", [1, 4 * 128], F32))
            bc64 = en1(nc.sbuf_tensor("bc64", [128, 4 * 128], F32))
            ps = en1(nc.psum_tensor("ps", [128, 4096], F32))
            blk = en1(nc.Block())

            ST = lambda r: ps[:, (r % 2) * 512:(r % 2) * 512 + 128]
            OO = lambda r, b: ps[b:b + 64, (2 + r % 6) * 512:(2 + r % 6) * 512 + 128]
            DEN = lambda r: ps[0:1, (2 + r % 6) * 512 + 128:(2 + r % 6) * 512 + 256]
            UP = lambda r, b: ps[b:b + 64, (2 + r % 6) * 512 + 256:(2 + r % 6) * 512 + 321]
            drow_dr = nc.dram_tensor("scan_drow_dr", [4, 128], F32, kind="Internal").ap()

            @blk.gpsimd
            def _(g_):
                g_.memset(trim[:, :], 1.0).then_inc(s_gfl, 1)
                g_.wait_ge(s_gfl, 1)
                g_.affine_select(trim[:, :], trim[:, :], [[1, 128]], ALU.is_ge, 0.0,
                                 channel_multiplier=-1).then_inc(s_init, 1)
                g_.memset(onesf[:, :], 1.0).then_inc(s_init, 1)
                g_.dma_start(out=wot_sb.rearrange("p (c f) -> p c f", f=D),
                             in_=wot_d.rearrange("(c p) f -> p c f", p=128)).then_inc(s_init, 16)

            @blk.sync
            def _(sy):
                sy.dma_start(out=pm[:, :], in_=pm_d[:, :]).then_inc(s_init, 16)
                for hf in range(2):
                    if hf == 1:
                        sy.wait_ge(s_stb, 136)
                        sy.wait_ge(s_sv, 128)
                    r0 = hf * 512
                    sy.dma_start(out=qfh.rearrange("p (c t) -> p c t", t=TC),
                                 in_=qf_d.rearrange("(c p) t -> p c t", p=128)[:, 4 * hf:4 * hf + 4]) \
                        .then_inc(s_ld[hf], 16)
                    sy.dma_start(out=kfh.rearrange("p (c t) -> p c t", t=TC),
                                 in_=kf_d.rearrange("(c p) t -> p c t", p=128)[:, 4 * hf:4 * hf + 4]) \
                        .then_inc(s_ld[hf], 16)
                    sy.dma_start(out=ktmh.rearrange("p (c t) -> p c t", t=512),
                                 in_=ktm_d[:, r0:r0 + 512].rearrange("(c p) t -> p c t", p=128)) \
                        .then_inc(s_ld[hf], 16)
                    sy.dma_start(out=vtmh.rearrange("p (c t) -> p c t", t=520),
                                 in_=vtm_d[:, hf * 520:(hf + 1) * 520]
                                 .rearrange("(c p) t -> p c t", p=128)).then_inc(s_ld[hf], 16)
                    for r in range(4):
                        for par in range(2):
                            sy.dma_start(
                                out=sgth.rearrange("p (r c e) -> p r c e", r=4, e=65)
                                [par * 64:(par + 1) * 64, r],
                                in_=stall_d.rearrange("(r h p) e -> r p h e", h=H, p=64)
                                [r, :, 8 * hf + par:8 * hf + 8:2, :]).then_inc(s_ld[hf], 16)
                    for lit in range(128):
                        it = hf * 128 + lit
                        sy.wait_ge(s_rd, it + 1)
                        if it >= 4:
                            sy.wait_ge(s_bdB[it % 4], 16 * (it // 4))
                        sy.dma_start(out=drow_dr[it % 4:it % 4 + 1, :],
                                     in_=drow[:, (it % 4) * 128:(it % 4) * 128 + 128]) \
                            .then_inc(s_bdA[it % 4], 16)
                        sy.wait_ge(s_bdA[it % 4], 16 * (it // 4 + 1))
                        if it >= 4:
                            sy.wait_ge(s_sv, it - 3)
                        b = (lit % 8 % 2) * 64
                        sy.dma_start(out=bc64[b:b + 64, (it % 4) * 128:(it % 4) * 128 + 128],
                                     in_=bass.AP(drow_dr.tensor, (it % 4) * 128,
                                                 [[0, 64], [1, 128]])) \
                            .then_inc(s_bdB[it % 4], 16)
                sy.wait_ge(s_sv, 256)
                sy.dma_start(out=sout_d.rearrange("(c p) t -> p c t", p=128),
                             in_=s_sb.rearrange("p (c t) -> p c t", t=TC)).then_inc(s_sst, 16)
                sy.wait_ge(s_sst, 16)

            @blk.tensor
            def _(t):
                t.wait_ge(s_init, 18)
                for hf in range(2):
                    t.wait_ge(s_ld[hf], 16 * 9)
                    t.wait_ge(s_bas, hf + 1)

                    def hb(lit):
                        it = hf * 128 + lit
                        ct, h = lit // 8, lit % 8
                        return (it, ct, h, (h % 2) * 64, (h // 2) * 65,
                                (h // 2) * TC + ct * 128,
                                slice((h % 2) * 64, (h % 2) * 64 + 64),
                                (h // 2) * 128 + (h % 2) * 64)

                    def em_st(lit):
                        it, ct, h, b, cg, hq, rw, off = hb(lit)
                        if lit >= 2:
                            t.wait_ge(s_pt, it - 1)
                        t.matmul(ST(it), kfh[rw, hq:hq + 128], qfh[rw, hq:hq + 128],
                                 start=True, stop=True).then_inc(s_int, 1)

                    def em_mid(lit):
                        it, ct, h, b, cg, hq, rw, off = hb(lit)
                        t.wait_ge(s_pt, it + 1)
                        t.wait_ge(s_stb, hf * 136 + lit + 1 if lit < 8 else hf * 136 + 8 + lit - 7)
                        if lit >= 6:
                            t.wait_ge(s_sv, it - 5)
                            t.wait_ge(s_stb, hf * 136 + 8 + lit - 5)
                        t.matmul(OO(it, b), vtmh[:, ct * 520 + h * 65: ct * 520 + h * 65 + 64],
                                 ptile[:, (it % 4) * 128:(it % 4) * 128 + 128],
                                 start=True, stop=False, tile_position=(0, b))
                        t.matmul(OO(it, b), stB[b:b + 64, cg:cg + 64], qfh[rw, hq:hq + 128],
                                 start=False, stop=True, tile_position=(b, b))
                        t.matmul(DEN(it), vtmh[:, ct * 520 + h * 65 + 64: ct * 520 + h * 65 + 65],
                                 ptile[:, (it % 4) * 128:(it % 4) * 128 + 128],
                                 start=True, stop=False, tile_position=(0, 0))
                        t.matmul(DEN(it), stB[b:b + 64, cg + 64:cg + 65], qfh[rw, hq:hq + 128],
                                 start=False, stop=True, tile_position=(b, 0)).then_inc(s_o, 1)
                        t.matmul(UP(it, b), ktmh[:, ct * 512 + off: ct * 512 + off + 64],
                                 vtmh[:, ct * 520 + h * 65: ct * 520 + (h + 1) * 65],
                                 start=True, stop=True, tile_position=(0, b)).then_inc(s_up, 1)

                    for lit in range(130):
                        if lit < 128:
                            em_st(lit)
                        if lit >= 2:
                            em_mid(lit - 2)

            @blk.vector
            def _(v):
                def VF(inst):
                    vfl[0] += 1
                    inst.then_inc(s_vfl, 1)
                    v.wait_ge(s_vfl, vfl[0])
                    return inst
                for hf in range(2):
                    v.wait_ge(s_ld[hf], 16 * 9)
                    VF(v.memset(stF[:, :], 0.0))
                    for r in range(4):
                        VF(v.scalar_tensor_tensor(
                            stF[:, :],
                            sgth[:, r * (HPH // 2) * 65:(r + 1) * (HPH // 2) * 65],
                            pm[:, r:r + 1], stF[:, :], ALU.mult, ALU.add))
                    for h in range(HPH):
                        b = (h % 2) * 64
                        cg = (h // 2) * 65
                        v.tensor_copy(stB[b:b + 64, cg:cg + 65],
                                      stF[b:b + 64, cg:cg + 65]).then_inc(s_stb, 1)
                    v.drain().then_inc(s_bas, 1)

                    def vb(lit):
                        it = hf * 128 + lit
                        ct, h = lit // 8, lit % 8
                        return it, ct, h, (h % 2) * 64, (h // 2) * 65

                    def vm_pt(lit):
                        it, ct, h, b, cg = vb(lit)
                        v.wait_ge(s_int, it + 1)
                        v.tensor_tensor(ptile[:, (it % 4) * 128:(it % 4) * 128 + 128],
                                        ST(it), trim[:, :], ALU.mult).then_inc(s_pt, 1)

                    def vm_ep(lit):
                        it, ct, h, b, cg = vb(lit)
                        v.wait_ge(s_o, it + 1)
                        if it >= 4:
                            v.wait_ge(s_bdA[it % 4], 16 * (it // 4))
                        VF(v.tensor_scalar(drow[:, (it % 4) * 128:(it % 4) * 128 + 128],
                                           DEN(it), 1e-8, None, ALU.add))
                        v.reciprocal(drow[:, (it % 4) * 128:(it % 4) * 128 + 128],
                                     drow[:, (it % 4) * 128:(it % 4) * 128 + 128]) \
                            .then_inc(s_rd, 1)
                        v.wait_ge(s_up, it + 1)
                        VF(v.tensor_tensor(stF[b:b + 64, cg:cg + 65],
                                           stF[b:b + 64, cg:cg + 65], UP(it, b), ALU.add))
                        v.tensor_copy(stB[b:b + 64, cg:cg + 65],
                                      stF[b:b + 64, cg:cg + 65]).then_inc(s_stb, 1)

                    def vm_so(lit):
                        it, ct, h, b, cg = vb(lit)
                        v.wait_ge(s_bdB[it % 4], 16 * (it // 4 + 1))
                        v.tensor_tensor(s_sb[b:b + 64, (4 * hf + h // 2) * TC + ct * 128:
                                             (4 * hf + h // 2) * TC + ct * 128 + 128],
                                        OO(it, b),
                                        bc64[b:b + 64, (it % 4) * 128:(it % 4) * 128 + 128],
                                        ALU.mult).then_inc(s_sv, 1)

                    for lit in range(134):
                        if lit < 128:
                            vm_pt(lit)
                        if 2 <= lit < 130:
                            vm_ep(lit - 2)
                        if lit >= 6:
                            vm_so(lit - 6)

        with ExitStack() as ctx2:
            en2 = ctx2.enter_context
            gtile = en2(nc.sbuf_tensor("gtile", [128, 2 * NCH * 512], BF16))
            xtile = en2(nc.sbuf_tensor("xtile", [128, 2 * NCH * 512], F32))
            x1stg = en2(nc.sbuf_tensor("x1stg", [128, 2 * NCH * 512], F32))
            t1s = en2(nc.sbuf_tensor("t1s", [128, 512], F32))
            ps2 = en2(nc.psum_tensor("ps2", [128, 2048], F32))
            blk2 = en2(nc.Block())
            WOPS = lambda r: ps2[:, (r % 4) * 512:(r % 4) * 512 + 512]

            @blk2.sync
            def _(sy):
                for tt in range(4):
                    sl = tt % 2
                    if tt >= 2:
                        sy.wait_ge(s_x1, 8 * (tt - 1))
                    sy.dma_start(out=xtile.rearrange("p (s c t) -> p s c t", s=2, t=512)[:, sl],
                                 in_=x_d.rearrange("(c p) t -> p c t", p=128)
                                 [:, :, tt * 512:(tt + 1) * 512]).then_inc(s_gx[sl], 16)
                for tt in range(4):
                    sl = tt % 2
                    sy.wait_ge(s_x1, 8 * (tt + 1))
                    sy.dma_start(out=x1_d.rearrange("(c p) t -> p c t", p=128)
                                 [:, :, tt * 512:(tt + 1) * 512],
                                 in_=x1stg.rearrange("p (s c t) -> p s c t", s=2, t=512)[:, sl]) \
                        .then_inc(s_ost[sl], 16)
                sy.wait_ge(s_ost[0], 32)
                sy.wait_ge(s_ost[1], 32)

            @blk2.gpsimd
            def _(g_):
                for tt in range(4):
                    sl = tt % 2
                    if tt >= 2:
                        g_.wait_ge(s_x1, 8 * (tt - 1))
                    g_.dma_start(out=gtile.rearrange("p (s c t) -> p s c t", s=2, t=512)[:, sl],
                                 in_=g_d.rearrange("(c p) t -> p c t", p=128)
                                 [:, :, tt * 512:(tt + 1) * 512]).then_inc(s_gx[sl], 16)

            @blk2.tensor
            def _(t):
                nwo = 0
                for tt in range(4):
                    sl = tt % 2
                    t.wait_ge(s_gx[sl], 32 * (tt // 2 + 1))
                    for oc in range(NCH):
                        if nwo >= 4:
                            t.wait_ge(s_x1, nwo - 3)
                        for ic in range(NCH):
                            mm = t.matmul(WOPS(nwo),
                                          wot_sb[:, ic * D + oc * 128: ic * D + oc * 128 + 128],
                                          s_sb[:, ic * TC + tt * 512: ic * TC + tt * 512 + 512],
                                          start=(ic == 0), stop=(ic == NCH - 1))
                        mm.then_inc(s_wo, 1)
                        nwo += 1

            @blk2.vector
            def _(v):
                def VF(inst):
                    vfl[0] += 1
                    inst.then_inc(s_vfl, 1)
                    v.wait_ge(s_vfl, vfl[0])
                    return inst
                nwo = 0
                for tt in range(4):
                    sl = tt % 2
                    v.wait_ge(s_gx[sl], 32 * (tt // 2 + 1))
                    if tt >= 2:
                        v.wait_ge(s_ost[sl], 16 * (tt // 2))
                    for oc in range(NCH):
                        v.wait_ge(s_wo, nwo + 1)
                        VF(v.tensor_tensor(t1s[:, :], WOPS(nwo),
                                           gtile[:, sl * NCH * 512 + oc * 512:
                                                 sl * NCH * 512 + (oc + 1) * 512], ALU.mult))
                        v.tensor_tensor(x1stg[:, sl * NCH * 512 + oc * 512:
                                              sl * NCH * 512 + (oc + 1) * 512],
                                        xtile[:, sl * NCH * 512 + oc * 512:
                                              sl * NCH * 512 + (oc + 1) * 512],
                                        t1s[:, :], ALU.add).then_inc(s_x1, 1)
                        nwo += 1

def build_l0_scan():
    nc = bass.Bass("TRN2", target_bir_lowering=False, debug=False)
    DD = _decl_l0_scan(nc)
    emit_l0_scan(nc, DD)
    return nc




def _decl_l1_lat(nc):
    DD = {}
    DD["xn1"] = nc.dram_tensor("xn1", [D, TC], BF16, kind="ExternalInput").ap()
    DD["wk"] = nc.dram_tensor("wk", [D, D], F32, kind="ExternalInput").ap()
    DD["wv"] = nc.dram_tensor("wv", [D, D], F32, kind="ExternalInput").ap()
    DD["wdn"] = nc.dram_tensor("wdn", [D, DLAT], F32, kind="ExternalInput").ap()
    DD["lat"] = nc.dram_tensor("lat", [DLAT, TC], BF16, kind="ExternalOutput").ap()
    return DD

def emit_l1_lat(nc, DD):
    xn_d = DD["xn1"]
    wk_d = DD["wk"]
    wv_d = DD["wv"]
    wdn_d = DD["wdn"]
    lat_d = DD["lat"]
    with ExitStack() as ctx:
        ent = ctx.enter_context
        xnsb = ent(nc.sbuf_tensor("xnsb", [128, NCH * TC], BF16))
        wk = ent(nc.sbuf_tensor("wk_sb", [128, NCH * D], BF16))
        wv = ent(nc.sbuf_tensor("wv_sb", [128, NCH * D], BF16))
        wdn = ent(nc.sbuf_tensor("wdn_sb", [128, NCH * DLAT], BF16))
        kvf = ent(nc.sbuf_tensor("kvf", [128, NCH * TC], BF16))
        lstg = ent(nc.sbuf_tensor("lstg", [128, 2 * 2 * 512], BF16))
        ps = ent(nc.psum_tensor("ps", [128, 4096], F32))
        s_init = ent(nc.semaphore("s_init"))
        s_kv = ent(nc.semaphore("s_kv"))
        s_kvv = ent(nc.semaphore("s_kvv"))
        s_lt = ent(nc.semaphore("s_lt"))
        s_ltv = ent(nc.semaphore("s_ltv"))
        s_ost = [ent(nc.semaphore(f"s_ost{i}")) for i in range(2)]
        block = ent(nc.Block())
        PRJ = lambda r: ps[:, (r % 4) * 512:(r % 4) * 512 + 512]
        LPS = lambda r: ps[:, 2048 + (r % 2) * 1024: 2048 + (r % 2) * 1024 + 1024]

        @block.gpsimd
        def _(g_):
            g_.dma_start(out=wk.rearrange("p (c f) -> p c f", f=D),
                         in_=wk_d.rearrange("(c p) f -> p c f", p=128)).then_inc(s_init, 16)
            g_.dma_start(out=wv.rearrange("p (c f) -> p c f", f=D),
                         in_=wv_d.rearrange("(c p) f -> p c f", p=128)).then_inc(s_init, 16)
            g_.dma_start(out=wdn.rearrange("p (c f) -> p c f", f=DLAT),
                         in_=wdn_d.rearrange("(c p) f -> p c f", p=128)).then_inc(s_init, 16)

        @block.sync
        def _(sy):
            sy.dma_start(out=xnsb.rearrange("p (c t) -> p c t", t=TC),
                         in_=xn_d.rearrange("(c p) t -> p c t", p=128)).then_inc(s_init, 16)
            for i in range(4):
                sy.wait_ge(s_ltv, i + 1)
                sy.dma_start(out=lat_d.rearrange("(c p) t -> p c t", p=128)
                             [:, :, i * 512:(i + 1) * 512],
                             in_=lstg.rearrange("p (s c t) -> p s c t", s=2, t=512)[:, i % 2]) \
                    .then_inc(s_ost[i % 2], 16)
            sy.wait_ge(s_ost[0], 32)
            sy.wait_ge(s_ost[1], 32)

        @block.tensor
        def _(t):
            t.wait_ge(s_init, 64)
            nkv = 0
            for oc in range(NCH):
                for ti in range(4):
                    if nkv >= 4:
                        t.wait_ge(s_kvv, nkv - 3)
                    for ic in range(NCH):
                        t.matmul(PRJ(nkv), wk[:, ic * D + oc * 128: ic * D + oc * 128 + 128],
                                 xnsb[:, ic * TC + ti * 512: ic * TC + ti * 512 + 512],
                                 start=(ic == 0), stop=False)
                        mm = t.matmul(PRJ(nkv), wv[:, ic * D + oc * 128: ic * D + oc * 128 + 128],
                                      xnsb[:, ic * TC + ti * 512: ic * TC + ti * 512 + 512],
                                      start=False, stop=(ic == NCH - 1))
                    mm.then_inc(s_kv, 1)
                    nkv += 1
            for ti in range(4):
                if ti >= 2:
                    t.wait_ge(s_ltv, ti - 1)
                t.wait_ge(s_kvv, 32)
                for lc in range(2):
                    for ic in range(NCH):
                        mm = t.matmul(LPS(ti)[:, lc * 512:(lc + 1) * 512],
                                      wdn[:, ic * DLAT + lc * 128: ic * DLAT + lc * 128 + 128],
                                      kvf[:, ic * TC + ti * 512: ic * TC + ti * 512 + 512],
                                      start=(ic == 0), stop=(ic == NCH - 1))
                mm.then_inc(s_lt, 1)

        @block.vector
        def _(v):
            nkv = 0
            for oc in range(NCH):
                for ti in range(4):
                    v.wait_ge(s_kv, nkv + 1)
                    v.tensor_copy(kvf[:, oc * TC + ti * 512: oc * TC + ti * 512 + 512],
                                  PRJ(nkv)).then_inc(s_kvv, 1)
                    nkv += 1
            for ti in range(4):
                v.wait_ge(s_lt, ti + 1)
                if ti >= 2:
                    v.wait_ge(s_ost[ti % 2], 16 * (ti // 2))
                v.tensor_copy(lstg[:, (ti % 2) * 1024:(ti % 2) * 1024 + 1024],
                              LPS(ti)).then_inc(s_ltv, 1)

def build_l1_lat():
    nc = bass.Bass("TRN2", target_bir_lowering=False, debug=False)
    DD = _decl_l1_lat(nc)
    emit_l1_lat(nc, DD)
    return nc







def _decl_l1_attn(nc):
    DD = {}
    NJT = S // 128
    DD["xn1"] = nc.dram_tensor("xn1", [D, TC], BF16, kind="ExternalInput").ap()
    DD["latf"] = nc.dram_tensor("latf", [DLAT, S], BF16, kind="ExternalInput").ap()
    DD["lato"] = nc.dram_tensor("lato", [DLAT, TC], BF16, kind="ExternalInput").ap()
    DD["wq"] = nc.dram_tensor("wq", [D, D], F32, kind="ExternalInput").ap()
    DD["wupk"] = nc.dram_tensor("wupk", [DLAT, D], F32, kind="ExternalInput").ap()
    DD["wupv"] = nc.dram_tensor("wupv", [DLAT, D], F32, kind="ExternalInput").ap()
    DD["posq"] = nc.dram_tensor("posq", [1, TC], F32, kind="ExternalInput").ap()
    DD["posk"] = nc.dram_tensor("posk", [1, S], F32, kind="ExternalInput").ap()
    DD["amask"] = nc.dram_tensor("amask", [1, S], F32, kind="ExternalInput").ap()
    DD["bmask"] = nc.dram_tensor("bmask", [1, S], F32, kind="ExternalInput").ap()
    DD["biascol"] = nc.dram_tensor("biascol", [128, NJT], F32, kind="ExternalInput").ap()
    DD["s1"] = nc.dram_tensor("s1", [D, TC], BF16, kind="ExternalOutput").ap()

    return DD

def emit_l1_attn(nc, DD, gathered=False):
    CDEC = 1e-4
    SCL = float(1.0 / (np.sqrt(np.float32(1024.0)) + 1e-8))
    NJT = S // 128
    NJO = TC // 128
    xn_d = DD["xn1"]
    latf_d = DD["latf"]
    lato_d = DD["lato"]
    wq_d = DD["wq"]
    wupk_d = DD["wupk"]
    wupv_d = DD["wupv"]
    posq_d = DD["posq"]
    posk_d = DD["posk"]
    am_d = DD["amask"]
    bm_d = DD["bmask"]
    bc_d = DD["biascol"]
    s1_d = DD["s1"]
    with ExitStack() as ctx:
        ent = ctx.enter_context
        xnsb = ent(nc.sbuf_tensor("xnsb", [128, NCH * TC], BF16))
        latf = ent(nc.sbuf_tensor("latf_sb", [128, 2 * S], BF16))
        lato = ent(nc.sbuf_tensor("lato_sb", [128, 2 * TC], BF16))
        wq = ent(nc.sbuf_tensor("wq_sb", [128, NCH * D], BF16))
        wupk = ent(nc.sbuf_tensor("wupk_sb", [128, 2 * D], BF16))
        wupv = ent(nc.sbuf_tensor("wupv_sb", [128, 2 * D], BF16))
        rowbuf = ent(nc.sbuf_tensor("rowbuf", [1, 5 * 512], F32))
        qfac = ent(nc.sbuf_tensor("qfac", [128, TC], F32))
        kfsel = ent(nc.sbuf_tensor("kfsel", [128, S], BF16))
        kfown = ent(nc.sbuf_tensor("kfown", [128, TC], BF16))
        corr = ent(nc.sbuf_tensor("corr", [128, 4 * 512], F32))
        bcol = ent(nc.sbuf_tensor("bcol", [128, NJT], F32))
        zcol = ent(nc.sbuf_tensor("zcol", [128, 1], F32))
        onesf = ent(nc.sbuf_tensor("onesf", [128, 128], F32))
        qh = ent(nc.sbuf_tensor("qh", [128, TC], BF16))
        khg = ent(nc.sbuf_tensor("khg", [128, S], BF16))
        kho = ent(nc.sbuf_tensor("kho", [128, TC], BF16))
        vstg = ent(nc.sbuf_tensor("vstg", [128, NJT * 65], BF16))
        vsto = ent(nc.sbuf_tensor("vsto", [128, NJO * 65], BF16))
        ptile = ent(nc.sbuf_tensor("ptile", [128, 4 * 1024], BF16))
        stile = ent(nc.sbuf_tensor("stile", [128, 2 * 512], F32))
        drow = ent(nc.sbuf_tensor("drow", [1, 2 * 512], F32))
        bc64a = ent(nc.sbuf_tensor("bc64a", [128, 2 * 512], F32))
        s1t = ent(nc.sbuf_tensor("s1t", [64, 2 * 512], BF16))
        ps = ent(nc.psum_tensor("ps", [128, 4096], F32))
        s_init = ent(nc.semaphore("s_init"))
        s_gfl = ent(nc.semaphore("s_gfl"))
        s_vfl = ent(nc.semaphore("s_vfl"))
        s_rw = ent(nc.semaphore("s_rw"))
        s_fd = ent(nc.semaphore("s_fd"))
        s_fexp = ent(nc.semaphore("s_fexp"))
        s_fmul = ent(nc.semaphore("s_fmul"))
        s_fb = ent(nc.semaphore("s_fb"))
        s_fv = ent(nc.semaphore("s_fv"))
        s_qp = ent(nc.semaphore("s_qp"))
        s_qv = ent(nc.semaphore("s_qv"))
        s_st = ent(nc.semaphore("s_st"))
        s_pt = ent(nc.semaphore("s_pt"))
        s_cr = ent(nc.semaphore("s_cr"))
        s_pv = ent(nc.semaphore("s_pv"))
        s_pp = ent(nc.semaphore("s_pp"))
        s_rd = ent(nc.semaphore("s_rd"))
        s_bd1 = [ent(nc.semaphore(f"s_bd1{i}")) for i in range(2)]
        s_bd2 = [ent(nc.semaphore(f"s_bd2{i}")) for i in range(2)]
        s_sv = ent(nc.semaphore("s_sv"))
        s_ost = [ent(nc.semaphore(f"s_ost{i}")) for i in range(2)]
        block = ent(nc.Block())

        STp = lambda r: ps[:, (r % 3) * 1024:(r % 3) * 1024 + 1024]
        PV = lambda r: ps[0:65, 3072 + (r % 2) * 512: 3072 + (r % 2) * 512 + 512]
        PRD = lambda r: ps[:, (r % 2) * 512:(r % 2) * 512 + 512]
        drow_dr = nc.dram_tensor("attn_drow_dr", [2, 512], F32, kind="Internal").ap()

        INIT_TOT = 16 * (14 if gathered else 7) + 4
        FPLAN = [("q", i) for i in range(4)] + [("k", i) for i in range(16)] \
            + [("o", i) for i in range(4)]
        RB = lambda sl: rowbuf[:, sl * 512:(sl + 1) * 512]

        @block.gpsimd
        def _(g_):
            g_.dma_start(out=wq.rearrange("p (c f) -> p c f", f=D),
                         in_=wq_d.rearrange("(c p) f -> p c f", p=128)).then_inc(s_init, 16)
            g_.dma_start(out=wupk.rearrange("p (c f) -> p c f", f=D),
                         in_=wupk_d.rearrange("(c p) f -> p c f", p=128)).then_inc(s_init, 16)
            g_.dma_start(out=wupv.rearrange("p (c f) -> p c f", f=D),
                         in_=wupv_d.rearrange("(c p) f -> p c f", p=128)).then_inc(s_init, 16)
            for r in range(4):
                g_.iota(corr[:, r * 512:(r + 1) * 512], [[-1, 512]], base=r * 128,
                        channel_multiplier=1,
                        allow_small_or_imprecise_dtypes=True).then_inc(s_gfl, 1)
            g_.wait_ge(s_gfl, 4)
            g_.tensor_scalar_max(corr[:, :], corr[:, :], 0.0).then_inc(s_gfl, 1)
            g_.memset(onesf[:, :], 1.0).then_inc(s_init, 1)
            g_.memset(vstg[:, :], 1.0).then_inc(s_init, 1)
            g_.memset(vsto[:, :], 1.0).then_inc(s_init, 1)
            g_.memset(zcol[:, :], 0.0).then_inc(s_init, 1)

        @block.sync
        def _(sy):
            sy.dma_start(out=xnsb.rearrange("p (c t) -> p c t", t=TC),
                         in_=xn_d.rearrange("(c p) t -> p c t", p=128)).then_inc(s_init, 16)
            if gathered:
                for rr in range(4):
                    for lc in range(2):
                        sy.dma_start(out=latf[:, lc * S + rr * TC: lc * S + (rr + 1) * TC],
                                     in_=latf_d[rr * 256 + lc * 128: rr * 256 + lc * 128 + 128, :]) \
                            .then_inc(s_init, 16)
            else:
                sy.dma_start(out=latf.rearrange("p (c t) -> p c t", t=S),
                             in_=latf_d.rearrange("(c p) t -> p c t", p=128)).then_inc(s_init, 16)
            sy.dma_start(out=lato.rearrange("p (c t) -> p c t", t=TC),
                         in_=lato_d.rearrange("(c p) t -> p c t", p=128)).then_inc(s_init, 16)
            sy.dma_start(out=bcol[:, :], in_=bc_d[:, :]).then_inc(s_init, 16)
            nfd = 0
            for p, (kind, idx) in enumerate(FPLAN):
                if p >= 1:
                    sy.wait_ge(s_fv, p)
                src = posk_d if kind == "k" else posq_d
                sy.dma_start(out=RB(0), in_=src[:, idx * 512:(idx + 1) * 512]) \
                    .then_inc(s_fd, 16)
                nfd += 16
                if kind == "k":
                    sy.dma_start(out=RB(1), in_=am_d[:, idx * 512:(idx + 1) * 512]) \
                        .then_inc(s_fd, 16)
                    sy.dma_start(out=RB(2), in_=bm_d[:, idx * 512:(idx + 1) * 512]) \
                        .then_inc(s_fd, 16)
                    nfd += 32
            for h in range(H):
                for it in range(4):
                    n = h * 4 + it
                    sy.wait_ge(s_rd, n + 1)
                    if n >= 2:
                        sy.wait_ge(s_bd2[n % 2], 16 * (n // 2))
                    sy.dma_start(out=drow_dr[n % 2:n % 2 + 1, :],
                                 in_=drow[:, (n % 2) * 512:(n % 2) * 512 + 512]) \
                        .then_inc(s_bd1[n % 2], 16)
                    sy.wait_ge(s_bd1[n % 2], 16 * (n // 2 + 1))
                    if n >= 2:
                        sy.wait_ge(s_sv, n - 1)
                    sy.dma_start(out=bc64a[0:64, (n % 2) * 512:(n % 2) * 512 + 512],
                                 in_=bass.AP(drow_dr.tensor, (n % 2) * 512,
                                             [[0, 64], [1, 512]])) \
                        .then_inc(s_bd2[n % 2], 16)
                    sy.wait_ge(s_sv, n + 1)
                    sy.dma_start(out=s1_d[h * 64:(h + 1) * 64, it * 512:(it + 1) * 512],
                                 in_=s1t[:, (n % 2) * 512:(n % 2) * 512 + 512]) \
                        .then_inc(s_ost[n % 2], 16)
            sy.wait_ge(s_ost[0], 16 * 32)
            sy.wait_ge(s_ost[1], 16 * 32)

        @block.scalar
        def _(s):
            s.wait_ge(s_init, INIT_TOT)
            s.wait_ge(s_gfl, 5)
            s.activation(corr[:, :], corr[:, :], AF.Exp, scale=-2.0 * CDEC).then_inc(s_rw, 1)
            nfd = 0
            for p, (kind, idx) in enumerate(FPLAN):
                nfd += 48 if kind == "k" else 16
                s.wait_ge(s_fd, nfd)
                sc0 = -CDEC if kind == "q" else CDEC
                s.activation(RB(3), RB(0), AF.Exp, scale=sc0)
                s.activation(RB(4), RB(0), AF.Exp, scale=-sc0).then_inc(s_fexp, 1)
            for h in range(H):
                for it in range(4):
                    hit = h * 4 + it
                    for Jp in range(40):
                        nst = hit * 40 + Jp
                        sl = (nst % 4) * 1024
                        s.wait_ge(s_st, nst + 1)
                        if nst >= 4:
                            s.wait_ge(s_pp, nst - 3)
                        if Jp < 32:
                            s.activation(ptile[:, sl:sl + 1024], STp(nst),
                                         AF.Exp, bias=bcol[:, 2 * Jp:2 * Jp + 1]) \
                                .then_inc(s_pt, 1)
                        else:
                            Jo0 = 2 * (Jp - 32)
                            d0, d1 = Jo0 - it * 4, Jo0 + 1 - it * 4
                            diag0, diag1 = 0 <= d0 < 4, 0 <= d1 < 4
                            if not diag0 and not diag1:
                                s.activation(ptile[:, sl:sl + 1024], STp(nst),
                                             AF.Exp).then_inc(s_pt, 1)
                            else:
                                insts = []
                                for half, dd, dg in ((0, d0, diag0), (1, d1, diag1)):
                                    dst = ptile[:, sl + half * 512: sl + (half + 1) * 512]
                                    if dg:
                                        s.wait_ge(s_cr, hit * 4 + dd + 1)
                                        insts.append(s.activation(
                                            dst, stile[:, (dd % 2) * 512:(dd % 2) * 512 + 512],
                                            AF.Exp))
                                    else:
                                        insts.append(s.activation(
                                            dst, STp(nst)[:, half * 512:(half + 1) * 512],
                                            AF.Exp))
                                insts[-1].then_inc(s_pt, 1)

        @block.tensor
        def _(t):
            t.wait_ge(s_init, INIT_TOT)
            for p, (kind, idx) in enumerate(FPLAN):
                t.wait_ge(s_fmul, p + 1)
                for ver in range(2):
                    mm = t.matmul(PRD(0)[ver * 64:(ver + 1) * 64, :], onesf[0:1, 0:64],
                                  RB(3 + ver), start=True, stop=True,
                                  tile_position=(0, ver * 64))
                mm.then_inc(s_fb, 1)
            nprd = [48]
            nst = 0
            def _emit_pv(t, hit, Jp):
                it = hit % 4
                npv = hit * 40 + Jp
                t.wait_ge(s_pt, npv + 1)
                for half in range(2):
                    if Jp < 32:
                        J = 2 * Jp + half
                        vsl = vstg[:, J * 65:(J + 1) * 65]
                    else:
                        Jo = 2 * (Jp - 32) + half
                        vsl = vsto[:, Jo * 65:(Jo + 1) * 65]
                    mm = t.matmul(PV(it), vsl,
                                  ptile[:, (npv % 4) * 1024 + half * 512:
                                        (npv % 4) * 1024 + (half + 1) * 512],
                                  start=(Jp == 0 and half == 0),
                                  stop=(Jp == 39 and half == 1))
                mm.then_inc(s_pp, 1)
                return mm
            t.wait_ge(s_fv, 24)
            def pguard():
                pi = nprd[0] - 48
                if pi >= 2:
                    t.wait_ge(s_qv, pi - 1)
            for h in range(H):
                hw = h * 64
                if h >= 1:
                    t.wait_ge(s_pt, 160 * h)
                for tt in range(4):
                    pguard()
                    for ver in range(2):
                        for ic in range(NCH):
                            mm = t.matmul(PRD(nprd[0])[ver * 64:(ver + 1) * 64, :],
                                          wq[:, ic * D + hw: ic * D + hw + 64],
                                          xnsb[:, ic * TC + tt * 512: ic * TC + (tt + 1) * 512],
                                          start=(ic == 0), stop=(ic == NCH - 1),
                                          tile_position=(0, ver * 64))
                    mm.then_inc(s_qp, 1)
                    nprd[0] += 1
                for jt in range(16):
                    pguard()
                    for ver in range(2):
                        for lc in range(2):
                            mm = t.matmul(PRD(nprd[0])[ver * 64:(ver + 1) * 64, :],
                                          wupk[:, lc * D + hw: lc * D + hw + 64],
                                          latf[:, lc * S + jt * 512: lc * S + (jt + 1) * 512],
                                          start=(lc == 0), stop=(lc == 1),
                                          tile_position=(0, ver * 64))
                    mm.then_inc(s_qp, 1)
                    nprd[0] += 1
                for jt in range(4):
                    pguard()
                    for ver in range(2):
                        for lc in range(2):
                            mm = t.matmul(PRD(nprd[0])[ver * 64:(ver + 1) * 64, :],
                                          wupk[:, lc * D + hw: lc * D + hw + 64],
                                          lato[:, lc * TC + jt * 512: lc * TC + (jt + 1) * 512],
                                          start=(lc == 0), stop=(lc == 1),
                                          tile_position=(0, ver * 64))
                    mm.then_inc(s_qp, 1)
                    nprd[0] += 1
                for pk in range(8):
                    pguard()
                    for j8 in range(8):
                        jt = pk * 8 + j8
                        for lc in range(2):
                            mm = t.matmul(PRD(nprd[0])[:, j8 * 64:(j8 + 1) * 64],
                                          latf[:, lc * S + jt * 128: lc * S + jt * 128 + 128],
                                          wupv[:, lc * D + hw: lc * D + hw + 64],
                                          start=(lc == 0), stop=(lc == 1))
                    mm.then_inc(s_qp, 1)
                    nprd[0] += 1
                for pk in range(2):
                    pguard()
                    for j8 in range(8):
                        jt = pk * 8 + j8
                        for lc in range(2):
                            mm = t.matmul(PRD(nprd[0])[:, j8 * 64:(j8 + 1) * 64],
                                          lato[:, lc * TC + jt * 128: lc * TC + jt * 128 + 128],
                                          wupv[:, lc * D + hw: lc * D + hw + 64],
                                          start=(lc == 0), stop=(lc == 1))
                    mm.then_inc(s_qp, 1)
                    nprd[0] += 1
                t.wait_ge(s_qv, (h + 1) * 34)
                for it in range(4):
                    hit = h * 4 + it
                    if hit >= 2:
                        t.wait_ge(s_sv, hit - 1)
                    for Jp in range(40):
                        if nst >= 3:
                            t.wait_ge(s_pt, nst - 2)
                        sl = STp(nst)
                        if Jp >= 2:
                            _emit_pv(t, hit, Jp - 2)
                        for half in range(2):
                            if Jp < 32:
                                J = 2 * Jp + half
                                mm = t.matmul(sl[:, half * 512:(half + 1) * 512],
                                              khg[:, J * 128:(J + 1) * 128],
                                              qh[:, it * 512:(it + 1) * 512],
                                              start=True, stop=True)
                            else:
                                Jo = 2 * (Jp - 32) + half
                                d = Jo - it * 4
                                if d >= 4:
                                    mm = t.matmul(sl[:, half * 512:(half + 1) * 512],
                                                  kho[64:128, Jo * 128:(Jo + 1) * 128],
                                                  qh[64:128, it * 512:(it + 1) * 512],
                                                  start=True, stop=True,
                                                  tile_position=(64, 0))
                                else:
                                    mm = t.matmul(sl[:, half * 512:(half + 1) * 512],
                                                  kho[0:64, Jo * 128:(Jo + 1) * 128],
                                                  qh[0:64, it * 512:(it + 1) * 512],
                                                  start=True, stop=True)
                        mm.then_inc(s_st, 1)
                        nst += 1
                    _emit_pv(t, hit, 38)
                    _emit_pv(t, hit, 39)
                    t.drain().then_inc(s_pv, 1)

        @block.vector
        def _(v):
            vfl = [0]
            def VF(inst):
                vfl[0] += 1
                inst.then_inc(s_vfl, 1)
                v.wait_ge(s_vfl, vfl[0])
                return inst
            v.wait_ge(s_init, INIT_TOT)
            for p, (kind, idx) in enumerate(FPLAN):
                v.wait_ge(s_fexp, p + 1)
                if kind == "q":
                    VF(v.tensor_scalar(RB(3), RB(3), SCL, None, ALU.mult))
                    v.tensor_scalar(RB(4), RB(4), SCL, None, ALU.mult).then_inc(s_fmul, 1)
                elif kind == "k":
                    VF(v.tensor_tensor(RB(3), RB(3), RB(1), ALU.mult))
                    v.tensor_tensor(RB(4), RB(4), RB(2), ALU.mult).then_inc(s_fmul, 1)
                else:
                    VF(v.tensor_scalar(RB(3), RB(3), 1.0, None, ALU.mult))
                    v.tensor_scalar(RB(4), RB(4), 1.0, None, ALU.mult).then_inc(s_fmul, 1)
                v.wait_ge(s_fb, p + 1)
                dst = {"q": qfac, "k": kfsel, "o": kfown}[kind]
                v.tensor_copy(dst[:, idx * 512:(idx + 1) * 512], PRD(0)).then_inc(s_fv, 1)
            for h in range(H):
                nq = 0
                for tt in range(4):
                    v.wait_ge(s_qp, h * 34 + nq + 1)
                    v.tensor_tensor(qh[:, tt * 512:(tt + 1) * 512], PRD(48 + h * 34 + nq),
                                    qfac[:, tt * 512:(tt + 1) * 512], ALU.mult) \
                        .then_inc(s_qv, 1)
                    nq += 1
                for jt in range(16):
                    v.wait_ge(s_qp, h * 34 + nq + 1)
                    v.tensor_tensor(khg[:, jt * 512:(jt + 1) * 512], PRD(48 + h * 34 + nq),
                                    kfsel[:, jt * 512:(jt + 1) * 512], ALU.mult) \
                        .then_inc(s_qv, 1)
                    nq += 1
                for jt in range(4):
                    v.wait_ge(s_qp, h * 34 + nq + 1)
                    v.tensor_tensor(kho[:, jt * 512:(jt + 1) * 512], PRD(48 + h * 34 + nq),
                                    kfown[:, jt * 512:(jt + 1) * 512], ALU.mult) \
                        .then_inc(s_qv, 1)
                    nq += 1
                for pk in range(8):
                    v.wait_ge(s_qp, h * 34 + nq + 1)
                    v.tensor_copy(vstg.rearrange("p (j e) -> p j e", e=65)
                                  [:, pk * 8:(pk + 1) * 8, 0:64],
                                  PRD(48 + h * 34 + nq).rearrange("p (j e) -> p j e", e=64)) \
                        .then_inc(s_qv, 1)
                    nq += 1
                for pk in range(2):
                    v.wait_ge(s_qp, h * 34 + nq + 1)
                    v.tensor_copy(vsto.rearrange("p (j e) -> p j e", e=65)
                                  [:, pk * 8:(pk + 1) * 8, 0:64],
                                  PRD(48 + h * 34 + nq).rearrange("p (j e) -> p j e", e=64)) \
                        .then_inc(s_qv, 1)
                    nq += 1
                for it in range(4):
                    hit = h * 4 + it
                    for d in range(4):
                        Jo = it * 4 + d
                        nst_d = hit * 40 + 32 + Jo // 2
                        v.wait_ge(s_st, nst_d + 1)
                        if hit >= 1 and d < 2:
                            v.wait_ge(s_pt, hit * 40)
                        if d >= 2:
                            v.wait_ge(s_pt, hit * 40 + 32 + (it * 4 + d - 2) // 2 + 1)
                        v.tensor_tensor(stile[:, (d % 2) * 512:(d % 2) * 512 + 512],
                                        STp(nst_d)[:, (Jo % 2) * 512:(Jo % 2) * 512 + 512],
                                        corr[:, d * 512:(d + 1) * 512], ALU.mult) \
                            .then_inc(s_cr, 1)
                    v.wait_ge(s_pv, hit + 1)
                    if hit >= 2:
                        v.wait_ge(s_bd1[hit % 2], 16 * (hit // 2))
                    VF(v.tensor_copy(drow[:, (hit % 2) * 512:(hit % 2) * 512 + 512],
                                     PV(it)[64:65, :]))
                    v.reciprocal(drow[:, (hit % 2) * 512:(hit % 2) * 512 + 512],
                                 drow[:, (hit % 2) * 512:(hit % 2) * 512 + 512]) \
                        .then_inc(s_rd, 1)
                    v.wait_ge(s_bd2[hit % 2], 16 * (hit // 2 + 1))
                    if hit >= 2:
                        v.wait_ge(s_ost[hit % 2], 16 * (hit // 2))
                    v.tensor_tensor(s1t[:, (hit % 2) * 512:(hit % 2) * 512 + 512],
                                    PV(it)[0:64, :],
                                    bc64a[0:64, (hit % 2) * 512:(hit % 2) * 512 + 512],
                                    ALU.mult).then_inc(s_sv, 1)

def build_l1_attn():
    nc = bass.Bass("TRN2", target_bir_lowering=False, debug=False)
    DD = _decl_l1_attn(nc)
    emit_l1_attn(nc, DD)
    return nc



def _decl_l1_ffn(nc):
    DD = {}
    DD["xn1"] = nc.dram_tensor("xn1", [D, TC], BF16, kind="ExternalInput").ap()
    DD["wgt"] = nc.dram_tensor("wgt", [D, HID], I32, kind="ExternalInput").ap()
    DD["wut"] = nc.dram_tensor("wut", [D, HID], I32, kind="ExternalInput").ap()
    DD["wdt"] = nc.dram_tensor("wdt", [HID, D], I32, kind="ExternalInput").ap()
    DD["s1"] = nc.dram_tensor("s1", [D, TC], BF16, kind="ExternalInput").ap()
    DD["wo"] = nc.dram_tensor("wo", [D, D], F32, kind="ExternalInput").ap()
    DD["x1"] = nc.dram_tensor("x1", [D, TC], F32, kind="ExternalInput").ap()
    DD["x2"] = nc.dram_tensor("x2", [D, TC], F32, kind="ExternalOutput").ap()
    DD["h_scr"] = nc.dram_tensor("h_scr", [HID, TC], BF16)

    return DD

def emit_l1_ffn(nc, DD):
    NHC = HID // 128
    QS = float(1.0 / 127.0)
    xn_d = DD["xn1"]
    wgt_d = DD["wgt"]
    wut_d = DD["wut"]
    wdt_d = DD["wdt"]
    s1_d = DD["s1"]
    wo_d = DD["wo"]
    x1_d = DD["x1"]
    x2_d = DD["x2"]
    h_d = DD["h_scr"]
    with ExitStack() as ctx:
        ent = ctx.enter_context
        xnsb = ent(nc.sbuf_tensor("xnsb", [128, NCH * TC], BF16))
        wo_sb = ent(nc.sbuf_tensor("wo_sb", [128, NCH * D], BF16))
        wgi = ent(nc.sbuf_tensor("wgi", [128, 3 * NCH * 128], I32))
        wui = ent(nc.sbuf_tensor("wui", [128, 3 * NCH * 128], I32))
        wgb = ent(nc.sbuf_tensor("wgb", [128, 3 * NCH * 128], BF16))
        wub = ent(nc.sbuf_tensor("wub", [128, 3 * NCH * 128], BF16))
        _gs_off = nc.sbuf_base
        gsil = ent(nc.sbuf_tensor("gsil", [128, 2 * 1024], BF16))
        hstg = ent(nc.sbuf_tensor("hstg", [128, 1024], BF16))
        wdi = ent(nc.sbuf_tensor("wdi", [128, 2 * NHC * 128], I32))
        wdb = ent(nc.sbuf_tensor("wdb", [128, 2 * NHC * 128], BF16))
        htl = ent(nc.sbuf_tensor("htl", [128, 2 * NHC * 512], BF16))
        s1t = ent(nc.sbuf_tensor("s1t", [128, 2 * NCH * 512], BF16))
        x1t = ent(nc.sbuf_tensor("x1t", [128, NCH * 512], F32))
        g1t = ent(nc.sbuf_tensor("g1t", [128, 512], BF16))
        t1s = nc.alloc_sbuf_tensor_at("t1s", [128, 512], F32, offset=_gs_off)
        ps = ent(nc.psum_tensor("ps", [128, 4096], F32))
        s_init = ent(nc.semaphore("s_init"))
        s_vfl = ent(nc.semaphore("s_vfl"))
        s_wg = [ent(nc.semaphore(f"s_wg{i}")) for i in range(3)]
        s_wdq = ent(nc.semaphore("s_wdq"))
        s_gu = ent(nc.semaphore("s_gu"))
        s_sil = ent(nc.semaphore("s_sil"))
        s_hv = ent(nc.semaphore("s_hv"))
        s_hst = [ent(nc.semaphore(f"s_hst{i}")) for i in range(2)]
        s_hld = [ent(nc.semaphore(f"s_hld{i}")) for i in range(2)]
        s_sx = [ent(nc.semaphore(f"s_sx{i}")) for i in range(2)]
        s_wd = [ent(nc.semaphore(f"s_wd{i}")) for i in range(2)]
        s_wdb = ent(nc.semaphore("s_wdb"))
        s_dn = ent(nc.semaphore("s_dn"))
        s_wo = ent(nc.semaphore("s_wo"))
        s_g1 = ent(nc.semaphore("s_g1"))
        s_x2 = ent(nc.semaphore("s_x2"))
        s_ost = [ent(nc.semaphore(f"s_ost{i}")) for i in range(2)]
        block = ent(nc.Block())

        GPS = lambda r: ps[:, (r % 2) * 1024:(r % 2) * 1024 + 1024]
        UPS = lambda r: ps[:, 2048 + (r % 2) * 1024: 2048 + (r % 2) * 1024 + 1024]
        DPS = lambda r: ps[:, (r % 2) * 512:(r % 2) * 512 + 512]
        WPS = lambda r: ps[:, 1024 + (r % 2) * 512: 1024 + (r % 2) * 512 + 512]

        @block.gpsimd
        def _(g_):
            g_.dma_start(out=wo_sb.rearrange("p (c f) -> p c f", f=D),
                         in_=wo_d.rearrange("(c p) f -> p c f", p=128)).then_inc(s_init, 16)
            for tt in range(4):
                sl = tt % 2
                if tt >= 2:
                    g_.wait_ge(s_x2, 8 * (tt - 1))
                g_.wait_ge(s_hst[0], 16 * 2 * NHC)
                g_.dma_start(out=htl[:, sl * NHC * 512:(sl + 1) * NHC * 512]
                             .rearrange("p (c t) -> p c t", t=512),
                             in_=h_d.rearrange("(c p) t -> p c t", p=128)
                             [:, :, tt * 512:(tt + 1) * 512]).then_inc(s_hld[sl], 16)

        @block.sync
        def _(sy):
            sy.dma_start(out=xnsb.rearrange("p (c t) -> p c t", t=TC),
                         in_=xn_d.rearrange("(c p) t -> p c t", p=128)).then_inc(s_init, 16)
            for k in range(2 * NHC):
                th, hc = k // NHC, k % NHC
                sl = k % 3
                if k >= 3:
                    sy.wait_ge(s_wdq, 2 * (k - 2))
                sy.dma_start(out=wgi[:, sl * 1024:(sl + 1) * 1024]
                             .rearrange("p (c f) -> p c f", f=128),
                             in_=wgt_d[:, hc * 128:(hc + 1) * 128]
                             .rearrange("(c p) f -> p c f", p=128)).then_inc(s_wg[sl], 16)
                sy.dma_start(out=wui[:, sl * 1024:(sl + 1) * 1024]
                             .rearrange("p (c f) -> p c f", f=128),
                             in_=wut_d[:, hc * 128:(hc + 1) * 128]
                             .rearrange("(c p) f -> p c f", p=128)).then_inc(s_wg[sl], 16)
                sy.wait_ge(s_hv, 2 * k + 2)
                sy.dma_start(out=h_d[hc * 128:(hc + 1) * 128, th * 1024:(th + 1) * 1024],
                             in_=hstg[:, :]).then_inc(s_hst[0], 16)
            for tt in range(4):
                sl = tt % 2
                if tt >= 2:
                    sy.wait_ge(s_x2, 8 * (tt - 1))
                sy.dma_start(out=s1t[:, sl * NCH * 512:(sl + 1) * NCH * 512]
                             .rearrange("p (c t) -> p c t", t=512),
                             in_=s1_d.rearrange("(c p) t -> p c t", p=128)
                             [:, :, tt * 512:(tt + 1) * 512]).then_inc(s_sx[sl], 16)
                if tt >= 1:
                    sy.wait_ge(s_ost[0], 16 * tt)
                sy.dma_start(out=x1t.rearrange("p (c t) -> p c t", t=512),
                             in_=x1_d.rearrange("(c p) t -> p c t", p=128)
                             [:, :, tt * 512:(tt + 1) * 512]).then_inc(s_sx[sl], 16)
                for oc in range(NCH):
                    n = tt * NCH + oc
                    so = n % 2
                    if n >= 2:
                        sy.wait_ge(s_wdb, n - 1)
                    sy.dma_start(out=wdi[:, so * NHC * 128:(so + 1) * NHC * 128]
                                 .rearrange("p (c f) -> p c f", f=128),
                                 in_=wdt_d[:, oc * 128:(oc + 1) * 128]
                                 .rearrange("(c p) f -> p c f", p=128)).then_inc(s_wd[so], 16)
                sy.wait_ge(s_x2, 8 * (tt + 1))
                sy.dma_start(out=x2_d.rearrange("(c p) t -> p c t", p=128)
                             [:, :, tt * 512:(tt + 1) * 512],
                             in_=x1t.rearrange("p (c t) -> p c t", t=512)) \
                    .then_inc(s_ost[0], 16)
            sy.wait_ge(s_ost[0], 64)

        @block.tensor
        def _(t):
            t.wait_ge(s_init, 32)
            for k in range(2 * NHC):
                th, hc = k // NHC, k % NHC
                sl = k % 3
                t.wait_ge(s_wdq, 2 * k + 2)
                if k >= 2:
                    t.wait_ge(s_hv, 2 * (k - 1))
                for half, PSL in ((0, GPS), (1, UPS)):
                    wsb = wgb if half == 0 else wub
                    for tt2 in range(2):
                        t0 = th * 1024 + tt2 * 512
                        for ic in range(NCH):
                            mm = t.matmul(PSL(k)[:, tt2 * 512:(tt2 + 1) * 512],
                                          wsb[:, sl * NCH * 128 + ic * 128: sl * NCH * 128 + (ic + 1) * 128],
                                          xnsb[:, ic * TC + t0: ic * TC + t0 + 512],
                                          start=(ic == 0), stop=(ic == NCH - 1))
                mm.then_inc(s_gu, 1)
            for tt in range(4):
                sl = tt % 2
                t.wait_ge(s_hld[sl], 16 * (tt // 2 + 1))
                t.wait_ge(s_sx[sl], 32 * (tt // 2 + 1))
                for oc in range(NCH):
                    n = tt * NCH + oc
                    t.wait_ge(s_wdb, n + 1)
                    if n >= 2:
                        t.wait_ge(s_x2, n - 1)
                    for hcc in range(NHC):
                        mm = t.matmul(DPS(n), wdb[:, (n % 2) * NHC * 128 + hcc * 128:
                                                  (n % 2) * NHC * 128 + (hcc + 1) * 128],
                                      htl[:, sl * NHC * 512 + hcc * 512:
                                          sl * NHC * 512 + (hcc + 1) * 512],
                                      start=(hcc == 0), stop=(hcc == NHC - 1))
                    mm.then_inc(s_dn, 1)
                    for ic in range(NCH):
                        mm = t.matmul(WPS(n), wo_sb[:, ic * D + oc * 128: ic * D + oc * 128 + 128],
                                      s1t[:, sl * NCH * 512 + ic * 512: sl * NCH * 512 + (ic + 1) * 512],
                                      start=(ic == 0), stop=(ic == NCH - 1))
                    mm.then_inc(s_wo, 1)

        @block.scalar
        def _(s):
            for k in range(2 * NHC):
                s.wait_ge(s_gu, k + 1)
                s.activation(gsil[:, (k % 2) * 1024:(k % 2) * 1024 + 1024], GPS(k),
                             AF.Silu).then_inc(s_sil, 1)
            for n in range(32):
                s.wait_ge(s_dn, n + 1)
                if n >= 1:
                    s.wait_ge(s_x2, n)
                s.activation(g1t[:, :], DPS(n), AF.Silu).then_inc(s_g1, 1)

        @block.vector
        def _(v):
            vfl = [0]
            def VF(inst):
                vfl[0] += 1
                inst.then_inc(s_vfl, 1)
                v.wait_ge(s_vfl, vfl[0])
                return inst
            for k in range(2 * NHC):
                sl = k % 3
                v.wait_ge(s_wg[sl], 32 * (k // 3 + 1))
                v.tensor_scalar(wgb[:, sl * 1024:(sl + 1) * 1024],
                                wgi[:, sl * 1024:(sl + 1) * 1024],
                                QS, None, ALU.mult).then_inc(s_wdq, 1)
                v.tensor_scalar(wub[:, sl * 1024:(sl + 1) * 1024],
                                wui[:, sl * 1024:(sl + 1) * 1024],
                                QS, None, ALU.mult).then_inc(s_wdq, 1)
                v.wait_ge(s_sil, k + 1)
                if k >= 1:
                    v.wait_ge(s_hst[0], 16 * k)
                v.tensor_tensor(hstg[:, :], UPS(k),
                                gsil[:, (k % 2) * 1024:(k % 2) * 1024 + 1024],
                                ALU.mult).then_inc(s_hv, 2)
            for tt in range(4):
                sl = tt % 2
                for oc in range(NCH):
                    n = tt * NCH + oc
                    so = n % 2
                    v.wait_ge(s_wd[so], 16 * (n // 2 + 1))
                    v.tensor_scalar(wdb[:, so * NHC * 128:(so + 1) * NHC * 128],
                                    wdi[:, so * NHC * 128:(so + 1) * NHC * 128],
                                    QS, None, ALU.mult).then_inc(s_wdb, 1)
                    v.wait_ge(s_g1, n + 1)
                    v.wait_ge(s_wo, n + 1)
                    VF(v.tensor_tensor(t1s[:, :], WPS(n), g1t[:, :], ALU.mult))
                    v.tensor_tensor(x1t[:, oc * 512:(oc + 1) * 512],
                                    x1t[:, oc * 512:(oc + 1) * 512],
                                    t1s[:, :], ALU.add).then_inc(s_x2, 1)

def build_l1_ffn():
    nc = bass.Bass("TRN2", target_bir_lowering=False, debug=False)
    DD = _decl_l1_ffn(nc)
    emit_l1_ffn(nc, DD)
    return nc




class _NCP:

    def __init__(self, nc, pfx):
        self._nc = nc
        self._p = pfx

    def sbuf_tensor(self, name, shape, dtype, **kw):
        return self._nc.sbuf_tensor(self._p + name, shape, dtype, **kw)

    def psum_tensor(self, name, shape, dtype, **kw):
        return self._nc.psum_tensor(self._p + name, shape, dtype, **kw)

    def semaphore(self, name=None, num=None):
        return self._nc.semaphore(self._p + (name or "s"), num)

    def __getattr__(self, a):
        return getattr(self._nc, a)


def _emit_gather(nc, src_ap, dst_ap, tag):
    RG = [[0, 1, 2, 3], [4, 5, 6, 7]]
    with nc.semaphore(f"s_cc_{tag}") as s_c, nc.Block() as blk:
        @blk.gpsimd
        def _(g_):
            g_.collective_compute("AllGather", ALU.bypass, RG,
                                  [src_ap.tensor.ap().opt()],
                                  [dst_ap.tensor.ap().opt()]).then_inc(s_c, 1)
            g_.wait_ge(s_c, 1)


def build_fused(nstages=10):
    nc = bass.Bass("TRN2", target_bir_lowering=False, debug=False)
    EI, INT = "ExternalInput", "Internal"
    dt = nc.dram_tensor
    x0 = dt("x0", [D, TC], F32, kind=EI).ap()
    w1 = dt("w1", [E, D, FF], F32, kind=EI).ap()
    w2 = dt("w2", [E, FF, D], F32, kind=EI).ap()
    sw1 = dt("sw1", [D, FF], F32, kind=EI).ap()
    sw2 = dt("sw2", [FF, D], F32, kind=EI).ap()
    b1t = dt("b1t", [E, 128, FF // 128], F32, kind=EI).ap()
    sb1t = dt("sb1t", [128, FF // 128], F32, kind=EI).ap()
    b2 = dt("b2", [E, D], F32, kind=EI).ap()
    sb2 = dt("sb2", [1, D], F32, kind=EI).ap()
    gwt = dt("gwt", [D, E], F32, kind=EI).ap()
    gb = dt("gb", [E, 1], F32, kind=EI).ap()
    wqt = dt("wqt", [D, D], F32, kind=EI).ap()
    wkt = dt("wkt", [D, D], F32, kind=EI).ap()
    wvt = dt("wvt", [D, D], F32, kind=EI).ap()
    wot = dt("wot", [D, D], F32, kind=EI).ap()
    pm64 = dt("pm64", [128, 4], F32, kind=EI).ap()
    wk1 = dt("wk", [D, D], F32, kind=EI).ap()
    wv1 = dt("wv", [D, D], F32, kind=EI).ap()
    wdn = dt("wdn", [D, DLAT], F32, kind=EI).ap()
    wq1 = dt("wq", [D, D], F32, kind=EI).ap()
    wupk = dt("wupk", [DLAT, D], F32, kind=EI).ap()
    wupv = dt("wupv", [DLAT, D], F32, kind=EI).ap()
    wo1 = dt("wo", [D, D], F32, kind=EI).ap()
    wgt = dt("wgt", [D, HID], I32, kind=EI).ap()
    wut = dt("wut", [D, HID], I32, kind=EI).ap()
    wdt = dt("wdt", [HID, D], I32, kind=EI).ap()
    posq = dt("posq", [1, TC], F32, kind=EI).ap()
    posk = dt("posk", [1, S], F32, kind=EI).ap()
    amask = dt("amask", [1, S], F32, kind=EI).ap()
    bmask = dt("bmask", [1, S], F32, kind=EI).ap()
    biascol = dt("biascol", [128, S // 128], F32, kind=EI).ap()
    xn = dt("xn", [D, TC], BF16, kind=INT).ap()
    xnf = dt("xnf", [D, TC], F32, kind=INT).ap()
    g0 = dt("g0", [D, TC], BF16, kind=INT).ap()
    qf = dt("qf", [D, TC], BF16, kind=INT).ap()
    kf = dt("kf", [D, TC], BF16, kind=INT).ap()
    ktm = dt("ktm", [TC, D], BF16, kind=INT).ap()
    vtm = dt("vtm", [TC, H * 65], BF16, kind=INT).ap()
    stot = dt("stot", [H * 64, 65], F32, kind=INT).ap()
    stall = dt("stall", [4 * H * 64, 65], F32, kind=INT).ap()
    x1 = dt("x1", [D, TC], F32, kind=INT).ap()
    s_out = dt("s_out", [D, TC], BF16, kind=INT).ap()
    xn1 = dt("xn1", [D, TC], BF16, kind=INT).ap()
    xnf1 = dt("xnf1", [D, TC], F32, kind=INT).ap()
    lat = dt("lat", [DLAT, TC], BF16, kind=INT).ap()
    latg = dt("latg", [4 * DLAT, TC], BF16, kind=INT).ap()
    s1 = dt("s1", [D, TC], BF16, kind=INT).ap()
    h_scr = dt("h_scr", [HID, TC], BF16, kind=INT).ap()
    w1scr = dt("moe_w1scr", [9 * 8, 128, NCH * 512], BF16, kind=INT).ap()
    w2scr = dt("moe_w2scr", [9, 2, 128, 32 * 512], BF16, kind=INT).ap()
    x2 = dt("x2", [D, TC], F32, kind="ExternalOutput").ap()

    _steps = []
    def _st0():
        emit_ln(_NCP(nc, 'g1_'), {"x": x0, "xn": xn, "xnf": xnf})
    _steps.append(_st0)
    def _st1():
        emit_l0_moe(_NCP(nc, 'g2_'), {"xn": xn, "xnf": xnf, "w1": w1, "w2": w2, "sw1": sw1,
                         "sw2": sw2, "b1t": b1t, "sb1t": sb1t, "b2": b2, "sb2": sb2,
                         "gwt": gwt, "gb": gb, "g": g0, "w1scr": w1scr, "w2scr": w2scr})
    _steps.append(_st1)
    def _st2():
        emit_l0_qkv(_NCP(nc, 'g3_'), {"xn": xn, "wqt": wqt, "wkt": wkt, "wvt": wvt, "qf": qf,
                         "kf": kf, "ktm": ktm, "vtm": vtm, "stot": stot})
    _steps.append(_st2)
    def _st3():
        _emit_gather(_NCP(nc, 'g4_'), stot, stall, "st")
    _steps.append(_st3)
    def _st4():
        emit_l0_scan(_NCP(nc, 'g5_'), {"qf": qf, "kf": kf, "ktm": ktm, "vtm": vtm, "stall": stall,
                          "pm64": pm64, "g": g0, "x": x0, "wot": wot, "x1": x1,
                          "s_out": s_out})
    _steps.append(_st4)
    def _st5():
        emit_ln(_NCP(nc, 'g6_'), {"x": x1, "xn": xn1, "xnf": xnf1})
    _steps.append(_st5)
    def _st6():
        emit_l1_lat(_NCP(nc, 'g7_'), {"xn1": xn1, "wk": wk1, "wv": wv1, "wdn": wdn, "lat": lat})
    _steps.append(_st6)
    def _st7():
        _emit_gather(_NCP(nc, 'g8_'), lat, latg, "lat")
    _steps.append(_st7)
    def _st8():
        emit_l1_attn(_NCP(nc, 'g9_'), {"xn1": xn1, "latf": latg, "lato": lat, "wq": wq1,
                          "wupk": wupk, "wupv": wupv, "posq": posq, "posk": posk,
                          "amask": amask, "bmask": bmask, "biascol": biascol,
                          "s1": s1}, gathered=True)
    _steps.append(_st8)
    def _st9():
        emit_l1_ffn(_NCP(nc, 'g10_'), {"xn1": xn1, "wgt": wgt, "wut": wut, "wdt": wdt, "s1": s1,
                         "wo": wo1, "x1": x1, "x2": x2, "h_scr": h_scr})
    _steps.append(_st9)
    for k, st in enumerate(_steps[:nstages]):
        if k:
            nc.reset()
        st()
    return nc


def _cc(a):
    return np.ascontiguousarray(a)


def kernel_multilaunch(**inputs):
    I = {k: np.asarray(v) for k, v in inputs.items()}
    import ml_dtypes
    BF = ml_dtypes.bfloat16
    cores = list(range(8))

    def slc(c):
        return c // 4, (c % 4) * TC

    nc_ln = build_ln()
    x_slices = [_cc(I["L0_attn_x"][c // 4, (c % 4) * TC:(c % 4 + 1) * TC, :].T)
                for c in cores]
    r = run_bass_kernel_spmd(nc_ln, [{"x": x_slices[c]} for c in cores], core_ids=cores)
    xn = [np.asarray(r.results[c]["xn"]) for c in cores]
    xnf = [np.asarray(r.results[c]["xnf"]) for c in cores]

    nc_moe = build_l0_moe()
    W0 = {"w1": I["L0_ffn_w1"], "w2": I["L0_ffn_w2"], "sw1": I["L0_ffn_sw1"],
          "sw2": I["L0_ffn_sw2"], "b1": I["L0_ffn_b1"], "sb1": I["L0_ffn_sb1"],
          "b2": I["L0_ffn_b2"], "sb2": I["L0_ffn_sb2"], "gate_w": I["L0_ffn_gate_w"],
          "gate_b": I["L0_ffn_gate_b"]}
    im = []
    for c in cores:
        m = _host_moe_inputs(xn[c], W0)
        m["xnf"] = xnf[c]
        im.append(m)
    r = run_bass_kernel_spmd(nc_moe, im, core_ids=cores)
    g0 = [np.asarray(r.results[c]["g"]) for c in cores]

    nc_qkv = build_l0_qkv()
    wqt, wkt = _cc(I["L0_attn_Wq"].T), _cc(I["L0_attn_Wk"].T)
    wvt, wot = _cc(I["L0_attn_Wv"].T), _cc(I["L0_attn_Wo"].T)
    r = run_bass_kernel_spmd(nc_qkv, [{"xn": xn[c], "wqt": wqt, "wkt": wkt,
                                       "wvt": wvt} for c in cores], core_ids=cores)
    qkv = [{k: np.asarray(r.results[c][k]) for k in ("qf", "kf", "ktm", "vtm", "stot")}
           for c in cores]

    nc_scan = build_l0_scan()
    im = []
    for c in cores:
        b, qtr = c // 4, c % 4
        stall = _cc(np.concatenate([qkv[4 * b + rr]["stot"] for rr in range(4)], axis=0))
        pm = np.zeros((128, 4), np.float32)
        pm[:, :qtr] = 1.0
        im.append({"qf": qkv[c]["qf"], "kf": qkv[c]["kf"], "ktm": qkv[c]["ktm"],
                   "vtm": qkv[c]["vtm"], "stall": stall, "pm64": pm, "g": g0[c],
                   "x": x_slices[c], "wot": wot})
    r = run_bass_kernel_spmd(nc_scan, im, core_ids=cores)
    x1 = [np.asarray(r.results[c]["x1"]) for c in cores]

    r = run_bass_kernel_spmd(nc_ln, [{"x": x1[c]} for c in cores], core_ids=cores)
    xn1 = [np.asarray(r.results[c]["xn"]) for c in cores]

    nc_lat = build_l1_lat()
    r = run_bass_kernel_spmd(nc_lat, [{"xn1": xn1[c], "wk": I["L1_attn_Wk"],
                                       "wv": I["L1_attn_Wv"],
                                       "wdn": I["L1_attn_W_down_kv"]} for c in cores],
                             core_ids=cores)
    lats = [np.asarray(r.results[c]["lat"]) for c in cores]
    latf = [_cc(np.concatenate([lats[4 * b + rr] for rr in range(4)], axis=1))
            for b in range(2)]

    nc_attn = build_l1_attn()
    im = []
    posk = np.arange(S, dtype=np.float32)[None, :]
    for c in cores:
        b, qtr = c // 4, c % 4
        cs, ce = qtr * TC, (qtr + 1) * TC
        biascol = np.zeros((128, 64), np.float32)
        for J in range(64):
            pp = J * 128 + np.arange(128)
            biascol[(pp >= cs) & (pp < ce), J] = -88.0
        im.append({"xn1": xn1[c], "latf": latf[b], "lato": lats[c],
                   "wq": I["L1_attn_Wq"], "wupk": I["L1_attn_W_up_k"],
                   "wupv": I["L1_attn_W_up_v"],
                   "posq": np.arange(cs, ce, dtype=np.float32)[None, :],
                   "posk": posk, "amask": (posk < cs).astype(np.float32),
                   "bmask": (posk >= ce).astype(np.float32), "biascol": biascol})
    r = run_bass_kernel_spmd(nc_attn, im, core_ids=cores)
    s1 = [np.asarray(r.results[c]["s1"]) for c in cores]

    nc_ffn = build_l1_ffn()
    wgt, wut = _cc(I["L1_ffn_w_gate"].T), _cc(I["L1_ffn_w_up"].T)
    wdt = _cc(I["L1_ffn_w_down"].T)
    r = run_bass_kernel_spmd(nc_ffn, [{"xn1": xn1[c], "wgt": wgt, "wut": wut,
                                       "wdt": wdt, "s1": s1[c], "wo": I["L1_attn_Wo"],
                                       "x1": x1[c]} for c in cores], core_ids=cores)
    x2 = [np.asarray(r.results[c]["x2"]) for c in cores]

    out = np.empty((2, 8192, D), np.float32)
    for c in cores:
        b, qtr = c // 4, c % 4
        out[b, qtr * TC:(qtr + 1) * TC, :] = x2[c].T
    return out


def kernel(**inputs):
    I = {k: np.asarray(v) for k, v in inputs.items()}
    cores = list(range(8))
    nc = build_fused()
    NP = FF // 128
    shared = {
        "w1": I["L0_ffn_w1"], "w2": I["L0_ffn_w2"], "sw1": I["L0_ffn_sw1"],
        "sw2": I["L0_ffn_sw2"],
        "b1t": _cc(I["L0_ffn_b1"].reshape(E, NP, 128).transpose(0, 2, 1)),
        "sb1t": _cc(I["L0_ffn_sb1"].reshape(NP, 128).T),
        "b2": I["L0_ffn_b2"], "sb2": I["L0_ffn_sb2"].reshape(1, D),
        "gwt": _cc(I["L0_ffn_gate_w"].T), "gb": I["L0_ffn_gate_b"].reshape(E, 1),
        "wqt": _cc(I["L0_attn_Wq"].T), "wkt": _cc(I["L0_attn_Wk"].T),
        "wvt": _cc(I["L0_attn_Wv"].T), "wot": _cc(I["L0_attn_Wo"].T),
        "wk": I["L1_attn_Wk"], "wv": I["L1_attn_Wv"], "wdn": I["L1_attn_W_down_kv"],
        "wq": I["L1_attn_Wq"], "wupk": I["L1_attn_W_up_k"], "wupv": I["L1_attn_W_up_v"],
        "wo": I["L1_attn_Wo"],
        "wgt": _cc(I["L1_ffn_w_gate"].T), "wut": _cc(I["L1_ffn_w_up"].T),
        "wdt": _cc(I["L1_ffn_w_down"].T),
        "posk": np.arange(S, dtype=np.float32)[None, :],
    }
    im = []
    for c in cores:
        b, qtr = c // 4, c % 4
        cs, ce = qtr * TC, (qtr + 1) * TC
        pm = np.zeros((128, 4), np.float32)
        pm[:, :qtr] = 1.0
        posk = shared["posk"]
        biascol = np.zeros((128, S // 128), np.float32)
        for J in range(S // 128):
            pp = J * 128 + np.arange(128)
            biascol[(pp >= cs) & (pp < ce), J] = -88.0
        m = dict(shared)
        m["x0"] = _cc(I["L0_attn_x"][b, cs:ce, :].T)
        m["pm64"] = pm
        m["posq"] = np.arange(cs, ce, dtype=np.float32)[None, :]
        m["amask"] = (posk < cs).astype(np.float32)
        m["bmask"] = (posk >= ce).astype(np.float32)
        m["biascol"] = biascol
        im.append(m)
    r = run_bass_kernel_spmd(nc, im, core_ids=cores)
    out = np.empty((2, 8192, D), np.float32)
    for c in cores:
        b, qtr = c // 4, c % 4
        out[b, qtr * TC:(qtr + 1) * TC, :] = np.asarray(r.results[c]["x2"]).T
    return out


# version 10 (final)
# speedup vs baseline: 1.0106x; 1.0106x over previous
import numpy as np
from contextlib import ExitStack
import concourse.bass as bass
import concourse.mybir as mybir
from concourse.bass_utils import run_bass_kernel_spmd

F32 = mybir.dt.float32
BF16 = mybir.dt.bfloat16
I32 = mybir.dt.int32
AF = mybir.ActivationFunctionType
ALU = mybir.AluOpType
AX = mybir.AxisListType.X

D = 1024
TC = 2048
S = 8192
B = 2
H = 16
DH = 64
E = 8
FF = 4096
HID = 3072
DLAT = 256
NCH = D // 128



def _decl_l0_moe(nc):
    DD = {}
    NP = FF // 128
    DD["xn"] = nc.dram_tensor("xn", [D, TC], BF16, kind="ExternalInput").ap()
    DD["xnf"] = nc.dram_tensor("xnf", [D, TC], F32, kind="ExternalInput").ap()
    DD["w1"] = nc.dram_tensor("w1", [E, D, FF], F32, kind="ExternalInput").ap()
    DD["w2"] = nc.dram_tensor("w2", [E, FF, D], F32, kind="ExternalInput").ap()
    DD["sw1"] = nc.dram_tensor("sw1", [D, FF], F32, kind="ExternalInput").ap()
    DD["sw2"] = nc.dram_tensor("sw2", [FF, D], F32, kind="ExternalInput").ap()
    DD["b1t"] = nc.dram_tensor("b1t", [E, 128, NP], F32, kind="ExternalInput").ap()
    DD["sb1t"] = nc.dram_tensor("sb1t", [128, NP], F32, kind="ExternalInput").ap()
    DD["b2"] = nc.dram_tensor("b2", [E, D], F32, kind="ExternalInput").ap()
    DD["sb2"] = nc.dram_tensor("sb2", [1, D], F32, kind="ExternalInput").ap()
    DD["gwt"] = nc.dram_tensor("gwt", [D, E], F32, kind="ExternalInput").ap()
    DD["gb"] = nc.dram_tensor("gb", [E, 1], F32, kind="ExternalInput").ap()
    DD["g"] = nc.dram_tensor("g", [D, TC], BF16, kind="ExternalOutput").ap()
    return DD


def emit_l0_moe(nc, DD, n_tt=4, n_e=9):
    TT = 512
    NP = FF // 128
    NMISC = n_e + NCH
    xn_d = DD["xn"]
    xnf_d = DD["xnf"]
    w1_d = DD["w1"]
    w2_d = DD["w2"]
    sw1_d = DD["sw1"]
    sw2_d = DD["sw2"]
    b1t_d = DD["b1t"]
    sb1t_d = DD["sb1t"]
    b2_d = DD["b2"]
    sb2_d = DD["sb2"]
    gwt_d = DD["gwt"]
    gb_d = DD["gb"]
    g_d = DD["g"]
    with ExitStack() as ctx:
        ent = ctx.enter_context
        xnsb = ent(nc.sbuf_tensor("xnsb", [128, 2 * NCH * 512], BF16))
        w1b = ent(nc.sbuf_tensor("w1b", [128, 2 * NCH * 512], BF16))
        w2b = ent(nc.sbuf_tensor("w2b", [128, 3 * 16384], BF16))
        hdn = ent(nc.sbuf_tensor("hdn", [128, NP * TT], BF16))
        _rt_off = nc.sbuf_base
        routed = ent(nc.sbuf_tensor("routed", [128, NCH * TT], F32))
        tmpy = ent(nc.sbuf_tensor("tmpy", [128, TT], F32))
        combbc = ent(nc.sbuf_tensor("combbc", [128, 8 * TT], BF16))
        gstg = ent(nc.sbuf_tensor("gstg", [128, TT], BF16))
        comb9 = ent(nc.sbuf_tensor("comb9", [9, TT], F32))
        scf = ent(nc.sbuf_tensor("scf", [8, n_tt * TT], F32))
        xnf2 = nc.alloc_sbuf_tensor_at("xnf2", [128, TC], F32, offset=_rt_off)
        gwtf = ent(nc.sbuf_tensor("gwtf", [128, NCH * E], F32))
        sctm = ent(nc.sbuf_tensor("sctm", [128, 8], F32))
        m1 = ent(nc.sbuf_tensor("m1", [128, 1], F32))
        m2 = ent(nc.sbuf_tensor("m2", [128, 1], F32))
        srec = ent(nc.sbuf_tensor("srec", [128, 1], F32))
        ns1 = ent(nc.sbuf_tensor("ns1", [128, 1], F32))
        ns2 = ent(nc.sbuf_tensor("ns2", [128, 1], F32))
        eq1 = ent(nc.sbuf_tensor("eq1", [128, 8], F32))
        eq2 = ent(nc.sbuf_tensor("eq2", [128, 8], F32))
        msk = ent(nc.sbuf_tensor("msk", [128, 8], F32))
        combtm = ent(nc.sbuf_tensor("combtm", [128, 8], F32))
        b29 = ent(nc.sbuf_tensor("b29", [9, D], F32))
        b1sb = ent(nc.sbuf_tensor("b1sb", [128, 9 * NP], F32))
        gb = ent(nc.sbuf_tensor("gbsb", [E, 1], F32))
        sel = ent(nc.sbuf_tensor("sel", [9, 9 * 128], F32))
        idf = ent(nc.sbuf_tensor("idf", [128, 128], F32))
        ps = ent(nc.psum_tensor("ps", [128, 4096], F32))
        if "w1scr" in DD:
            w1scr, w2scr = DD["w1scr"], DD["w2scr"]
        else:
            w1scr = nc.dram_tensor("moe_w1scr", [n_e * 8, 128, NCH * 512], BF16,
                                   kind="Internal").ap()
            w2scr = nc.dram_tensor("moe_w2scr", [n_e, 2, 128, 32 * 512], BF16,
                                   kind="Internal").ap()
        s_init = ent(nc.semaphore("s_init"))
        s_w1wr = ent(nc.semaphore("s_w1wr"))
        s_w2wr = ent(nc.semaphore("s_w2wr"))
        s_w1s = [ent(nc.semaphore(f"s_w1s{i}")) for i in range(2)]
        s_w1f = ent(nc.semaphore("s_w1f"))
        s_w2sl = [ent(nc.semaphore(f"s_w2sl{i}")) for i in range(3)]
        s_w2f = ent(nc.semaphore("s_w2f"))
        s_sig = ent(nc.semaphore("s_sig"))
        s_tr = ent(nc.semaphore("s_tr"))
        s_trf = ent(nc.semaphore("s_trf"))
        s_comb = ent(nc.semaphore("s_comb"))
        s_misc = ent(nc.semaphore("s_misc"))
        s_miscf = ent(nc.semaphore("s_miscf"))
        s_hps = ent(nc.semaphore("s_hps"))
        s_gelu = ent(nc.semaphore("s_gelu"))
        s_y = ent(nc.semaphore("s_y"))
        s_yf_hd = ent(nc.semaphore("s_yf_hd"))
        s_yf = ent(nc.semaphore("s_yf"))
        s_silu = ent(nc.semaphore("s_silu"))
        s_gsts = [ent(nc.semaphore(f"s_gsts{i}")) for i in range(2)]
        s_vfl = ent(nc.semaphore("s_vfl"))
        s_gate = ent(nc.semaphore("s_gate"))
        s_xnfs = [ent(nc.semaphore(f"s_xnfs{i}")) for i in range(2)]
        s_xbs = [ent(nc.semaphore(f"s_xbs{i}")) for i in range(2)]
        s_xnff = ent(nc.semaphore("s_xnff"))
        s_gfl = ent(nc.semaphore("s_gfl"))
        block = ent(nc.Block())

        Y = lambda oc: ps[:, oc * 512:(oc + 1) * 512]
        HSL = lambda sl: ps[:, 3072 + sl * 512: 3072 + (sl + 1) * 512] if sl < 2 \
            else ps[:, (sl - 2) * 512: (sl - 1) * 512]
        GATE = ps[0:8, 0:TT]
        TRA = ps[:, 512:520]
        TRB = ps[0:8, 1024:1152]
        MISC = lambda r: ps[:, 1536 + (r % 3) * 512: 1536 + (r % 3 + 1) * 512]

        w2thr = {}
        _cum = [0, 0, 0]
        for _h in range(2 * n_tt * n_e):
            _cum[_h % 3] += 16 if _h < 2 * n_e else 64
            w2thr[_h] = _cum[_h % 3]

        INIT_TOT = 16 * 6 + 4

        @block.gpsimd
        def _(g_):
            g_.memset(idf[:, :], 1.0).then_inc(s_gfl, 1)
            g_.wait_ge(s_gfl, 1)
            g_.affine_select(idf[:, :], idf[:, :], [[-1, 128]], ALU.is_equal, 0.0,
                             channel_multiplier=1).then_inc(s_init, 1)
            g_.memset(sel[:, :], 1.0).then_inc(s_gfl, 1)
            g_.wait_ge(s_gfl, 2)
            g_.affine_select(sel[:, :], sel[:, :], [[1, 9 * 128]], ALU.is_ge, 0.0,
                             channel_multiplier=-128).then_inc(s_gfl, 1)
            g_.wait_ge(s_gfl, 3)
            g_.affine_select(sel[:, :], sel[:, :], [[-1, 9 * 128]], ALU.is_ge, 0.0,
                             base=127, channel_multiplier=128).then_inc(s_init, 1)
            g_.nop().then_inc(s_init, 1)
            g_.memset(comb9[:, :], 0.1).then_inc(s_init, 1)

            for e in range(n_e):
                eidx = e
                for hc in range(8):
                    idx = eidx * 8 + hc
                    slot = idx % 2
                    if idx >= 2:
                        g_.wait_ge(s_w1f, idx - 1)
                        g_.wait_ge(s_w1wr, 16 * idx)
                    dst = w1b[:, slot * NCH * 512:(slot + 1) * NCH * 512] \
                        .rearrange("p (c f) -> p c f", f=512)
                    dst2 = w1b[:, slot * NCH * 512:(slot + 1) * NCH * 512]
                    src = w1_d[e, :, hc * 512:(hc + 1) * 512] if e < E \
                        else sw1_d[:, hc * 512:(hc + 1) * 512]
                    g_.dma_start(out=dst, in_=src.rearrange("(c p) f -> p c f", p=128)
                                 ).then_inc(s_w1s[slot], 16)
                    g_.wait_ge(s_w1s[slot], 16 * (idx // 2 + 1))
                    g_.dma_start(out=w1scr[idx], in_=dst2).then_inc(s_w1wr, 16)
                if eidx >= 1:
                    g_.wait_ge(s_w2wr, 16 * 2 * eidx)
                w2src = w2_d[e] if e < E else sw2_d
                for oh in range(2):
                    h = 2 * eidx + oh
                    sl = h % 3
                    if h >= 3:
                        g_.wait_ge(s_w2f, eidx - 1 + oh)
                    g_.dma_start(
                        out=w2b[:, sl * 16384:(sl + 1) * 16384]
                        .rearrange("p (c f) -> p c f", f=512),
                        in_=w2src.rearrange("(c p) o -> p c o", p=128)
                        [:, :, oh * 512:(oh + 1) * 512]).then_inc(s_w2sl[sl], 16)
                for oh in range(2):
                    h = 2 * eidx + oh
                    sl = h % 3
                    g_.wait_ge(s_w2sl[sl], w2thr[h])
                    g_.dma_start(out=w2scr[e, oh],
                                 in_=w2b[:, sl * 16384:(sl + 1) * 16384]
                                 ).then_inc(s_w2wr, 16)
            g_.wait_ge(s_w1wr, 16 * n_e * 8)
            g_.wait_ge(s_w2wr, 16 * 2 * n_e)
            for tt in range(1, n_tt):
                if 2 <= tt + 1 < n_tt:
                    nb = tt + 1
                    g_.wait_ge(s_w2f, (nb - 1) * n_e)
                    g_.dma_start(out=xnsb[:, (nb % 2) * 4096:(nb % 2 + 1) * 4096]
                                 .rearrange("p (c t) -> p c t", t=512),
                                 in_=xn_d.rearrange("(c p) t -> p c t", p=128)
                                 [:, :, nb * TT:(nb + 1) * TT]).then_inc(s_xbs[nb % 2], 16)
                for e in range(n_e):
                    eidx = tt * n_e + e
                    for hc in range(8):
                        idx = eidx * 8 + hc
                        slot = idx % 2
                        g_.wait_ge(s_w1f, idx - 1)
                        g_.dma_start(out=w1b[:, slot * NCH * 512:(slot + 1) * NCH * 512],
                                     in_=w1scr[e * 8 + hc]).then_inc(s_w1s[slot], 16)
                        oh, q = hc // 4, hc % 4
                        h = 2 * eidx + oh
                        sl = h % 3
                        if q == 0:
                            g_.wait_ge(s_w2f, eidx - 1 + oh)
                        g_.dma_start(
                            out=w2b[:, sl * 16384 + q * 4096: sl * 16384 + (q + 1) * 4096],
                            in_=w2scr[e, oh][:, q * 4096:(q + 1) * 4096]
                            ).then_inc(s_w2sl[sl], 16)

        @block.tensor
        def _(t):
            t.wait_ge(s_init, INIT_TOT)
            for c in range(NCH):
                t.wait_ge(s_xnfs[0], 16 * (c + 1))
                for tt in range(n_tt):
                    mm = t.matmul(ps[0:8, tt * TT:(tt + 1) * TT],
                                  gwtf[:, c * E:(c + 1) * E],
                                  xnf2[:, tt * TT: tt * TT + TT],
                                  start=(c == 0), stop=(c == NCH - 1))
                    if c == NCH - 1:
                        mm.then_inc(s_gate, 1)
                t.drain().then_inc(s_xnff, 1)
            nmisc = 0
            for tt in range(n_tt):
                t0 = tt * TT
                t.wait_ge(s_xbs[tt % 2], 16 * (tt // 2 + 1))
                if tt > 0:
                    t.wait_ge(s_yf, tt * n_e)
                for st in range(4):
                    t.wait_ge(s_sig, n_tt)
                    if tt > 0 or st > 0:
                        t.wait_ge(s_trf, tt * 4 + st)
                    t.transpose(TRA, scf[:, tt * TT + st * 128: tt * TT + (st + 1) * 128], idf[0:8, 0:8]) \
                        .then_inc(s_tr, 1)
                    t.wait_ge(s_comb, tt * 4 + st + 1)
                    t.transpose(TRB, combtm[:, :], idf[:, :]).then_inc(s_tr, 1)
                t.wait_ge(s_trf, (tt + 1) * 4)
                for e in range(n_e):
                    if nmisc >= 3:
                        t.wait_ge(s_miscf, nmisc - 2)
                    t.matmul(MISC(nmisc), sel[:, e * 128:(e + 1) * 128],
                             comb9[:, :], start=True, stop=True).then_inc(s_misc, 1)
                    nmisc += 1
                for oc in range(NCH):
                    if nmisc >= 3:
                        t.wait_ge(s_miscf, nmisc - 2)
                    t.matmul(MISC(nmisc), b29[:, oc * 128:(oc + 1) * 128],
                             comb9[:, :], start=True, stop=True).then_inc(s_misc, 1)
                    nmisc += 1
                for e in range(n_e):
                    eidx = tt * n_e + e
                    for p in range(NP):
                        hc, pl = p // 4, p % 4
                        w1idx = eidx * 8 + hc
                        gidx = eidx * NP + p
                        if pl == 0:
                            t.wait_ge(s_w1s[w1idx % 2], 16 * (w1idx // 2 + 1))
                        if gidx >= 4:
                            t.wait_ge(s_gelu, gidx - 3)
                        if p == 0 and eidx >= 1:
                            t.wait_ge(s_yf_hd, 2 * eidx - 1)
                        if p == 2 and eidx >= 1:
                            t.wait_ge(s_yf_hd, 2 * eidx)
                        if p == 2 and e == 0:
                            t.wait_ge(s_trf, (tt + 1) * 4)
                        base = (w1idx % 2) * NCH * 512 + pl * 128
                        for c in range(NCH):
                            mm = t.matmul(HSL(gidx % 4), w1b[:, base + c * 512: base + c * 512 + 128],
                                          xnsb[:, (tt % 2) * 4096 + c * TT:
                                               (tt % 2) * 4096 + c * TT + TT],
                                          start=(c == 0), stop=(c == NCH - 1))
                        mm.then_inc(s_hps, 1)
                        if pl == 3:
                            t.drain().then_inc(s_w1f, 1)
                    t.wait_ge(s_gelu, eidx * NP + NP)
                    t.wait_ge(s_w2sl[(2 * eidx) % 3], w2thr[2 * eidx])
                    if e == 0:
                        t.wait_ge(s_miscf, (tt + 1) * (n_e + NCH))
                    if eidx >= 1:
                        t.wait_ge(s_yf, eidx)
                    for oc in range(NCH):
                        if oc == 4:
                            hB = 2 * eidx + 1
                            t.wait_ge(s_w2sl[hB % 3], w2thr[hB])
                        sb = ((2 * eidx + oc // 4) % 3) * 16384 + (oc % 4) * 128
                        for p in range(NP):
                            mm = t.matmul(Y(oc), w2b[:, sb + p * 512: sb + p * 512 + 128],
                                          hdn[:, p * TT:(p + 1) * TT],
                                          start=(p == 0), stop=(p == NP - 1))
                        if oc == NCH - 1:
                            mm.then_inc(s_y, 1)
                            t.drain().then_inc(s_w2f, 1)

        @block.scalar
        def _(s):
            s.wait_ge(s_init, INIT_TOT)
            for tt in range(n_tt):
                s.wait_ge(s_gate, tt + 1)
                s.activation(scf[:, tt * TT:(tt + 1) * TT], ps[0:8, tt * TT:(tt + 1) * TT],
                             AF.Sigmoid, bias=gb[:, :]).then_inc(s_sig, 1)
            for tt in range(n_tt):
                for e in range(n_e):
                    eidx = tt * n_e + e
                    for p in range(NP):
                        gidx = eidx * NP + p
                        s.wait_ge(s_hps, gidx + 1)
                        s.activation(hdn[:, p * TT:(p + 1) * TT], HSL(gidx % 4),
                                     AF.Gelu, bias=b1sb[:, e * NP + p: e * NP + p + 1]) \
                            .then_inc(s_gelu, 1)
                s.wait_ge(s_yf, (tt + 1) * n_e)
                for oc in range(NCH):
                    gi = tt * NCH + oc
                    if gi >= 1:
                        s.wait_ge(s_gsts[0], 16 * gi)
                    s.activation(gstg[:, 0:TT],
                                 routed[:, oc * TT:(oc + 1) * TT], AF.Silu) \
                        .then_inc(s_silu, 1)

        @block.vector
        def _(v):
            v.wait_ge(s_init, INIT_TOT)
            nmisc_v = 0
            vfl = [0]
            def VF(inst):
                vfl[0] += 1
                inst.then_inc(s_vfl, 1)
                v.wait_ge(s_vfl, vfl[0])
                return inst
            for tt in range(n_tt):
                for st in range(4):
                    v.wait_ge(s_tr, tt * 8 + 2 * st + 1)
                    VF(v.tensor_copy(sctm[:, :], TRA))
                    VF(v.reduce_max(m1[:, :], sctm[:, :], axis=AX))
                    VF(v.tensor_scalar(eq1[:, :], sctm[:, :], m1[:, :], None, ALU.is_equal))
                    VF(v.scalar_tensor_tensor(msk[:, :], eq1[:, :], -2.0, sctm[:, :],
                                              ALU.mult, ALU.add))
                    VF(v.reduce_max(m2[:, :], msk[:, :], axis=AX))
                    VF(v.tensor_scalar(eq2[:, :], msk[:, :], m2[:, :], None, ALU.is_equal))
                    VF(v.tensor_tensor(srec[:, :], m1[:, :], m2[:, :], ALU.add))
                    VF(v.tensor_scalar(srec[:, :], srec[:, :], 1e-6, None, ALU.add))
                    VF(v.reciprocal(srec[:, :], srec[:, :]))
                    VF(v.tensor_tensor(ns1[:, :], m1[:, :], srec[:, :], ALU.mult))
                    VF(v.tensor_tensor(ns2[:, :], m2[:, :], srec[:, :], ALU.mult))
                    VF(v.tensor_scalar(eq1[:, :], eq1[:, :], ns1[:, :], None, ALU.mult))
                    VF(v.tensor_scalar(eq2[:, :], eq2[:, :], ns2[:, :], None, ALU.mult))
                    v.tensor_tensor(combtm[:, :], eq1[:, :], eq2[:, :], ALU.add) \
                        .then_inc(s_comb, 1)
                    v.wait_ge(s_tr, tt * 8 + 2 * st + 2)
                    v.tensor_copy(comb9[0:8, st * 128:(st + 1) * 128], TRB) \
                        .then_inc(s_trf, 1)
                v.wait_ge(s_silu, NCH * tt)
                for e in range(n_e):
                    v.wait_ge(s_misc, tt * NMISC + e + 1)
                    if e < E:
                        v.tensor_copy(combbc[:, e * TT:(e + 1) * TT], MISC(nmisc_v)) \
                            .then_inc(s_miscf, 1)
                    else:
                        v.nop().then_inc(s_miscf, 1)
                    nmisc_v += 1
                for oc in range(NCH):
                    v.wait_ge(s_misc, tt * NMISC + n_e + oc + 1)
                    v.tensor_copy(routed[:, oc * TT:(oc + 1) * TT], MISC(nmisc_v)) \
                        .then_inc(s_miscf, 1)
                    nmisc_v += 1
                for e in range(n_e):
                    eidx = tt * n_e + e
                    v.wait_ge(s_y, eidx + 1)
                    for k, oc in enumerate([6, 7, 0, 1, 2, 3, 4, 5]):
                        if e < E:
                            VF(v.tensor_tensor(tmpy[:, :], Y(oc),
                                               combbc[:, e * TT:(e + 1) * TT], ALU.mult))
                        else:
                            VF(v.tensor_scalar(tmpy[:, :], Y(oc), 0.1, None, ALU.mult))
                        inst = v.tensor_tensor(routed[:, oc * TT:(oc + 1) * TT],
                                               routed[:, oc * TT:(oc + 1) * TT],
                                               tmpy[:, :], ALU.add)
                        if k in (1, 3):
                            inst.then_inc(s_yf_hd, 1)
                    inst.then_inc(s_yf, 1)

        @block.sync
        def _(sy):
            sy.dma_start(out=b29[0:8, :], in_=b2_d[:, :]).then_inc(s_init, 16)
            sy.dma_start(out=b29[8:9, :], in_=sb2_d[:, :]).then_inc(s_init, 16)
            sy.dma_start(out=b1sb[:, 0:8 * NP],
                         in_=b1t_d.rearrange("e p f -> p e f")).then_inc(s_init, 16)
            sy.dma_start(out=b1sb[:, 8 * NP:9 * NP], in_=sb1t_d[:, :]).then_inc(s_init, 16)
            sy.dma_start(out=gwtf[:, :],
                         in_=gwt_d.rearrange("(c p) e -> p c e", p=128)).then_inc(s_init, 16)
            sy.dma_start(out=gb[:, :], in_=gb_d[:, :]).then_inc(s_init, 16)
            for tt in range(min(2, n_tt)):
                sy.dma_start(out=xnsb[:, tt * 4096:(tt + 1) * 4096]
                             .rearrange("p (c t) -> p c t", t=512),
                             in_=xn_d.rearrange("(c p) t -> p c t", p=128)
                             [:, :, tt * TT:(tt + 1) * TT]).then_inc(s_xbs[tt % 2], 16)
            for c in range(NCH):
                if c >= 1:
                    sy.wait_ge(s_xnff, c)
                sy.dma_start(out=xnf2[:, :],
                             in_=xnf_d[c * 128:(c + 1) * 128, :]).then_inc(s_xnfs[0], 16)
            for tt in range(n_tt):
                for oc in range(NCH):
                    gi = tt * NCH + oc
                    sy.wait_ge(s_silu, gi + 1)
                    sy.dma_start(out=g_d.rearrange("(c p) t -> p c t", p=128)
                                 [:, oc, tt * TT:(tt + 1) * TT],
                                 in_=gstg[:, 0:TT]).then_inc(s_gsts[0], 16)
            sy.wait_ge(s_gsts[0], 16 * n_tt * NCH)


def build_l0_moe(n_tt=4, n_e=9):
    nc = bass.Bass("TRN2", target_bir_lowering=False, debug=False)
    DD = _decl_l0_moe(nc)
    emit_l0_moe(nc, DD, n_tt, n_e)
    return nc


def _host_moe_inputs(xn_fm_bf16, W):
    NP = FF // 128
    return {
        "xn": xn_fm_bf16,
        "w1": W["w1"], "w2": W["w2"], "sw1": W["sw1"], "sw2": W["sw2"],
        "b1t": np.ascontiguousarray(W["b1"].reshape(E, NP, 128).transpose(0, 2, 1)),
        "sb1t": np.ascontiguousarray(W["sb1"].reshape(NP, 128).T),
        "b2": W["b2"], "sb2": W["sb2"].reshape(1, D),
        "gwt": np.ascontiguousarray(W["gate_w"].T), "gb": W["gate_b"].reshape(E, 1),
    }





def _decl_ln(nc):
    DD = {}
    DD["x"] = nc.dram_tensor("x", [D, TC], F32, kind="ExternalInput").ap()
    DD["xn"] = nc.dram_tensor("xn", [D, TC], BF16, kind="ExternalOutput").ap()
    DD["xnf"] = nc.dram_tensor("xnf", [D, TC], F32, kind="ExternalOutput").ap()
    return DD

def emit_ln(nc, DD):
    TT = 512
    NT = TC // TT
    x_d = DD["x"]
    xn_d = DD["xn"]
    xnf_d = DD["xnf"]
    with ExitStack() as ctx:
        ent = ctx.enter_context
        xt = ent(nc.sbuf_tensor("xt", [128, 2 * NCH * TT], F32))
        xsq = ent(nc.sbuf_tensor("xsq", [128, NCH * TT], F32))
        ones = ent(nc.sbuf_tensor("ones", [128, 128], F32))
        mrow = ent(nc.sbuf_tensor("mrow", [1, TT], F32))
        e2row = ent(nc.sbuf_tensor("e2row", [1, TT], F32))
        vrow = ent(nc.sbuf_tensor("vrow", [1, TT], F32))
        srow = ent(nc.sbuf_tensor("srow", [1, TT], F32))
        rrow = ent(nc.sbuf_tensor("rrow", [1, TT], F32))
        mrrow = ent(nc.sbuf_tensor("mrrow", [1, TT], F32))
        xnf_s = ent(nc.sbuf_tensor("xnf_s", [128, 2 * NCH * TT], F32))
        xn_s = ent(nc.sbuf_tensor("xn_s", [128, 2 * NCH * TT], BF16))
        ps = ent(nc.psum_tensor("lnps", [128, 4096], F32))
        s_xs = [ent(nc.semaphore(f"s_xs{i}")) for i in range(2)]
        s_xf = ent(nc.semaphore("s_xf"))
        s_init = ent(nc.semaphore("s_init"))
        s_sq = ent(nc.semaphore("s_sq"))
        s_sum = ent(nc.semaphore("s_sum"))
        s_st = ent(nc.semaphore("s_st"))
        s_sqr = ent(nc.semaphore("s_sqr"))
        s_rr = ent(nc.semaphore("s_rr"))
        s_bc = ent(nc.semaphore("s_bc"))
        s_xn = ent(nc.semaphore("s_xn"))
        s_os = [ent(nc.semaphore(f"s_os{i}")) for i in range(2)]
        s_vfl = ent(nc.semaphore("s_vfl"))
        block = ent(nc.Block())

        def S1(p): return ps[0:1, p * 2048: p * 2048 + TT]
        def S2(p): return ps[0:1, p * 2048 + TT: p * 2048 + 2 * TT]
        def RB(p): return ps[:, p * 2048 + 1024: p * 2048 + 1024 + TT]
        def MRB(p): return ps[:, p * 2048 + 1536: p * 2048 + 1536 + TT]

        @block.gpsimd
        def _(g_):
            g_.memset(ones[:, :], 1.0).then_inc(s_init, 1)

        @block.sync
        def _(sy):
            for i in range(NT):
                if i >= 2:
                    sy.wait_ge(s_xf, i - 1)
                sy.dma_start(out=xt.rearrange("p (s c t) -> p s c t", s=2, t=TT)[:, i % 2],
                             in_=x_d.rearrange("(c p) t -> p c t", p=128)[:, :, i * TT:(i + 1) * TT])                     .then_inc(s_xs[i % 2], 16)
            for i in range(NT):
                for half, sem in ((0, s_os[0]), (1, s_os[1])):
                    sy.wait_ge(s_xn, 2 * i + half + 1)
                    sl = slice((i % 2) * NCH * TT + half * 4 * TT,
                               (i % 2) * NCH * TT + (half + 1) * 4 * TT)
                    off = half * 4
                    sy.dma_start(out=xnf_d.rearrange("(c p) t -> p c t", p=128)
                                 [:, off:off + 4, i * TT:(i + 1) * TT],
                                 in_=xnf_s.rearrange("p (s c t) -> p s c t", s=2, t=TT)
                                 [:, i % 2, off:off + 4]).then_inc(sem, 16)
                    sy.dma_start(out=xn_d.rearrange("(c p) t -> p c t", p=128)
                                 [:, off:off + 4, i * TT:(i + 1) * TT],
                                 in_=xn_s.rearrange("p (s c t) -> p s c t", s=2, t=TT)
                                 [:, i % 2, off:off + 4]).then_inc(sem, 16)

        @block.tensor
        def _(t):
            t.wait_ge(s_init, 1)
            for i in range(NT):
                p = i % 2
                t.wait_ge(s_xs[p], 16 * (i // 2 + 1))
                if i >= 2:
                    t.wait_ge(s_xn, 2 * (i - 1))
                for c in range(NCH):
                    t.matmul(S1(p), ones[:, 0:1],
                             xt[:, p * NCH * TT + c * TT:(p * NCH * TT) + (c + 1) * TT],
                             start=(c == 0), stop=(c == NCH - 1))
                t.wait_ge(s_sq, i * NCH + NCH)
                for c in range(NCH):
                    mm = t.matmul(S2(p), ones[:, 0:1], xsq[:, c * TT:(c + 1) * TT],
                                  start=(c == 0), stop=(c == NCH - 1))
                mm.then_inc(s_sum, 1)
                t.wait_ge(s_rr, i + 1)
                t.matmul(RB(p), ones[0:1, :], rrow[:, :], start=True, stop=True)
                t.matmul(MRB(p), ones[0:1, :], mrrow[:, :], start=True, stop=True)                     .then_inc(s_bc, 1)

        @block.scalar
        def _(s):
            for i in range(NT):
                s.wait_ge(s_st, i + 1)
                s.activation(srow[:, :], vrow[:, :], AF.Sqrt).then_inc(s_sqr, 1)

        @block.vector
        def _(v):
            vfl = [0]
            def VF(inst):
                vfl[0] += 1
                inst.then_inc(s_vfl, 1)
                v.wait_ge(s_vfl, vfl[0])
                return inst
            for i in range(NT):
                p = i % 2
                v.wait_ge(s_xs[p], 16 * (i // 2 + 1))
                for c in range(NCH):
                    src = xt[:, p * NCH * TT + c * TT: p * NCH * TT + (c + 1) * TT]
                    v.tensor_tensor(xsq[:, c * TT:(c + 1) * TT], src, src, ALU.mult)                         .then_inc(s_sq, 1)
                v.wait_ge(s_sum, i + 1)
                VF(v.tensor_scalar(mrow[:, :], S1(p), 1.0 / D, None, ALU.mult))
                VF(v.tensor_scalar(e2row[:, :], S2(p), 1.0 / D, None, ALU.mult))
                VF(v.tensor_tensor(vrow[:, :], mrow[:, :], mrow[:, :], ALU.mult))
                VF(v.tensor_tensor(vrow[:, :], e2row[:, :], vrow[:, :], ALU.subtract))
                v.tensor_scalar(vrow[:, :], vrow[:, :], 1e-5, None, ALU.add) \
                    .then_inc(s_st, 1)
                v.wait_ge(s_sqr, i + 1)
                VF(v.reciprocal(rrow[:, :], srow[:, :]))
                v.tensor_tensor(mrrow[:, :], mrow[:, :], rrow[:, :], ALU.mult) \
                    .then_inc(s_rr, 1)
                v.wait_ge(s_bc, i + 1)
                v.wait_ge(s_os[0], 32 * (i // 2))
                v.wait_ge(s_os[1], 32 * (i // 2))
                for c in range(NCH):
                    src = xt[:, p * NCH * TT + c * TT: p * NCH * TT + (c + 1) * TT]
                    dstf = xnf_s[:, p * NCH * TT + c * TT: p * NCH * TT + (c + 1) * TT]
                    dstb = xn_s[:, p * NCH * TT + c * TT: p * NCH * TT + (c + 1) * TT]
                    VF(v.tensor_tensor(dstf, src, RB(p), ALU.mult))
                    VF(v.tensor_tensor(dstf, dstf, MRB(p), ALU.subtract))
                    inst = v.tensor_copy(dstb, dstf)
                    if c == 3 or c == NCH - 1:
                        inst.then_inc(s_xn, 1)
                v.drain().then_inc(s_xf, 1)

def build_ln():
    nc = bass.Bass("TRN2", target_bir_lowering=False, debug=False)
    DD = _decl_ln(nc)
    emit_ln(nc, DD)
    return nc




def _decl_l0_qkv(nc):
    DD = {}
    DD["xn"] = nc.dram_tensor("xn", [D, TC], BF16, kind="ExternalInput").ap()
    DD["wqt"] = nc.dram_tensor("wqt", [D, D], F32, kind="ExternalInput").ap()
    DD["wkt"] = nc.dram_tensor("wkt", [D, D], F32, kind="ExternalInput").ap()
    DD["wvt"] = nc.dram_tensor("wvt", [D, D], F32, kind="ExternalInput").ap()
    DD["qf"] = nc.dram_tensor("qf", [D, TC], BF16, kind="ExternalOutput").ap()
    DD["kf"] = nc.dram_tensor("kf", [D, TC], BF16, kind="ExternalOutput").ap()
    DD["ktm"] = nc.dram_tensor("ktm", [TC, D], BF16, kind="ExternalOutput").ap()
    DD["vtm"] = nc.dram_tensor("vtm", [TC, H * 65], BF16, kind="ExternalOutput").ap()
    DD["stot"] = nc.dram_tensor("stot", [H * 64, 65], F32, kind="ExternalOutput").ap()

    return DD

def emit_l0_qkv(nc, DD):
    NCT = TC // 128
    xn_d = DD["xn"]
    wqt_d = DD["wqt"]
    wkt_d = DD["wkt"]
    wvt_d = DD["wvt"]
    qf_d = DD["qf"]
    kf_d = DD["kf"]
    ktm_d = DD["ktm"]
    vtm_d = DD["vtm"]
    stot_d = DD["stot"]
    with ExitStack() as ctx:
        ent = ctx.enter_context
        xnsb = ent(nc.sbuf_tensor("xnsb", [128, NCH * TC], BF16))
        wq = ent(nc.sbuf_tensor("wq", [128, NCH * D], BF16))
        wk = ent(nc.sbuf_tensor("wk", [128, NCH * D], BF16))
        wv = ent(nc.sbuf_tensor("wv", [128, NCH * D], BF16))
        kf = ent(nc.sbuf_tensor("kfsb", [128, NCH * TC], BF16))
        el1 = ent(nc.sbuf_tensor("el1", [128, 2 * 512], F32))
        el3 = ent(nc.sbuf_tensor("el3", [128, 2 * 512], F32))
        qstg = ent(nc.sbuf_tensor("qstg", [128, 2 * 512], BF16))
        vstg = ent(nc.sbuf_tensor("vstg", [128, 2 * 1040], BF16))
        kts = ent(nc.sbuf_tensor("kts", [128, 2 * 1024], BF16))
        sts = ent(nc.sbuf_tensor("sts", [64, 16 * 65], F32))
        identb = ent(nc.sbuf_tensor("identb", [128, 128], BF16))
        ps = ent(nc.psum_tensor("ps", [128, 3584], F32))
        psb = ent(nc.psum_tensor("psb", [128, 256], BF16))
        s_init = ent(nc.semaphore("s_init"))
        s_proj = ent(nc.semaphore("s_proj"))
        s_emn = ent(nc.semaphore("s_emn"))
        s_eex = ent(nc.semaphore("s_eex"))
        s_eev = ent(nc.semaphore("s_eev"))
        s_ktr = ent(nc.semaphore("s_ktr"))
        s_ktv = ent(nc.semaphore("s_ktv"))
        s_stm = ent(nc.semaphore("s_stm"))
        s_stv = ent(nc.semaphore("s_stv"))
        s_vfl = ent(nc.semaphore("s_vfl"))
        s_gfl = ent(nc.semaphore("s_gfl"))
        s_qst = [ent(nc.semaphore(f"s_qst{i}")) for i in range(2)]
        s_kst = ent(nc.semaphore("s_kst"))
        s_tst = [ent(nc.semaphore(f"s_tst{i}")) for i in range(2)]
        s_vst = [ent(nc.semaphore(f"s_vst{i}")) for i in range(2)]
        s_sst = ent(nc.semaphore("s_sst"))
        block = ent(nc.Block())

        PRJ = lambda r: ps[:, (r % 3) * 512:(r % 3) * 512 + 512]
        VPS = lambda r: ps[:, (r % 2) * 1024:(r % 2) * 1024 + 1024]
        KTR = psb[:, 0:128]
        SKV = lambda h: ps[0:64, (5 + h // 8) * 512 + (h % 8) * 64:
                           (5 + h // 8) * 512 + (h % 8) * 64 + 64]
        SKS = lambda h: ps[0:64, 4 * 512 + h: 4 * 512 + h + 1]

        @block.gpsimd
        def _(g_):
            for w_sb, w_dd in ((wq, wqt_d), (wk, wkt_d), (wv, wvt_d)):
                g_.dma_start(out=w_sb.rearrange("p (c f) -> p c f", f=D),
                             in_=w_dd.rearrange("(c p) f -> p c f", p=128)).then_inc(s_init, 16)
            g_.memset(identb[:, :], 1.0).then_inc(s_gfl, 1)
            g_.wait_ge(s_gfl, 1)
            g_.affine_select(identb[:, :], identb[:, :], [[-1, 128]], ALU.is_equal, 0.0,
                             channel_multiplier=1).then_inc(s_init, 1)

        @block.sync
        def _(sy):
            sy.dma_start(out=xnsb.rearrange("p (c t) -> p c t", t=TC),
                         in_=xn_d.rearrange("(c p) t -> p c t", p=128)).then_inc(s_init, 16)
            for i in range(32):
                oc, ti = i // 4, i % 4
                sy.wait_ge(s_eev, i + 1)
                sy.dma_start(out=qf_d[oc * 128:(oc + 1) * 128, ti * 512:(ti + 1) * 512],
                             in_=qstg[:, (i % 2) * 512:(i % 2) * 512 + 512]) \
                    .then_inc(s_qst[i % 2], 16)
            for i in range(32):
                oc, ti = i // 4, i % 4
                sy.wait_ge(s_eev, 33 + i)
                sy.dma_start(out=kf_d[oc * 128:(oc + 1) * 128, ti * 512:(ti + 1) * 512],
                             in_=kf[:, oc * TC + ti * 512: oc * TC + ti * 512 + 512]) \
                    .then_inc(s_kst, 16)
            for ct in range(NCT):
                sy.wait_ge(s_ktv, ct * 8 + 8)
                sy.dma_start(out=ktm_d[ct * 128:(ct + 1) * 128, :],
                             in_=kts[:, (ct % 2) * 1024:(ct % 2) * 1024 + 1024]) \
                    .then_inc(s_tst[ct % 2], 16)
                sy.wait_ge(s_eev, 64 + ct + 1)
                sy.dma_start(out=vtm_d[ct * 128:(ct + 1) * 128, :],
                             in_=vstg[:, (ct % 2) * 1040:(ct % 2) * 1040 + 1040]) \
                    .then_inc(s_vst[ct % 2], 16)
            sy.wait_ge(s_stv, 1)
            sy.dma_start(out=stot_d.rearrange("(h p) e -> p h e", p=64),
                         in_=sts.rearrange("p (h e) -> p h e", e=65)).then_inc(s_sst, 16)
            sy.wait_ge(s_qst[0], 16 * 16)
            sy.wait_ge(s_qst[1], 16 * 16)
            sy.wait_ge(s_kst, 16 * 32)
            sy.wait_ge(s_tst[0], 16 * 8)
            sy.wait_ge(s_tst[1], 16 * 8)
            sy.wait_ge(s_vst[0], 16 * 8)
            sy.wait_ge(s_vst[1], 16 * 8)
            sy.wait_ge(s_sst, 16)

        @block.tensor
        def _(t):
            t.wait_ge(s_init, 65)
            nprj = 0
            for w_sb in (wq, wk):
                for oc in range(NCH):
                    for ti in range(4):
                        if nprj >= 3:
                            t.wait_ge(s_eev, nprj - 2)
                        for ic in range(NCH):
                            mm = t.matmul(PRJ(nprj),
                                          w_sb[:, ic * D + oc * 128: ic * D + oc * 128 + 128],
                                          xnsb[:, ic * TC + ti * 512: ic * TC + ti * 512 + 512],
                                          start=(ic == 0), stop=(ic == NCH - 1))
                        mm.then_inc(s_proj, 1)
                        nprj += 1
            t.wait_ge(s_eev, 64)
            nktr = 0
            for ct in range(NCT):
                if ct >= 2:
                    t.wait_ge(s_eev, 64 + ct - 1)
                for ic in range(NCH):
                    t.matmul(VPS(ct)[:, 0:512],
                             xnsb[:, ic * TC + ct * 128: ic * TC + ct * 128 + 128],
                             wv[:, ic * D: ic * D + 512],
                             start=(ic == 0), stop=(ic == NCH - 1))
                    mm = t.matmul(VPS(ct)[:, 512:1024],
                                  xnsb[:, ic * TC + ct * 128: ic * TC + ct * 128 + 128],
                                  wv[:, ic * D + 512: ic * D + 1024],
                                  start=(ic == 0), stop=(ic == NCH - 1))
                mm.then_inc(s_proj, 1)
                for fc in range(NCH):
                    if nktr >= 1:
                        t.wait_ge(s_ktv, nktr)
                    t.transpose(KTR, kf[:, fc * TC + ct * 128: fc * TC + ct * 128 + 128],
                                identb[:, :]).then_inc(s_ktr, 1)
                    nktr += 1
                t.wait_ge(s_ktv, ct * 8 + 8)
                t.wait_ge(s_eev, 64 + ct + 1)
                for h in range(H):
                    off = (h // 2) * 128 + (h % 2) * 64
                    kslice = kts[:, (ct % 2) * 1024 + off: (ct % 2) * 1024 + off + 64]
                    t.matmul(SKV(h), kslice,
                             vstg[:, (ct % 2) * 1040 + h * 65: (ct % 2) * 1040 + h * 65 + 64],
                             start=(ct == 0 and h % 8 == 0), stop=(ct == NCT - 1))
                    mm = t.matmul(SKS(h), kslice,
                                  vstg[:, (ct % 2) * 1040 + h * 65 + 64:
                                       (ct % 2) * 1040 + h * 65 + 65],
                                  start=(ct == 0 and h == 0), stop=(ct == NCT - 1))
                if ct == NCT - 1:
                    mm.then_inc(s_stm, 1)

        @block.scalar
        def _(s):
            for i in range(64):
                s.wait_ge(s_emn, i + 1)
                s.activation(el1[:, (i % 2) * 512:(i % 2) * 512 + 512],
                             el1[:, (i % 2) * 512:(i % 2) * 512 + 512], AF.Exp) \
                    .then_inc(s_eex, 1)

        @block.vector
        def _(v):
            vfl = [0]
            def VF(inst):
                vfl[0] += 1
                inst.then_inc(s_vfl, 1)
                v.wait_ge(s_vfl, vfl[0])
                return inst
            v.memset(vstg[:, :], 1.0)
            for phase in range(2):
                for i in range(32):
                    sl = i % 2
                    gidx = phase * 32 + i
                    v.wait_ge(s_proj, gidx + 1)
                    v.tensor_scalar_min(el1[:, sl * 512:(sl + 1) * 512], PRJ(gidx), 0.0) \
                        .then_inc(s_emn, 1)
                    VF(v.tensor_scalar_max(el3[:, sl * 512:(sl + 1) * 512], PRJ(gidx), 0.0))
                    v.wait_ge(s_eex, gidx + 1)
                    oc, ti = i // 4, i % 4
                    if phase == 0:
                        if i >= 2:
                            v.wait_ge(s_qst[sl], 16 * (i // 2))
                        dst = qstg[:, sl * 512:(sl + 1) * 512]
                    else:
                        dst = kf[:, oc * TC + ti * 512: oc * TC + ti * 512 + 512]
                    v.tensor_tensor(dst, el1[:, sl * 512:(sl + 1) * 512],
                                    el3[:, sl * 512:(sl + 1) * 512], ALU.add).then_inc(s_eev, 1)
            nktr = 0
            for ct in range(NCT):
                sl = ct % 2
                for fc in range(NCH):
                    v.wait_ge(s_ktr, nktr + 1)
                    if fc == 0 and ct >= 2:
                        v.wait_ge(s_tst[sl], 16 * (ct // 2))
                    v.tensor_copy(kts[:, sl * 1024 + fc * 128: sl * 1024 + fc * 128 + 128],
                                  KTR).then_inc(s_ktv, 1)
                    nktr += 1
                v.wait_ge(s_proj, 64 + ct + 1)
                if ct >= 2:
                    v.wait_ge(s_vst[sl], 16 * (ct // 2))
                v.tensor_copy(vstg.rearrange("p (s h e) -> p s h e", s=2, e=65)[:, sl, :, 0:64],
                              VPS(ct).rearrange("p (h e) -> p h e", e=64)) \
                    .then_inc(s_eev, 1)
            v.wait_ge(s_stm, 1)
            for h in range(H):
                v.tensor_copy(sts[:, h * 65: h * 65 + 64], SKV(h))
                v.tensor_copy(sts[:, h * 65 + 64: h * 65 + 65], SKS(h))
            v.drain().then_inc(s_stv, 1)

def build_l0_qkv():
    nc = bass.Bass("TRN2", target_bir_lowering=False, debug=False)
    DD = _decl_l0_qkv(nc)
    emit_l0_qkv(nc, DD)
    return nc




def _decl_l0_scan(nc):
    DD = {}
    DD["qf"] = nc.dram_tensor("qf", [D, TC], BF16, kind="ExternalInput").ap()
    DD["kf"] = nc.dram_tensor("kf", [D, TC], BF16, kind="ExternalInput").ap()
    DD["ktm"] = nc.dram_tensor("ktm", [TC, D], BF16, kind="ExternalInput").ap()
    DD["vtm"] = nc.dram_tensor("vtm", [TC, H * 65], BF16, kind="ExternalInput").ap()
    DD["stall"] = nc.dram_tensor("stall", [4 * H * 64, 65], F32, kind="ExternalInput").ap()
    DD["pm64"] = nc.dram_tensor("pm64", [128, 4], F32, kind="ExternalInput").ap()
    DD["g"] = nc.dram_tensor("g", [D, TC], BF16, kind="ExternalInput").ap()
    DD["x"] = nc.dram_tensor("x", [D, TC], F32, kind="ExternalInput").ap()
    DD["wot"] = nc.dram_tensor("wot", [D, D], F32, kind="ExternalInput").ap()
    DD["x1"] = nc.dram_tensor("x1", [D, TC], F32, kind="ExternalOutput").ap()
    DD["s_out"] = nc.dram_tensor("s_out", [D, TC], BF16, kind="ExternalOutput").ap()

    return DD

def emit_l0_scan(nc, DD):
    NCT = TC // 128
    HPH = 8
    qf_d = DD["qf"]
    kf_d = DD["kf"]
    ktm_d = DD["ktm"]
    vtm_d = DD["vtm"]
    stall_d = DD["stall"]
    pm_d = DD["pm64"]
    g_d = DD["g"]
    x_d = DD["x"]
    wot_d = DD["wot"]
    x1_d = DD["x1"]
    sout_d = DD["s_out"]
    with ExitStack() as ctx:
        ent = ctx.enter_context
        s_sb = ent(nc.sbuf_tensor("s_sb", [128, NCH * TC], BF16))
        trim = ent(nc.sbuf_tensor("trim", [128, 128], F32))
        pm = ent(nc.sbuf_tensor("pm", [128, 4], F32))
        onesf = ent(nc.sbuf_tensor("onesf", [128, 128], F32))
        wot_sb = ent(nc.sbuf_tensor("wot_sb", [128, NCH * D], BF16))
        s_init = ent(nc.semaphore("s_init"))
        s_ld = [ent(nc.semaphore(f"s_ld{i}")) for i in range(2)]
        s_int = ent(nc.semaphore("s_int"))
        s_pt = ent(nc.semaphore("s_pt"))
        s_o = ent(nc.semaphore("s_o"))
        s_up = ent(nc.semaphore("s_up"))
        s_rd = ent(nc.semaphore("s_rd"))
        s_bdA = [ent(nc.semaphore(f"s_bdA{i}")) for i in range(4)]
        s_bdB = [ent(nc.semaphore(f"s_bdB{i}")) for i in range(4)]
        s_sv = ent(nc.semaphore("s_sv"))
        s_stb = ent(nc.semaphore("s_stb"))
        s_bas = ent(nc.semaphore("s_bas"))
        s_vfl = ent(nc.semaphore("s_vfl"))
        s_gfl = ent(nc.semaphore("s_gfl"))
        s_wo = ent(nc.semaphore("s_wo"))
        s_x1 = ent(nc.semaphore("s_x1"))
        s_gx = [ent(nc.semaphore(f"s_gx{i}")) for i in range(2)]
        s_ost = [ent(nc.semaphore(f"s_ost{i}")) for i in range(2)]
        s_sst = ent(nc.semaphore("s_sst"))

        vfl = [0]
        with ExitStack() as ctx1:
            en1 = ctx1.enter_context
            qfh = en1(nc.sbuf_tensor("qfh", [128, 4 * TC], BF16))
            kfh = en1(nc.sbuf_tensor("kfh", [128, 4 * TC], BF16))
            ktmh = en1(nc.sbuf_tensor("ktmh", [128, NCT * 512], BF16))
            vtmh = en1(nc.sbuf_tensor("vtmh", [128, NCT * 520], BF16))
            sgth = en1(nc.sbuf_tensor("sgth", [128, 4 * (HPH // 2) * 65], F32))
            stF = en1(nc.sbuf_tensor("stF", [128, (HPH // 2) * 65], F32))
            stB = en1(nc.sbuf_tensor("stB", [128, (HPH // 2) * 65], BF16))
            ptile = en1(nc.sbuf_tensor("ptile", [128, 4 * 128], BF16))
            drow = en1(nc.sbuf_tensor("drow", [1, 4 * 128], F32))
            bc64 = en1(nc.sbuf_tensor("bc64", [128, 4 * 128], F32))
            ps = en1(nc.psum_tensor("ps", [128, 4096], F32))
            blk = en1(nc.Block())

            ST = lambda r: ps[:, (r % 2) * 512:(r % 2) * 512 + 128]
            OO = lambda r, b: ps[b:b + 64, (2 + r % 6) * 512:(2 + r % 6) * 512 + 128]
            DEN = lambda r: ps[0:1, (2 + r % 6) * 512 + 128:(2 + r % 6) * 512 + 256]
            UP = lambda r, b: ps[b:b + 64, (2 + r % 6) * 512 + 256:(2 + r % 6) * 512 + 321]
            drow_dr = nc.dram_tensor("scan_drow_dr", [4, 128], F32, kind="Internal").ap()

            @blk.gpsimd
            def _(g_):
                g_.memset(trim[:, :], 1.0).then_inc(s_gfl, 1)
                g_.wait_ge(s_gfl, 1)
                g_.affine_select(trim[:, :], trim[:, :], [[1, 128]], ALU.is_ge, 0.0,
                                 channel_multiplier=-1).then_inc(s_init, 1)
                g_.memset(onesf[:, :], 1.0).then_inc(s_init, 1)
                g_.dma_start(out=wot_sb.rearrange("p (c f) -> p c f", f=D),
                             in_=wot_d.rearrange("(c p) f -> p c f", p=128)).then_inc(s_init, 16)

            @blk.sync
            def _(sy):
                sy.dma_start(out=pm[:, :], in_=pm_d[:, :]).then_inc(s_init, 16)
                for hf in range(2):
                    if hf == 1:
                        sy.wait_ge(s_stb, 136)
                        sy.wait_ge(s_sv, 128)
                    r0 = hf * 512
                    sy.dma_start(out=qfh.rearrange("p (c t) -> p c t", t=TC),
                                 in_=qf_d.rearrange("(c p) t -> p c t", p=128)[:, 4 * hf:4 * hf + 4]) \
                        .then_inc(s_ld[hf], 16)
                    sy.dma_start(out=kfh.rearrange("p (c t) -> p c t", t=TC),
                                 in_=kf_d.rearrange("(c p) t -> p c t", p=128)[:, 4 * hf:4 * hf + 4]) \
                        .then_inc(s_ld[hf], 16)
                    sy.dma_start(out=ktmh.rearrange("p (c t) -> p c t", t=512),
                                 in_=ktm_d[:, r0:r0 + 512].rearrange("(c p) t -> p c t", p=128)) \
                        .then_inc(s_ld[hf], 16)
                    sy.dma_start(out=vtmh.rearrange("p (c t) -> p c t", t=520),
                                 in_=vtm_d[:, hf * 520:(hf + 1) * 520]
                                 .rearrange("(c p) t -> p c t", p=128)).then_inc(s_ld[hf], 16)
                    for r in range(4):
                        for par in range(2):
                            sy.dma_start(
                                out=sgth.rearrange("p (r c e) -> p r c e", r=4, e=65)
                                [par * 64:(par + 1) * 64, r],
                                in_=stall_d.rearrange("(r h p) e -> r p h e", h=H, p=64)
                                [r, :, 8 * hf + par:8 * hf + 8:2, :]).then_inc(s_ld[hf], 16)
                    for lit in range(128):
                        it = hf * 128 + lit
                        sy.wait_ge(s_rd, it + 1)
                        if it >= 4:
                            sy.wait_ge(s_bdB[it % 4], 16 * (it // 4))
                        sy.dma_start(out=drow_dr[it % 4:it % 4 + 1, :],
                                     in_=drow[:, (it % 4) * 128:(it % 4) * 128 + 128]) \
                            .then_inc(s_bdA[it % 4], 16)
                        sy.wait_ge(s_bdA[it % 4], 16 * (it // 4 + 1))
                        if it >= 4:
                            sy.wait_ge(s_sv, it - 3)
                        b = (lit % 8 % 2) * 64
                        sy.dma_start(out=bc64[b:b + 64, (it % 4) * 128:(it % 4) * 128 + 128],
                                     in_=bass.AP(drow_dr.tensor, (it % 4) * 128,
                                                 [[0, 64], [1, 128]])) \
                            .then_inc(s_bdB[it % 4], 16)
                sy.wait_ge(s_sv, 256)
                sy.dma_start(out=sout_d.rearrange("(c p) t -> p c t", p=128),
                             in_=s_sb.rearrange("p (c t) -> p c t", t=TC)).then_inc(s_sst, 16)
                sy.wait_ge(s_sst, 16)

            @blk.tensor
            def _(t):
                t.wait_ge(s_init, 18)
                for hf in range(2):
                    t.wait_ge(s_ld[hf], 16 * 9)
                    t.wait_ge(s_bas, hf + 1)

                    def hb(lit):
                        it = hf * 128 + lit
                        ct, h = lit // 8, lit % 8
                        return (it, ct, h, (h % 2) * 64, (h // 2) * 65,
                                (h // 2) * TC + ct * 128,
                                slice((h % 2) * 64, (h % 2) * 64 + 64),
                                (h // 2) * 128 + (h % 2) * 64)

                    def em_st(lit):
                        it, ct, h, b, cg, hq, rw, off = hb(lit)
                        if lit >= 2:
                            t.wait_ge(s_pt, it - 1)
                        t.matmul(ST(it), kfh[rw, hq:hq + 128], qfh[rw, hq:hq + 128],
                                 start=True, stop=True).then_inc(s_int, 1)

                    def em_mid(lit):
                        it, ct, h, b, cg, hq, rw, off = hb(lit)
                        t.wait_ge(s_pt, it + 1)
                        t.wait_ge(s_stb, hf * 136 + lit + 1 if lit < 8 else hf * 136 + 8 + lit - 7)
                        if lit >= 6:
                            t.wait_ge(s_sv, it - 5)
                            t.wait_ge(s_stb, hf * 136 + 8 + lit - 5)
                        t.matmul(OO(it, b), vtmh[:, ct * 520 + h * 65: ct * 520 + h * 65 + 64],
                                 ptile[:, (it % 4) * 128:(it % 4) * 128 + 128],
                                 start=True, stop=False, tile_position=(0, b))
                        t.matmul(OO(it, b), stB[b:b + 64, cg:cg + 64], qfh[rw, hq:hq + 128],
                                 start=False, stop=True, tile_position=(b, b))
                        t.matmul(DEN(it), vtmh[:, ct * 520 + h * 65 + 64: ct * 520 + h * 65 + 65],
                                 ptile[:, (it % 4) * 128:(it % 4) * 128 + 128],
                                 start=True, stop=False, tile_position=(0, 0))
                        t.matmul(DEN(it), stB[b:b + 64, cg + 64:cg + 65], qfh[rw, hq:hq + 128],
                                 start=False, stop=True, tile_position=(b, 0)).then_inc(s_o, 1)
                        t.matmul(UP(it, b), ktmh[:, ct * 512 + off: ct * 512 + off + 64],
                                 vtmh[:, ct * 520 + h * 65: ct * 520 + (h + 1) * 65],
                                 start=True, stop=True, tile_position=(0, b)).then_inc(s_up, 1)

                    for lit in range(130):
                        if lit < 128:
                            em_st(lit)
                        if lit >= 2:
                            em_mid(lit - 2)

            @blk.vector
            def _(v):
                def VF(inst):
                    vfl[0] += 1
                    inst.then_inc(s_vfl, 1)
                    v.wait_ge(s_vfl, vfl[0])
                    return inst
                for hf in range(2):
                    v.wait_ge(s_ld[hf], 16 * 9)
                    VF(v.memset(stF[:, :], 0.0))
                    for r in range(4):
                        VF(v.scalar_tensor_tensor(
                            stF[:, :],
                            sgth[:, r * (HPH // 2) * 65:(r + 1) * (HPH // 2) * 65],
                            pm[:, r:r + 1], stF[:, :], ALU.mult, ALU.add))
                    for h in range(HPH):
                        b = (h % 2) * 64
                        cg = (h // 2) * 65
                        v.tensor_copy(stB[b:b + 64, cg:cg + 65],
                                      stF[b:b + 64, cg:cg + 65]).then_inc(s_stb, 1)
                    v.drain().then_inc(s_bas, 1)

                    def vb(lit):
                        it = hf * 128 + lit
                        ct, h = lit // 8, lit % 8
                        return it, ct, h, (h % 2) * 64, (h // 2) * 65

                    def vm_pt(lit):
                        it, ct, h, b, cg = vb(lit)
                        v.wait_ge(s_int, it + 1)
                        v.tensor_tensor(ptile[:, (it % 4) * 128:(it % 4) * 128 + 128],
                                        ST(it), trim[:, :], ALU.mult).then_inc(s_pt, 1)

                    def vm_ep(lit):
                        it, ct, h, b, cg = vb(lit)
                        v.wait_ge(s_o, it + 1)
                        if it >= 4:
                            v.wait_ge(s_bdA[it % 4], 16 * (it // 4))
                        VF(v.tensor_scalar(drow[:, (it % 4) * 128:(it % 4) * 128 + 128],
                                           DEN(it), 1e-8, None, ALU.add))
                        v.reciprocal(drow[:, (it % 4) * 128:(it % 4) * 128 + 128],
                                     drow[:, (it % 4) * 128:(it % 4) * 128 + 128]) \
                            .then_inc(s_rd, 1)
                        v.wait_ge(s_up, it + 1)
                        VF(v.tensor_tensor(stF[b:b + 64, cg:cg + 65],
                                           stF[b:b + 64, cg:cg + 65], UP(it, b), ALU.add))
                        v.tensor_copy(stB[b:b + 64, cg:cg + 65],
                                      stF[b:b + 64, cg:cg + 65]).then_inc(s_stb, 1)

                    def vm_so(lit):
                        it, ct, h, b, cg = vb(lit)
                        v.wait_ge(s_bdB[it % 4], 16 * (it // 4 + 1))
                        v.tensor_tensor(s_sb[b:b + 64, (4 * hf + h // 2) * TC + ct * 128:
                                             (4 * hf + h // 2) * TC + ct * 128 + 128],
                                        OO(it, b),
                                        bc64[b:b + 64, (it % 4) * 128:(it % 4) * 128 + 128],
                                        ALU.mult).then_inc(s_sv, 1)

                    for lit in range(134):
                        if lit < 128:
                            vm_pt(lit)
                        if 2 <= lit < 130:
                            vm_ep(lit - 2)
                        if lit >= 6:
                            vm_so(lit - 6)

        with ExitStack() as ctx2:
            en2 = ctx2.enter_context
            gtile = en2(nc.sbuf_tensor("gtile", [128, 2 * NCH * 512], BF16))
            xtile = en2(nc.sbuf_tensor("xtile", [128, 2 * NCH * 512], F32))
            x1stg = en2(nc.sbuf_tensor("x1stg", [128, 2 * NCH * 512], F32))
            t1s = en2(nc.sbuf_tensor("t1s", [128, 512], F32))
            ps2 = en2(nc.psum_tensor("ps2", [128, 2048], F32))
            blk2 = en2(nc.Block())
            WOPS = lambda r: ps2[:, (r % 4) * 512:(r % 4) * 512 + 512]

            @blk2.sync
            def _(sy):
                for tt in range(4):
                    sl = tt % 2
                    if tt >= 2:
                        sy.wait_ge(s_x1, 8 * (tt - 1))
                    sy.dma_start(out=xtile.rearrange("p (s c t) -> p s c t", s=2, t=512)[:, sl],
                                 in_=x_d.rearrange("(c p) t -> p c t", p=128)
                                 [:, :, tt * 512:(tt + 1) * 512]).then_inc(s_gx[sl], 16)
                for tt in range(4):
                    sl = tt % 2
                    sy.wait_ge(s_x1, 8 * (tt + 1))
                    sy.dma_start(out=x1_d.rearrange("(c p) t -> p c t", p=128)
                                 [:, :, tt * 512:(tt + 1) * 512],
                                 in_=x1stg.rearrange("p (s c t) -> p s c t", s=2, t=512)[:, sl]) \
                        .then_inc(s_ost[sl], 16)
                sy.wait_ge(s_ost[0], 32)
                sy.wait_ge(s_ost[1], 32)

            @blk2.gpsimd
            def _(g_):
                for tt in range(4):
                    sl = tt % 2
                    if tt >= 2:
                        g_.wait_ge(s_x1, 8 * (tt - 1))
                    g_.dma_start(out=gtile.rearrange("p (s c t) -> p s c t", s=2, t=512)[:, sl],
                                 in_=g_d.rearrange("(c p) t -> p c t", p=128)
                                 [:, :, tt * 512:(tt + 1) * 512]).then_inc(s_gx[sl], 16)

            @blk2.tensor
            def _(t):
                nwo = 0
                for tt in range(4):
                    sl = tt % 2
                    t.wait_ge(s_gx[sl], 32 * (tt // 2 + 1))
                    for oc in range(NCH):
                        if nwo >= 4:
                            t.wait_ge(s_x1, nwo - 3)
                        for ic in range(NCH):
                            mm = t.matmul(WOPS(nwo),
                                          wot_sb[:, ic * D + oc * 128: ic * D + oc * 128 + 128],
                                          s_sb[:, ic * TC + tt * 512: ic * TC + tt * 512 + 512],
                                          start=(ic == 0), stop=(ic == NCH - 1))
                        mm.then_inc(s_wo, 1)
                        nwo += 1

            @blk2.vector
            def _(v):
                def VF(inst):
                    vfl[0] += 1
                    inst.then_inc(s_vfl, 1)
                    v.wait_ge(s_vfl, vfl[0])
                    return inst
                nwo = 0
                for tt in range(4):
                    sl = tt % 2
                    v.wait_ge(s_gx[sl], 32 * (tt // 2 + 1))
                    if tt >= 2:
                        v.wait_ge(s_ost[sl], 16 * (tt // 2))
                    for oc in range(NCH):
                        v.wait_ge(s_wo, nwo + 1)
                        VF(v.tensor_tensor(t1s[:, :], WOPS(nwo),
                                           gtile[:, sl * NCH * 512 + oc * 512:
                                                 sl * NCH * 512 + (oc + 1) * 512], ALU.mult))
                        v.tensor_tensor(x1stg[:, sl * NCH * 512 + oc * 512:
                                              sl * NCH * 512 + (oc + 1) * 512],
                                        xtile[:, sl * NCH * 512 + oc * 512:
                                              sl * NCH * 512 + (oc + 1) * 512],
                                        t1s[:, :], ALU.add).then_inc(s_x1, 1)
                        nwo += 1

def build_l0_scan():
    nc = bass.Bass("TRN2", target_bir_lowering=False, debug=False)
    DD = _decl_l0_scan(nc)
    emit_l0_scan(nc, DD)
    return nc




def _decl_l1_lat(nc):
    DD = {}
    DD["xn1"] = nc.dram_tensor("xn1", [D, TC], BF16, kind="ExternalInput").ap()
    DD["wk"] = nc.dram_tensor("wk", [D, D], F32, kind="ExternalInput").ap()
    DD["wv"] = nc.dram_tensor("wv", [D, D], F32, kind="ExternalInput").ap()
    DD["wdn"] = nc.dram_tensor("wdn", [D, DLAT], F32, kind="ExternalInput").ap()
    DD["lat"] = nc.dram_tensor("lat", [DLAT, TC], BF16, kind="ExternalOutput").ap()
    return DD

def emit_l1_lat(nc, DD):
    xn_d = DD["xn1"]
    wk_d = DD["wk"]
    wv_d = DD["wv"]
    wdn_d = DD["wdn"]
    lat_d = DD["lat"]
    with ExitStack() as ctx:
        ent = ctx.enter_context
        xnsb = ent(nc.sbuf_tensor("xnsb", [128, NCH * TC], BF16))
        wk = ent(nc.sbuf_tensor("wk_sb", [128, NCH * D], BF16))
        wv = ent(nc.sbuf_tensor("wv_sb", [128, NCH * D], BF16))
        wdn = ent(nc.sbuf_tensor("wdn_sb", [128, NCH * DLAT], BF16))
        kvf = ent(nc.sbuf_tensor("kvf", [128, NCH * TC], BF16))
        lstg = ent(nc.sbuf_tensor("lstg", [128, 2 * 2 * 512], BF16))
        ps = ent(nc.psum_tensor("ps", [128, 4096], F32))
        s_init = ent(nc.semaphore("s_init"))
        s_kv = ent(nc.semaphore("s_kv"))
        s_kvv = ent(nc.semaphore("s_kvv"))
        s_lt = ent(nc.semaphore("s_lt"))
        s_ltv = ent(nc.semaphore("s_ltv"))
        s_ost = [ent(nc.semaphore(f"s_ost{i}")) for i in range(2)]
        block = ent(nc.Block())
        PRJ = lambda r: ps[:, (r % 4) * 512:(r % 4) * 512 + 512]
        LPS = lambda r: ps[:, 2048 + (r % 2) * 1024: 2048 + (r % 2) * 1024 + 1024]

        @block.gpsimd
        def _(g_):
            g_.dma_start(out=wk.rearrange("p (c f) -> p c f", f=D),
                         in_=wk_d.rearrange("(c p) f -> p c f", p=128)).then_inc(s_init, 16)
            g_.dma_start(out=wv.rearrange("p (c f) -> p c f", f=D),
                         in_=wv_d.rearrange("(c p) f -> p c f", p=128)).then_inc(s_init, 16)
            g_.dma_start(out=wdn.rearrange("p (c f) -> p c f", f=DLAT),
                         in_=wdn_d.rearrange("(c p) f -> p c f", p=128)).then_inc(s_init, 16)

        @block.sync
        def _(sy):
            sy.dma_start(out=xnsb.rearrange("p (c t) -> p c t", t=TC),
                         in_=xn_d.rearrange("(c p) t -> p c t", p=128)).then_inc(s_init, 16)
            for i in range(4):
                sy.wait_ge(s_ltv, i + 1)
                sy.dma_start(out=lat_d.rearrange("(c p) t -> p c t", p=128)
                             [:, :, i * 512:(i + 1) * 512],
                             in_=lstg.rearrange("p (s c t) -> p s c t", s=2, t=512)[:, i % 2]) \
                    .then_inc(s_ost[i % 2], 16)
            sy.wait_ge(s_ost[0], 32)
            sy.wait_ge(s_ost[1], 32)

        @block.tensor
        def _(t):
            t.wait_ge(s_init, 64)
            nkv = 0
            for oc in range(NCH):
                for ti in range(4):
                    if nkv >= 4:
                        t.wait_ge(s_kvv, nkv - 3)
                    for ic in range(NCH):
                        t.matmul(PRJ(nkv), wk[:, ic * D + oc * 128: ic * D + oc * 128 + 128],
                                 xnsb[:, ic * TC + ti * 512: ic * TC + ti * 512 + 512],
                                 start=(ic == 0), stop=False)
                        mm = t.matmul(PRJ(nkv), wv[:, ic * D + oc * 128: ic * D + oc * 128 + 128],
                                      xnsb[:, ic * TC + ti * 512: ic * TC + ti * 512 + 512],
                                      start=False, stop=(ic == NCH - 1))
                    mm.then_inc(s_kv, 1)
                    nkv += 1
            for ti in range(4):
                if ti >= 2:
                    t.wait_ge(s_ltv, ti - 1)
                t.wait_ge(s_kvv, 32)
                for lc in range(2):
                    for ic in range(NCH):
                        mm = t.matmul(LPS(ti)[:, lc * 512:(lc + 1) * 512],
                                      wdn[:, ic * DLAT + lc * 128: ic * DLAT + lc * 128 + 128],
                                      kvf[:, ic * TC + ti * 512: ic * TC + ti * 512 + 512],
                                      start=(ic == 0), stop=(ic == NCH - 1))
                mm.then_inc(s_lt, 1)

        @block.vector
        def _(v):
            nkv = 0
            for oc in range(NCH):
                for ti in range(4):
                    v.wait_ge(s_kv, nkv + 1)
                    v.tensor_copy(kvf[:, oc * TC + ti * 512: oc * TC + ti * 512 + 512],
                                  PRJ(nkv)).then_inc(s_kvv, 1)
                    nkv += 1
            for ti in range(4):
                v.wait_ge(s_lt, ti + 1)
                if ti >= 2:
                    v.wait_ge(s_ost[ti % 2], 16 * (ti // 2))
                v.tensor_copy(lstg[:, (ti % 2) * 1024:(ti % 2) * 1024 + 1024],
                              LPS(ti)).then_inc(s_ltv, 1)

def build_l1_lat():
    nc = bass.Bass("TRN2", target_bir_lowering=False, debug=False)
    DD = _decl_l1_lat(nc)
    emit_l1_lat(nc, DD)
    return nc







def _decl_l1_attn(nc):
    DD = {}
    NJT = S // 128
    DD["xn1"] = nc.dram_tensor("xn1", [D, TC], BF16, kind="ExternalInput").ap()
    DD["latf"] = nc.dram_tensor("latf", [DLAT, S], BF16, kind="ExternalInput").ap()
    DD["lato"] = nc.dram_tensor("lato", [DLAT, TC], BF16, kind="ExternalInput").ap()
    DD["wq"] = nc.dram_tensor("wq", [D, D], F32, kind="ExternalInput").ap()
    DD["wupk"] = nc.dram_tensor("wupk", [DLAT, D], F32, kind="ExternalInput").ap()
    DD["wupv"] = nc.dram_tensor("wupv", [DLAT, D], F32, kind="ExternalInput").ap()
    DD["posq"] = nc.dram_tensor("posq", [1, TC], F32, kind="ExternalInput").ap()
    DD["posk"] = nc.dram_tensor("posk", [1, S], F32, kind="ExternalInput").ap()
    DD["amask"] = nc.dram_tensor("amask", [1, S], F32, kind="ExternalInput").ap()
    DD["bmask"] = nc.dram_tensor("bmask", [1, S], F32, kind="ExternalInput").ap()
    DD["biascol"] = nc.dram_tensor("biascol", [128, NJT], F32, kind="ExternalInput").ap()
    DD["s1"] = nc.dram_tensor("s1", [D, TC], BF16, kind="ExternalOutput").ap()

    return DD

def emit_l1_attn(nc, DD, gathered=False):
    CDEC = 1e-4
    SCL = float(1.0 / (np.sqrt(np.float32(1024.0)) + 1e-8))
    NJT = S // 128
    NJO = TC // 128
    xn_d = DD["xn1"]
    latf_d = DD["latf"]
    lato_d = DD["lato"]
    wq_d = DD["wq"]
    wupk_d = DD["wupk"]
    wupv_d = DD["wupv"]
    posq_d = DD["posq"]
    posk_d = DD["posk"]
    am_d = DD["amask"]
    bm_d = DD["bmask"]
    bc_d = DD["biascol"]
    s1_d = DD["s1"]
    with ExitStack() as ctx:
        ent = ctx.enter_context
        xnsb = ent(nc.sbuf_tensor("xnsb", [128, NCH * TC], BF16))
        latf = ent(nc.sbuf_tensor("latf_sb", [128, 2 * S], BF16))
        lato = ent(nc.sbuf_tensor("lato_sb", [128, 2 * TC], BF16))
        wq = ent(nc.sbuf_tensor("wq_sb", [128, NCH * D], BF16))
        wupk = ent(nc.sbuf_tensor("wupk_sb", [128, 2 * D], BF16))
        wupv = ent(nc.sbuf_tensor("wupv_sb", [128, 2 * D], BF16))
        rowbuf = ent(nc.sbuf_tensor("rowbuf", [1, 5 * 512], F32))
        qfac = ent(nc.sbuf_tensor("qfac", [128, TC], F32))
        kfsel = ent(nc.sbuf_tensor("kfsel", [128, S], BF16))
        kfown = ent(nc.sbuf_tensor("kfown", [128, TC], BF16))
        corr = ent(nc.sbuf_tensor("corr", [128, 4 * 512], F32))
        bcol = ent(nc.sbuf_tensor("bcol", [128, NJT], F32))
        zcol = ent(nc.sbuf_tensor("zcol", [128, 1], F32))
        onesf = ent(nc.sbuf_tensor("onesf", [128, 128], F32))
        qh = ent(nc.sbuf_tensor("qh", [128, TC], BF16))
        khg = ent(nc.sbuf_tensor("khg", [128, S], BF16))
        kho = ent(nc.sbuf_tensor("kho", [128, TC], BF16))
        vstg = ent(nc.sbuf_tensor("vstg", [128, NJT * 65], BF16))
        vsto = ent(nc.sbuf_tensor("vsto", [128, NJO * 65], BF16))
        ptile = ent(nc.sbuf_tensor("ptile", [128, 4 * 1024], BF16))
        stile = ent(nc.sbuf_tensor("stile", [128, 2 * 512], F32))
        drow = ent(nc.sbuf_tensor("drow", [1, 2 * 512], F32))
        bc64a = ent(nc.sbuf_tensor("bc64a", [128, 2 * 512], F32))
        s1t = ent(nc.sbuf_tensor("s1t", [64, 2 * 512], BF16))
        ps = ent(nc.psum_tensor("ps", [128, 4096], F32))
        s_init = ent(nc.semaphore("s_init"))
        s_gfl = ent(nc.semaphore("s_gfl"))
        s_vfl = ent(nc.semaphore("s_vfl"))
        s_rw = ent(nc.semaphore("s_rw"))
        s_fd = ent(nc.semaphore("s_fd"))
        s_fexp = ent(nc.semaphore("s_fexp"))
        s_fmul = ent(nc.semaphore("s_fmul"))
        s_fb = ent(nc.semaphore("s_fb"))
        s_fv = ent(nc.semaphore("s_fv"))
        s_qp = ent(nc.semaphore("s_qp"))
        s_qv = ent(nc.semaphore("s_qv"))
        s_st = ent(nc.semaphore("s_st"))
        s_pt = ent(nc.semaphore("s_pt"))
        s_cr = ent(nc.semaphore("s_cr"))
        s_pv = ent(nc.semaphore("s_pv"))
        s_pp = ent(nc.semaphore("s_pp"))
        s_rd = ent(nc.semaphore("s_rd"))
        s_bd1 = [ent(nc.semaphore(f"s_bd1{i}")) for i in range(2)]
        s_bd2 = [ent(nc.semaphore(f"s_bd2{i}")) for i in range(2)]
        s_sv = ent(nc.semaphore("s_sv"))
        s_ost = [ent(nc.semaphore(f"s_ost{i}")) for i in range(2)]
        block = ent(nc.Block())

        STp = lambda r: ps[:, (r % 3) * 1024:(r % 3) * 1024 + 1024]
        PV = lambda r: ps[0:65, 3072 + (r % 2) * 512: 3072 + (r % 2) * 512 + 512]
        PRD = lambda r: ps[:, (r % 2) * 512:(r % 2) * 512 + 512]
        drow_dr = nc.dram_tensor("attn_drow_dr", [2, 512], F32, kind="Internal").ap()

        INIT_TOT = 16 * (14 if gathered else 7) + 4
        FPLAN = [("q", i) for i in range(4)] + [("k", i) for i in range(16)] \
            + [("o", i) for i in range(4)]
        RB = lambda sl: rowbuf[:, sl * 512:(sl + 1) * 512]

        @block.gpsimd
        def _(g_):
            g_.dma_start(out=wq.rearrange("p (c f) -> p c f", f=D),
                         in_=wq_d.rearrange("(c p) f -> p c f", p=128)).then_inc(s_init, 16)
            g_.dma_start(out=wupk.rearrange("p (c f) -> p c f", f=D),
                         in_=wupk_d.rearrange("(c p) f -> p c f", p=128)).then_inc(s_init, 16)
            g_.dma_start(out=wupv.rearrange("p (c f) -> p c f", f=D),
                         in_=wupv_d.rearrange("(c p) f -> p c f", p=128)).then_inc(s_init, 16)
            for r in range(4):
                g_.iota(corr[:, r * 512:(r + 1) * 512], [[-1, 512]], base=r * 128,
                        channel_multiplier=1,
                        allow_small_or_imprecise_dtypes=True).then_inc(s_gfl, 1)
            g_.wait_ge(s_gfl, 4)
            g_.tensor_scalar_max(corr[:, :], corr[:, :], 0.0).then_inc(s_gfl, 1)
            g_.memset(onesf[:, :], 1.0).then_inc(s_init, 1)
            g_.memset(vstg[:, :], 1.0).then_inc(s_init, 1)
            g_.memset(vsto[:, :], 1.0).then_inc(s_init, 1)
            g_.memset(zcol[:, :], 0.0).then_inc(s_init, 1)

        @block.sync
        def _(sy):
            sy.dma_start(out=xnsb.rearrange("p (c t) -> p c t", t=TC),
                         in_=xn_d.rearrange("(c p) t -> p c t", p=128)).then_inc(s_init, 16)
            if gathered:
                for rr in range(4):
                    for lc in range(2):
                        sy.dma_start(out=latf[:, lc * S + rr * TC: lc * S + (rr + 1) * TC],
                                     in_=latf_d[rr * 256 + lc * 128: rr * 256 + lc * 128 + 128, :]) \
                            .then_inc(s_init, 16)
            else:
                sy.dma_start(out=latf.rearrange("p (c t) -> p c t", t=S),
                             in_=latf_d.rearrange("(c p) t -> p c t", p=128)).then_inc(s_init, 16)
            sy.dma_start(out=lato.rearrange("p (c t) -> p c t", t=TC),
                         in_=lato_d.rearrange("(c p) t -> p c t", p=128)).then_inc(s_init, 16)
            sy.dma_start(out=bcol[:, :], in_=bc_d[:, :]).then_inc(s_init, 16)
            nfd = 0
            for p, (kind, idx) in enumerate(FPLAN):
                if p >= 1:
                    sy.wait_ge(s_fv, p)
                src = posk_d if kind == "k" else posq_d
                sy.dma_start(out=RB(0), in_=src[:, idx * 512:(idx + 1) * 512]) \
                    .then_inc(s_fd, 16)
                nfd += 16
                if kind == "k":
                    sy.dma_start(out=RB(1), in_=am_d[:, idx * 512:(idx + 1) * 512]) \
                        .then_inc(s_fd, 16)
                    sy.dma_start(out=RB(2), in_=bm_d[:, idx * 512:(idx + 1) * 512]) \
                        .then_inc(s_fd, 16)
                    nfd += 32
            for h in range(H):
                for it in range(4):
                    n = h * 4 + it
                    sy.wait_ge(s_rd, n + 1)
                    if n >= 2:
                        sy.wait_ge(s_bd2[n % 2], 16 * (n // 2))
                    sy.dma_start(out=drow_dr[n % 2:n % 2 + 1, :],
                                 in_=drow[:, (n % 2) * 512:(n % 2) * 512 + 512]) \
                        .then_inc(s_bd1[n % 2], 16)
                    sy.wait_ge(s_bd1[n % 2], 16 * (n // 2 + 1))
                    if n >= 2:
                        sy.wait_ge(s_sv, n - 1)
                    sy.dma_start(out=bc64a[0:64, (n % 2) * 512:(n % 2) * 512 + 512],
                                 in_=bass.AP(drow_dr.tensor, (n % 2) * 512,
                                             [[0, 64], [1, 512]])) \
                        .then_inc(s_bd2[n % 2], 16)
                    sy.wait_ge(s_sv, n + 1)
                    sy.dma_start(out=s1_d[h * 64:(h + 1) * 64, it * 512:(it + 1) * 512],
                                 in_=s1t[:, (n % 2) * 512:(n % 2) * 512 + 512]) \
                        .then_inc(s_ost[n % 2], 16)
            sy.wait_ge(s_ost[0], 16 * 32)
            sy.wait_ge(s_ost[1], 16 * 32)

        @block.scalar
        def _(s):
            s.wait_ge(s_init, INIT_TOT)
            s.wait_ge(s_gfl, 5)
            s.activation(corr[:, :], corr[:, :], AF.Exp, scale=-2.0 * CDEC).then_inc(s_rw, 1)
            nfd = 0
            for p, (kind, idx) in enumerate(FPLAN):
                nfd += 48 if kind == "k" else 16
                s.wait_ge(s_fd, nfd)
                sc0 = -CDEC if kind == "q" else CDEC
                s.activation(RB(3), RB(0), AF.Exp, scale=sc0)
                s.activation(RB(4), RB(0), AF.Exp, scale=-sc0).then_inc(s_fexp, 1)
            for h in range(H):
                for it in range(4):
                    hit = h * 4 + it
                    for Jp in range(40):
                        nst = hit * 40 + Jp
                        sl = (nst % 4) * 1024
                        s.wait_ge(s_st, nst + 1)
                        if nst >= 4:
                            s.wait_ge(s_pp, nst - 3)
                        if Jp < 32:
                            s.activation(ptile[:, sl:sl + 1024], STp(nst),
                                         AF.Exp, bias=bcol[:, 2 * Jp:2 * Jp + 1]) \
                                .then_inc(s_pt, 1)
                        else:
                            Jo0 = 2 * (Jp - 32)
                            d0, d1 = Jo0 - it * 4, Jo0 + 1 - it * 4
                            diag0, diag1 = 0 <= d0 < 4, 0 <= d1 < 4
                            if not diag0 and not diag1:
                                s.activation(ptile[:, sl:sl + 1024], STp(nst),
                                             AF.Exp).then_inc(s_pt, 1)
                            else:
                                insts = []
                                for half, dd, dg in ((0, d0, diag0), (1, d1, diag1)):
                                    dst = ptile[:, sl + half * 512: sl + (half + 1) * 512]
                                    if dg:
                                        s.wait_ge(s_cr, hit * 4 + dd + 1)
                                        insts.append(s.activation(
                                            dst, stile[:, (dd % 2) * 512:(dd % 2) * 512 + 512],
                                            AF.Exp))
                                    else:
                                        insts.append(s.activation(
                                            dst, STp(nst)[:, half * 512:(half + 1) * 512],
                                            AF.Exp))
                                insts[-1].then_inc(s_pt, 1)

        @block.tensor
        def _(t):
            t.wait_ge(s_init, INIT_TOT)
            for p, (kind, idx) in enumerate(FPLAN):
                t.wait_ge(s_fmul, p + 1)
                for ver in range(2):
                    mm = t.matmul(PRD(0)[ver * 64:(ver + 1) * 64, :], onesf[0:1, 0:64],
                                  RB(3 + ver), start=True, stop=True,
                                  tile_position=(0, ver * 64))
                mm.then_inc(s_fb, 1)
            nprd = [48]
            nst = 0
            def _emit_pv(t, hit, Jp):
                it = hit % 4
                npv = hit * 40 + Jp
                t.wait_ge(s_pt, npv + 1)
                for half in range(2):
                    if Jp < 32:
                        J = 2 * Jp + half
                        vsl = vstg[:, J * 65:(J + 1) * 65]
                    else:
                        Jo = 2 * (Jp - 32) + half
                        vsl = vsto[:, Jo * 65:(Jo + 1) * 65]
                    mm = t.matmul(PV(it), vsl,
                                  ptile[:, (npv % 4) * 1024 + half * 512:
                                        (npv % 4) * 1024 + (half + 1) * 512],
                                  start=(Jp == 0 and half == 0),
                                  stop=(Jp == 39 and half == 1))
                mm.then_inc(s_pp, 1)
                return mm
            t.wait_ge(s_fv, 24)
            def pguard():
                pi = nprd[0] - 48
                if pi >= 2:
                    t.wait_ge(s_qv, pi - 1)
            for h in range(H):
                hw = h * 64
                if h >= 1:
                    t.wait_ge(s_pt, 160 * h)
                for tt in range(4):
                    pguard()
                    for ver in range(2):
                        for ic in range(NCH):
                            mm = t.matmul(PRD(nprd[0])[ver * 64:(ver + 1) * 64, :],
                                          wq[:, ic * D + hw: ic * D + hw + 64],
                                          xnsb[:, ic * TC + tt * 512: ic * TC + (tt + 1) * 512],
                                          start=(ic == 0), stop=(ic == NCH - 1),
                                          tile_position=(0, ver * 64))
                    mm.then_inc(s_qp, 1)
                    nprd[0] += 1
                for jt in range(16):
                    pguard()
                    for ver in range(2):
                        for lc in range(2):
                            mm = t.matmul(PRD(nprd[0])[ver * 64:(ver + 1) * 64, :],
                                          wupk[:, lc * D + hw: lc * D + hw + 64],
                                          latf[:, lc * S + jt * 512: lc * S + (jt + 1) * 512],
                                          start=(lc == 0), stop=(lc == 1),
                                          tile_position=(0, ver * 64))
                    mm.then_inc(s_qp, 1)
                    nprd[0] += 1
                for jt in range(4):
                    pguard()
                    for ver in range(2):
                        for lc in range(2):
                            mm = t.matmul(PRD(nprd[0])[ver * 64:(ver + 1) * 64, :],
                                          wupk[:, lc * D + hw: lc * D + hw + 64],
                                          lato[:, lc * TC + jt * 512: lc * TC + (jt + 1) * 512],
                                          start=(lc == 0), stop=(lc == 1),
                                          tile_position=(0, ver * 64))
                    mm.then_inc(s_qp, 1)
                    nprd[0] += 1
                for pk in range(8):
                    pguard()
                    for j8 in range(8):
                        jt = pk * 8 + j8
                        for lc in range(2):
                            mm = t.matmul(PRD(nprd[0])[:, j8 * 64:(j8 + 1) * 64],
                                          latf[:, lc * S + jt * 128: lc * S + jt * 128 + 128],
                                          wupv[:, lc * D + hw: lc * D + hw + 64],
                                          start=(lc == 0), stop=(lc == 1))
                    mm.then_inc(s_qp, 1)
                    nprd[0] += 1
                for pk in range(2):
                    pguard()
                    for j8 in range(8):
                        jt = pk * 8 + j8
                        for lc in range(2):
                            mm = t.matmul(PRD(nprd[0])[:, j8 * 64:(j8 + 1) * 64],
                                          lato[:, lc * TC + jt * 128: lc * TC + jt * 128 + 128],
                                          wupv[:, lc * D + hw: lc * D + hw + 64],
                                          start=(lc == 0), stop=(lc == 1))
                    mm.then_inc(s_qp, 1)
                    nprd[0] += 1
                t.wait_ge(s_qv, (h + 1) * 34)
                for it in range(4):
                    hit = h * 4 + it
                    if hit >= 2:
                        t.wait_ge(s_sv, hit - 1)
                    for Jp in range(40):
                        if nst >= 3:
                            t.wait_ge(s_pt, nst - 2)
                        sl = STp(nst)
                        if Jp >= 2:
                            _emit_pv(t, hit, Jp - 2)
                        for half in range(2):
                            if Jp < 32:
                                J = 2 * Jp + half
                                mm = t.matmul(sl[:, half * 512:(half + 1) * 512],
                                              khg[:, J * 128:(J + 1) * 128],
                                              qh[:, it * 512:(it + 1) * 512],
                                              start=True, stop=True)
                            else:
                                Jo = 2 * (Jp - 32) + half
                                d = Jo - it * 4
                                if d >= 4:
                                    mm = t.matmul(sl[:, half * 512:(half + 1) * 512],
                                                  kho[64:128, Jo * 128:(Jo + 1) * 128],
                                                  qh[64:128, it * 512:(it + 1) * 512],
                                                  start=True, stop=True,
                                                  tile_position=(64, 0))
                                else:
                                    mm = t.matmul(sl[:, half * 512:(half + 1) * 512],
                                                  kho[0:64, Jo * 128:(Jo + 1) * 128],
                                                  qh[0:64, it * 512:(it + 1) * 512],
                                                  start=True, stop=True)
                        mm.then_inc(s_st, 1)
                        nst += 1
                    _emit_pv(t, hit, 38)
                    _emit_pv(t, hit, 39)
                    t.drain().then_inc(s_pv, 1)

        @block.vector
        def _(v):
            vfl = [0]
            def VF(inst):
                vfl[0] += 1
                inst.then_inc(s_vfl, 1)
                v.wait_ge(s_vfl, vfl[0])
                return inst
            v.wait_ge(s_init, INIT_TOT)
            for p, (kind, idx) in enumerate(FPLAN):
                v.wait_ge(s_fexp, p + 1)
                if kind == "q":
                    VF(v.tensor_scalar(RB(3), RB(3), SCL, None, ALU.mult))
                    v.tensor_scalar(RB(4), RB(4), SCL, None, ALU.mult).then_inc(s_fmul, 1)
                elif kind == "k":
                    VF(v.tensor_tensor(RB(3), RB(3), RB(1), ALU.mult))
                    v.tensor_tensor(RB(4), RB(4), RB(2), ALU.mult).then_inc(s_fmul, 1)
                else:
                    VF(v.tensor_scalar(RB(3), RB(3), 1.0, None, ALU.mult))
                    v.tensor_scalar(RB(4), RB(4), 1.0, None, ALU.mult).then_inc(s_fmul, 1)
                v.wait_ge(s_fb, p + 1)
                dst = {"q": qfac, "k": kfsel, "o": kfown}[kind]
                v.tensor_copy(dst[:, idx * 512:(idx + 1) * 512], PRD(0)).then_inc(s_fv, 1)
            for h in range(H):
                nq = 0
                for tt in range(4):
                    v.wait_ge(s_qp, h * 34 + nq + 1)
                    v.tensor_tensor(qh[:, tt * 512:(tt + 1) * 512], PRD(48 + h * 34 + nq),
                                    qfac[:, tt * 512:(tt + 1) * 512], ALU.mult) \
                        .then_inc(s_qv, 1)
                    nq += 1
                for jt in range(16):
                    v.wait_ge(s_qp, h * 34 + nq + 1)
                    v.tensor_tensor(khg[:, jt * 512:(jt + 1) * 512], PRD(48 + h * 34 + nq),
                                    kfsel[:, jt * 512:(jt + 1) * 512], ALU.mult) \
                        .then_inc(s_qv, 1)
                    nq += 1
                for jt in range(4):
                    v.wait_ge(s_qp, h * 34 + nq + 1)
                    v.tensor_tensor(kho[:, jt * 512:(jt + 1) * 512], PRD(48 + h * 34 + nq),
                                    kfown[:, jt * 512:(jt + 1) * 512], ALU.mult) \
                        .then_inc(s_qv, 1)
                    nq += 1
                for pk in range(8):
                    v.wait_ge(s_qp, h * 34 + nq + 1)
                    v.tensor_copy(vstg.rearrange("p (j e) -> p j e", e=65)
                                  [:, pk * 8:(pk + 1) * 8, 0:64],
                                  PRD(48 + h * 34 + nq).rearrange("p (j e) -> p j e", e=64)) \
                        .then_inc(s_qv, 1)
                    nq += 1
                for pk in range(2):
                    v.wait_ge(s_qp, h * 34 + nq + 1)
                    v.tensor_copy(vsto.rearrange("p (j e) -> p j e", e=65)
                                  [:, pk * 8:(pk + 1) * 8, 0:64],
                                  PRD(48 + h * 34 + nq).rearrange("p (j e) -> p j e", e=64)) \
                        .then_inc(s_qv, 1)
                    nq += 1
                for it in range(4):
                    hit = h * 4 + it
                    for d in range(4):
                        Jo = it * 4 + d
                        nst_d = hit * 40 + 32 + Jo // 2
                        v.wait_ge(s_st, nst_d + 1)
                        if hit >= 1 and d < 2:
                            v.wait_ge(s_pt, hit * 40)
                        if d >= 2:
                            v.wait_ge(s_pt, hit * 40 + 32 + (it * 4 + d - 2) // 2 + 1)
                        v.tensor_tensor(stile[:, (d % 2) * 512:(d % 2) * 512 + 512],
                                        STp(nst_d)[:, (Jo % 2) * 512:(Jo % 2) * 512 + 512],
                                        corr[:, d * 512:(d + 1) * 512], ALU.mult) \
                            .then_inc(s_cr, 1)
                    v.wait_ge(s_pv, hit + 1)
                    if hit >= 2:
                        v.wait_ge(s_bd1[hit % 2], 16 * (hit // 2))
                    VF(v.tensor_copy(drow[:, (hit % 2) * 512:(hit % 2) * 512 + 512],
                                     PV(it)[64:65, :]))
                    v.reciprocal(drow[:, (hit % 2) * 512:(hit % 2) * 512 + 512],
                                 drow[:, (hit % 2) * 512:(hit % 2) * 512 + 512]) \
                        .then_inc(s_rd, 1)
                    v.wait_ge(s_bd2[hit % 2], 16 * (hit // 2 + 1))
                    if hit >= 2:
                        v.wait_ge(s_ost[hit % 2], 16 * (hit // 2))
                    v.tensor_tensor(s1t[:, (hit % 2) * 512:(hit % 2) * 512 + 512],
                                    PV(it)[0:64, :],
                                    bc64a[0:64, (hit % 2) * 512:(hit % 2) * 512 + 512],
                                    ALU.mult).then_inc(s_sv, 1)

def build_l1_attn():
    nc = bass.Bass("TRN2", target_bir_lowering=False, debug=False)
    DD = _decl_l1_attn(nc)
    emit_l1_attn(nc, DD)
    return nc



def _decl_l1_ffn(nc):
    DD = {}
    DD["xn1"] = nc.dram_tensor("xn1", [D, TC], BF16, kind="ExternalInput").ap()
    DD["wgt"] = nc.dram_tensor("wgt", [D, HID], I32, kind="ExternalInput").ap()
    DD["wut"] = nc.dram_tensor("wut", [D, HID], I32, kind="ExternalInput").ap()
    DD["wdt"] = nc.dram_tensor("wdt", [HID, D], I32, kind="ExternalInput").ap()
    DD["s1"] = nc.dram_tensor("s1", [D, TC], BF16, kind="ExternalInput").ap()
    DD["wo"] = nc.dram_tensor("wo", [D, D], F32, kind="ExternalInput").ap()
    DD["x1"] = nc.dram_tensor("x1", [D, TC], F32, kind="ExternalInput").ap()
    DD["x2"] = nc.dram_tensor("x2", [D, TC], F32, kind="ExternalOutput").ap()
    DD["h_scr"] = nc.dram_tensor("h_scr", [HID, TC], BF16)

    return DD

def emit_l1_ffn(nc, DD):
    NHC = HID // 128
    QS = float(1.0 / 127.0)
    xn_d = DD["xn1"]
    wgt_d = DD["wgt"]
    wut_d = DD["wut"]
    wdt_d = DD["wdt"]
    s1_d = DD["s1"]
    wo_d = DD["wo"]
    x1_d = DD["x1"]
    x2_d = DD["x2"]
    h_d = DD["h_scr"]
    with ExitStack() as ctx:
        ent = ctx.enter_context
        xnsb = ent(nc.sbuf_tensor("xnsb", [128, NCH * TC], BF16))
        wo_sb = ent(nc.sbuf_tensor("wo_sb", [128, NCH * D], BF16))
        wgi = ent(nc.sbuf_tensor("wgi", [128, 3 * NCH * 128], I32))
        wui = ent(nc.sbuf_tensor("wui", [128, 3 * NCH * 128], I32))
        wgb = ent(nc.sbuf_tensor("wgb", [128, 3 * NCH * 128], BF16))
        wub = ent(nc.sbuf_tensor("wub", [128, 3 * NCH * 128], BF16))
        _gs_off = nc.sbuf_base
        gsil = ent(nc.sbuf_tensor("gsil", [128, 2 * 1024], BF16))
        hstg = ent(nc.sbuf_tensor("hstg", [128, 1024], BF16))
        wdi = ent(nc.sbuf_tensor("wdi", [128, 2 * NHC * 128], I32))
        wdb = ent(nc.sbuf_tensor("wdb", [128, 2 * NHC * 128], BF16))
        htl = ent(nc.sbuf_tensor("htl", [128, 2 * NHC * 512], BF16))
        s1t = ent(nc.sbuf_tensor("s1t", [128, 2 * NCH * 512], BF16))
        x1t = ent(nc.sbuf_tensor("x1t", [128, NCH * 512], F32))
        g1t = ent(nc.sbuf_tensor("g1t", [128, 512], BF16))
        t1s = nc.alloc_sbuf_tensor_at("t1s", [128, 512], F32, offset=_gs_off)
        ps = ent(nc.psum_tensor("ps", [128, 4096], F32))
        s_init = ent(nc.semaphore("s_init"))
        s_vfl = ent(nc.semaphore("s_vfl"))
        s_wg = [ent(nc.semaphore(f"s_wg{i}")) for i in range(3)]
        s_wdq = ent(nc.semaphore("s_wdq"))
        s_gu = ent(nc.semaphore("s_gu"))
        s_sil = ent(nc.semaphore("s_sil"))
        s_hv = ent(nc.semaphore("s_hv"))
        s_hst = [ent(nc.semaphore(f"s_hst{i}")) for i in range(2)]
        s_hld = [ent(nc.semaphore(f"s_hld{i}")) for i in range(2)]
        s_sx = [ent(nc.semaphore(f"s_sx{i}")) for i in range(2)]
        s_wd = [ent(nc.semaphore(f"s_wd{i}")) for i in range(2)]
        s_wdb = ent(nc.semaphore("s_wdb"))
        s_dn = ent(nc.semaphore("s_dn"))
        s_wo = ent(nc.semaphore("s_wo"))
        s_g1 = ent(nc.semaphore("s_g1"))
        s_x2 = ent(nc.semaphore("s_x2"))
        s_ost = [ent(nc.semaphore(f"s_ost{i}")) for i in range(2)]
        block = ent(nc.Block())

        GPS = lambda r: ps[:, (r % 2) * 1024:(r % 2) * 1024 + 1024]
        UPS = lambda r: ps[:, 2048 + (r % 2) * 1024: 2048 + (r % 2) * 1024 + 1024]
        DPS = lambda r: ps[:, (r % 2) * 512:(r % 2) * 512 + 512]
        WPS = lambda r: ps[:, 1024 + (r % 2) * 512: 1024 + (r % 2) * 512 + 512]

        @block.gpsimd
        def _(g_):
            g_.dma_start(out=wo_sb.rearrange("p (c f) -> p c f", f=D),
                         in_=wo_d.rearrange("(c p) f -> p c f", p=128)).then_inc(s_init, 16)
            for tt in range(4):
                sl = tt % 2
                if tt >= 2:
                    g_.wait_ge(s_x2, 8 * (tt - 1))
                g_.wait_ge(s_hst[0], 16 * 2 * NHC)
                g_.dma_start(out=htl[:, sl * NHC * 512:(sl + 1) * NHC * 512]
                             .rearrange("p (c t) -> p c t", t=512),
                             in_=h_d.rearrange("(c p) t -> p c t", p=128)
                             [:, :, tt * 512:(tt + 1) * 512]).then_inc(s_hld[sl], 16)

        @block.sync
        def _(sy):
            sy.dma_start(out=xnsb.rearrange("p (c t) -> p c t", t=TC),
                         in_=xn_d.rearrange("(c p) t -> p c t", p=128)).then_inc(s_init, 16)
            for k in range(2 * NHC):
                th, hc = k // NHC, k % NHC
                sl = k % 3
                if k >= 3:
                    sy.wait_ge(s_wdq, 2 * (k - 2))
                sy.dma_start(out=wgi[:, sl * 1024:(sl + 1) * 1024]
                             .rearrange("p (c f) -> p c f", f=128),
                             in_=wgt_d[:, hc * 128:(hc + 1) * 128]
                             .rearrange("(c p) f -> p c f", p=128)).then_inc(s_wg[sl], 16)
                sy.dma_start(out=wui[:, sl * 1024:(sl + 1) * 1024]
                             .rearrange("p (c f) -> p c f", f=128),
                             in_=wut_d[:, hc * 128:(hc + 1) * 128]
                             .rearrange("(c p) f -> p c f", p=128)).then_inc(s_wg[sl], 16)
                sy.wait_ge(s_hv, 2 * k + 2)
                sy.dma_start(out=h_d[hc * 128:(hc + 1) * 128, th * 1024:(th + 1) * 1024],
                             in_=hstg[:, :]).then_inc(s_hst[0], 16)
            for tt in range(4):
                sl = tt % 2
                if tt >= 2:
                    sy.wait_ge(s_x2, 8 * (tt - 1))
                sy.dma_start(out=s1t[:, sl * NCH * 512:(sl + 1) * NCH * 512]
                             .rearrange("p (c t) -> p c t", t=512),
                             in_=s1_d.rearrange("(c p) t -> p c t", p=128)
                             [:, :, tt * 512:(tt + 1) * 512]).then_inc(s_sx[sl], 16)
                if tt >= 1:
                    sy.wait_ge(s_ost[0], 16 * tt)
                sy.dma_start(out=x1t.rearrange("p (c t) -> p c t", t=512),
                             in_=x1_d.rearrange("(c p) t -> p c t", p=128)
                             [:, :, tt * 512:(tt + 1) * 512]).then_inc(s_sx[sl], 16)
                for oc in range(NCH):
                    n = tt * NCH + oc
                    so = n % 2
                    if n >= 2:
                        sy.wait_ge(s_wdb, n - 1)
                    sy.dma_start(out=wdi[:, so * NHC * 128:(so + 1) * NHC * 128]
                                 .rearrange("p (c f) -> p c f", f=128),
                                 in_=wdt_d[:, oc * 128:(oc + 1) * 128]
                                 .rearrange("(c p) f -> p c f", p=128)).then_inc(s_wd[so], 16)
                sy.wait_ge(s_x2, 8 * (tt + 1))
                sy.dma_start(out=x2_d.rearrange("(c p) t -> p c t", p=128)
                             [:, :, tt * 512:(tt + 1) * 512],
                             in_=x1t.rearrange("p (c t) -> p c t", t=512)) \
                    .then_inc(s_ost[0], 16)
            sy.wait_ge(s_ost[0], 64)

        @block.tensor
        def _(t):
            t.wait_ge(s_init, 32)
            for k in range(2 * NHC):
                th, hc = k // NHC, k % NHC
                sl = k % 3
                t.wait_ge(s_wdq, 2 * k + 2)
                if k >= 2:
                    t.wait_ge(s_hv, 2 * (k - 1))
                for half, PSL in ((0, GPS), (1, UPS)):
                    wsb = wgb if half == 0 else wub
                    for tt2 in range(2):
                        t0 = th * 1024 + tt2 * 512
                        for ic in range(NCH):
                            mm = t.matmul(PSL(k)[:, tt2 * 512:(tt2 + 1) * 512],
                                          wsb[:, sl * NCH * 128 + ic * 128: sl * NCH * 128 + (ic + 1) * 128],
                                          xnsb[:, ic * TC + t0: ic * TC + t0 + 512],
                                          start=(ic == 0), stop=(ic == NCH - 1))
                mm.then_inc(s_gu, 1)
            for tt in range(4):
                sl = tt % 2
                t.wait_ge(s_hld[sl], 16 * (tt // 2 + 1))
                t.wait_ge(s_sx[sl], 32 * (tt // 2 + 1))
                for oc in range(NCH):
                    n = tt * NCH + oc
                    t.wait_ge(s_wdb, n + 1)
                    if n >= 2:
                        t.wait_ge(s_x2, n - 1)
                    for hcc in range(NHC):
                        mm = t.matmul(DPS(n), wdb[:, (n % 2) * NHC * 128 + hcc * 128:
                                                  (n % 2) * NHC * 128 + (hcc + 1) * 128],
                                      htl[:, sl * NHC * 512 + hcc * 512:
                                          sl * NHC * 512 + (hcc + 1) * 512],
                                      start=(hcc == 0), stop=(hcc == NHC - 1))
                    mm.then_inc(s_dn, 1)
                    for ic in range(NCH):
                        mm = t.matmul(WPS(n), wo_sb[:, ic * D + oc * 128: ic * D + oc * 128 + 128],
                                      s1t[:, sl * NCH * 512 + ic * 512: sl * NCH * 512 + (ic + 1) * 512],
                                      start=(ic == 0), stop=(ic == NCH - 1))
                    mm.then_inc(s_wo, 1)

        @block.scalar
        def _(s):
            for k in range(2 * NHC):
                s.wait_ge(s_gu, k + 1)
                s.activation(gsil[:, (k % 2) * 1024:(k % 2) * 1024 + 1024], GPS(k),
                             AF.Silu).then_inc(s_sil, 1)
            for n in range(32):
                s.wait_ge(s_dn, n + 1)
                if n >= 1:
                    s.wait_ge(s_x2, n)
                s.activation(g1t[:, :], DPS(n), AF.Silu).then_inc(s_g1, 1)

        @block.vector
        def _(v):
            vfl = [0]
            def VF(inst):
                vfl[0] += 1
                inst.then_inc(s_vfl, 1)
                v.wait_ge(s_vfl, vfl[0])
                return inst
            for k in range(2 * NHC):
                sl = k % 3
                v.wait_ge(s_wg[sl], 32 * (k // 3 + 1))
                v.tensor_scalar(wgb[:, sl * 1024:(sl + 1) * 1024],
                                wgi[:, sl * 1024:(sl + 1) * 1024],
                                QS, None, ALU.mult).then_inc(s_wdq, 1)
                v.tensor_scalar(wub[:, sl * 1024:(sl + 1) * 1024],
                                wui[:, sl * 1024:(sl + 1) * 1024],
                                QS, None, ALU.mult).then_inc(s_wdq, 1)
                v.wait_ge(s_sil, k + 1)
                if k >= 1:
                    v.wait_ge(s_hst[0], 16 * k)
                v.tensor_tensor(hstg[:, :], UPS(k),
                                gsil[:, (k % 2) * 1024:(k % 2) * 1024 + 1024],
                                ALU.mult).then_inc(s_hv, 2)
            for tt in range(4):
                sl = tt % 2
                for oc in range(NCH):
                    n = tt * NCH + oc
                    so = n % 2
                    v.wait_ge(s_wd[so], 16 * (n // 2 + 1))
                    v.tensor_scalar(wdb[:, so * NHC * 128:(so + 1) * NHC * 128],
                                    wdi[:, so * NHC * 128:(so + 1) * NHC * 128],
                                    QS, None, ALU.mult).then_inc(s_wdb, 1)
                    v.wait_ge(s_g1, n + 1)
                    v.wait_ge(s_wo, n + 1)
                    VF(v.tensor_tensor(t1s[:, :], WPS(n), g1t[:, :], ALU.mult))
                    v.tensor_tensor(x1t[:, oc * 512:(oc + 1) * 512],
                                    x1t[:, oc * 512:(oc + 1) * 512],
                                    t1s[:, :], ALU.add).then_inc(s_x2, 1)

def build_l1_ffn():
    nc = bass.Bass("TRN2", target_bir_lowering=False, debug=False)
    DD = _decl_l1_ffn(nc)
    emit_l1_ffn(nc, DD)
    return nc




class _NCP:

    def __init__(self, nc, pfx):
        self._nc = nc
        self._p = pfx

    def sbuf_tensor(self, name, shape, dtype, **kw):
        return self._nc.sbuf_tensor(self._p + name, shape, dtype, **kw)

    def psum_tensor(self, name, shape, dtype, **kw):
        return self._nc.psum_tensor(self._p + name, shape, dtype, **kw)

    def semaphore(self, name=None, num=None):
        return self._nc.semaphore(self._p + (name or "s"), num)

    def __getattr__(self, a):
        return getattr(self._nc, a)


def _emit_gather(nc, src_ap, dst_ap, tag):
    RG = [[0, 1, 2, 3], [4, 5, 6, 7]]
    with nc.semaphore(f"s_cc_{tag}") as s_c, nc.Block() as blk:
        @blk.gpsimd
        def _(g_):
            g_.collective_compute("AllGather", ALU.bypass, RG,
                                  [src_ap.tensor.ap().opt()],
                                  [dst_ap.tensor.ap().opt()]).then_inc(s_c, 1)
            g_.wait_ge(s_c, 1)


def build_fused(nstages=10):
    nc = bass.Bass("TRN2", target_bir_lowering=False, debug=False)
    EI, INT = "ExternalInput", "Internal"
    dt = nc.dram_tensor
    x0 = dt("x0", [D, TC], F32, kind=EI).ap()
    w1 = dt("w1", [E, D, FF], F32, kind=EI).ap()
    w2 = dt("w2", [E, FF, D], F32, kind=EI).ap()
    sw1 = dt("sw1", [D, FF], F32, kind=EI).ap()
    sw2 = dt("sw2", [FF, D], F32, kind=EI).ap()
    b1t = dt("b1t", [E, 128, FF // 128], F32, kind=EI).ap()
    sb1t = dt("sb1t", [128, FF // 128], F32, kind=EI).ap()
    b2 = dt("b2", [E, D], F32, kind=EI).ap()
    sb2 = dt("sb2", [1, D], F32, kind=EI).ap()
    gwt = dt("gwt", [D, E], F32, kind=EI).ap()
    gb = dt("gb", [E, 1], F32, kind=EI).ap()
    wqt = dt("wqt", [D, D], F32, kind=EI).ap()
    wkt = dt("wkt", [D, D], F32, kind=EI).ap()
    wvt = dt("wvt", [D, D], F32, kind=EI).ap()
    wot = dt("wot", [D, D], F32, kind=EI).ap()
    pm64 = dt("pm64", [128, 4], F32, kind=EI).ap()
    wk1 = dt("wk", [D, D], F32, kind=EI).ap()
    wv1 = dt("wv", [D, D], F32, kind=EI).ap()
    wdn = dt("wdn", [D, DLAT], F32, kind=EI).ap()
    wq1 = dt("wq", [D, D], F32, kind=EI).ap()
    wupk = dt("wupk", [DLAT, D], F32, kind=EI).ap()
    wupv = dt("wupv", [DLAT, D], F32, kind=EI).ap()
    wo1 = dt("wo", [D, D], F32, kind=EI).ap()
    wgt = dt("wgt", [D, HID], I32, kind=EI).ap()
    wut = dt("wut", [D, HID], I32, kind=EI).ap()
    wdt = dt("wdt", [HID, D], I32, kind=EI).ap()
    posq = dt("posq", [1, TC], F32, kind=EI).ap()
    posk = dt("posk", [1, S], F32, kind=EI).ap()
    amask = dt("amask", [1, S], F32, kind=EI).ap()
    bmask = dt("bmask", [1, S], F32, kind=EI).ap()
    biascol = dt("biascol", [128, S // 128], F32, kind=EI).ap()
    xn = dt("xn", [D, TC], BF16, kind=INT).ap()
    xnf = dt("xnf", [D, TC], F32, kind=INT).ap()
    g0 = dt("g0", [D, TC], BF16, kind=INT).ap()
    qf = dt("qf", [D, TC], BF16, kind=INT).ap()
    kf = dt("kf", [D, TC], BF16, kind=INT).ap()
    ktm = dt("ktm", [TC, D], BF16, kind=INT).ap()
    vtm = dt("vtm", [TC, H * 65], BF16, kind=INT).ap()
    stot = dt("stot", [H * 64, 65], F32, kind=INT).ap()
    stall = dt("stall", [4 * H * 64, 65], F32, kind=INT).ap()
    x1 = dt("x1", [D, TC], F32, kind=INT).ap()
    s_out = dt("s_out", [D, TC], BF16, kind=INT).ap()
    xn1 = dt("xn1", [D, TC], BF16, kind=INT).ap()
    xnf1 = dt("xnf1", [D, TC], F32, kind=INT).ap()
    lat = dt("lat", [DLAT, TC], BF16, kind=INT).ap()
    latg = dt("latg", [4 * DLAT, TC], BF16, kind=INT).ap()
    s1 = dt("s1", [D, TC], BF16, kind=INT).ap()
    h_scr = dt("h_scr", [HID, TC], BF16, kind=INT).ap()
    w1scr = dt("moe_w1scr", [9 * 8, 128, NCH * 512], BF16, kind=INT).ap()
    w2scr = dt("moe_w2scr", [9, 2, 128, 32 * 512], BF16, kind=INT).ap()
    x2 = dt("x2", [D, TC], F32, kind="ExternalOutput").ap()

    _steps = []
    def _st0():
        emit_ln(_NCP(nc, 'g1_'), {"x": x0, "xn": xn, "xnf": xnf})
    _steps.append(_st0)
    def _st1():
        emit_l0_moe(_NCP(nc, 'g2_'), {"xn": xn, "xnf": xnf, "w1": w1, "w2": w2, "sw1": sw1,
                         "sw2": sw2, "b1t": b1t, "sb1t": sb1t, "b2": b2, "sb2": sb2,
                         "gwt": gwt, "gb": gb, "g": g0, "w1scr": w1scr, "w2scr": w2scr})
    _steps.append(_st1)
    def _st2():
        emit_l0_qkv(_NCP(nc, 'g3_'), {"xn": xn, "wqt": wqt, "wkt": wkt, "wvt": wvt, "qf": qf,
                         "kf": kf, "ktm": ktm, "vtm": vtm, "stot": stot})
    _steps.append(_st2)
    def _st3():
        _emit_gather(_NCP(nc, 'g4_'), stot, stall, "st")
    _steps.append(_st3)
    def _st4():
        emit_l0_scan(_NCP(nc, 'g5_'), {"qf": qf, "kf": kf, "ktm": ktm, "vtm": vtm, "stall": stall,
                          "pm64": pm64, "g": g0, "x": x0, "wot": wot, "x1": x1,
                          "s_out": s_out})
    _steps.append(_st4)
    def _st5():
        emit_ln(_NCP(nc, 'g6_'), {"x": x1, "xn": xn1, "xnf": xnf1})
    _steps.append(_st5)
    def _st6():
        emit_l1_lat(_NCP(nc, 'g7_'), {"xn1": xn1, "wk": wk1, "wv": wv1, "wdn": wdn, "lat": lat})
    _steps.append(_st6)
    def _st7():
        _emit_gather(_NCP(nc, 'g8_'), lat, latg, "lat")
    _steps.append(_st7)
    def _st8():
        emit_l1_attn(_NCP(nc, 'g9_'), {"xn1": xn1, "latf": latg, "lato": lat, "wq": wq1,
                          "wupk": wupk, "wupv": wupv, "posq": posq, "posk": posk,
                          "amask": amask, "bmask": bmask, "biascol": biascol,
                          "s1": s1}, gathered=True)
    _steps.append(_st8)
    def _st9():
        emit_l1_ffn(_NCP(nc, 'g10_'), {"xn1": xn1, "wgt": wgt, "wut": wut, "wdt": wdt, "s1": s1,
                         "wo": wo1, "x1": x1, "x2": x2, "h_scr": h_scr})
    _steps.append(_st9)
    for k, st in enumerate(_steps[:nstages]):
        if k:
            nc.reset()
        st()
    return nc


def _cc(a):
    return np.ascontiguousarray(a)


def kernel_multilaunch(**inputs):
    I = {k: np.asarray(v) for k, v in inputs.items()}
    import ml_dtypes
    BF = ml_dtypes.bfloat16
    cores = list(range(8))

    def slc(c):
        return c // 4, (c % 4) * TC

    nc_ln = build_ln()
    x_slices = [_cc(I["L0_attn_x"][c // 4, (c % 4) * TC:(c % 4 + 1) * TC, :].T)
                for c in cores]
    r = run_bass_kernel_spmd(nc_ln, [{"x": x_slices[c]} for c in cores], core_ids=cores)
    xn = [np.asarray(r.results[c]["xn"]) for c in cores]
    xnf = [np.asarray(r.results[c]["xnf"]) for c in cores]

    nc_moe = build_l0_moe()
    W0 = {"w1": I["L0_ffn_w1"], "w2": I["L0_ffn_w2"], "sw1": I["L0_ffn_sw1"],
          "sw2": I["L0_ffn_sw2"], "b1": I["L0_ffn_b1"], "sb1": I["L0_ffn_sb1"],
          "b2": I["L0_ffn_b2"], "sb2": I["L0_ffn_sb2"], "gate_w": I["L0_ffn_gate_w"],
          "gate_b": I["L0_ffn_gate_b"]}
    im = []
    for c in cores:
        m = _host_moe_inputs(xn[c], W0)
        m["xnf"] = xnf[c]
        im.append(m)
    r = run_bass_kernel_spmd(nc_moe, im, core_ids=cores)
    g0 = [np.asarray(r.results[c]["g"]) for c in cores]

    nc_qkv = build_l0_qkv()
    wqt, wkt = _cc(I["L0_attn_Wq"].T), _cc(I["L0_attn_Wk"].T)
    wvt, wot = _cc(I["L0_attn_Wv"].T), _cc(I["L0_attn_Wo"].T)
    r = run_bass_kernel_spmd(nc_qkv, [{"xn": xn[c], "wqt": wqt, "wkt": wkt,
                                       "wvt": wvt} for c in cores], core_ids=cores)
    qkv = [{k: np.asarray(r.results[c][k]) for k in ("qf", "kf", "ktm", "vtm", "stot")}
           for c in cores]

    nc_scan = build_l0_scan()
    im = []
    for c in cores:
        b, qtr = c // 4, c % 4
        stall = _cc(np.concatenate([qkv[4 * b + rr]["stot"] for rr in range(4)], axis=0))
        pm = np.zeros((128, 4), np.float32)
        pm[:, :qtr] = 1.0
        im.append({"qf": qkv[c]["qf"], "kf": qkv[c]["kf"], "ktm": qkv[c]["ktm"],
                   "vtm": qkv[c]["vtm"], "stall": stall, "pm64": pm, "g": g0[c],
                   "x": x_slices[c], "wot": wot})
    r = run_bass_kernel_spmd(nc_scan, im, core_ids=cores)
    x1 = [np.asarray(r.results[c]["x1"]) for c in cores]

    r = run_bass_kernel_spmd(nc_ln, [{"x": x1[c]} for c in cores], core_ids=cores)
    xn1 = [np.asarray(r.results[c]["xn"]) for c in cores]

    nc_lat = build_l1_lat()
    r = run_bass_kernel_spmd(nc_lat, [{"xn1": xn1[c], "wk": I["L1_attn_Wk"],
                                       "wv": I["L1_attn_Wv"],
                                       "wdn": I["L1_attn_W_down_kv"]} for c in cores],
                             core_ids=cores)
    lats = [np.asarray(r.results[c]["lat"]) for c in cores]
    latf = [_cc(np.concatenate([lats[4 * b + rr] for rr in range(4)], axis=1))
            for b in range(2)]

    nc_attn = build_l1_attn()
    im = []
    posk = np.arange(S, dtype=np.float32)[None, :]
    for c in cores:
        b, qtr = c // 4, c % 4
        cs, ce = qtr * TC, (qtr + 1) * TC
        biascol = np.zeros((128, 64), np.float32)
        for J in range(64):
            pp = J * 128 + np.arange(128)
            biascol[(pp >= cs) & (pp < ce), J] = -88.0
        im.append({"xn1": xn1[c], "latf": latf[b], "lato": lats[c],
                   "wq": I["L1_attn_Wq"], "wupk": I["L1_attn_W_up_k"],
                   "wupv": I["L1_attn_W_up_v"],
                   "posq": np.arange(cs, ce, dtype=np.float32)[None, :],
                   "posk": posk, "amask": (posk < cs).astype(np.float32),
                   "bmask": (posk >= ce).astype(np.float32), "biascol": biascol})
    r = run_bass_kernel_spmd(nc_attn, im, core_ids=cores)
    s1 = [np.asarray(r.results[c]["s1"]) for c in cores]

    nc_ffn = build_l1_ffn()
    wgt, wut = _cc(I["L1_ffn_w_gate"].T), _cc(I["L1_ffn_w_up"].T)
    wdt = _cc(I["L1_ffn_w_down"].T)
    r = run_bass_kernel_spmd(nc_ffn, [{"xn1": xn1[c], "wgt": wgt, "wut": wut,
                                       "wdt": wdt, "s1": s1[c], "wo": I["L1_attn_Wo"],
                                       "x1": x1[c]} for c in cores], core_ids=cores)
    x2 = [np.asarray(r.results[c]["x2"]) for c in cores]

    out = np.empty((2, 8192, D), np.float32)
    for c in cores:
        b, qtr = c // 4, c % 4
        out[b, qtr * TC:(qtr + 1) * TC, :] = x2[c].T
    return out


def kernel(**inputs):
    I = {k: np.asarray(v) for k, v in inputs.items()}
    cores = list(range(8))
    nc = build_fused()
    NP = FF // 128
    shared = {
        "w1": I["L0_ffn_w1"], "w2": I["L0_ffn_w2"], "sw1": I["L0_ffn_sw1"],
        "sw2": I["L0_ffn_sw2"],
        "b1t": _cc(I["L0_ffn_b1"].reshape(E, NP, 128).transpose(0, 2, 1)),
        "sb1t": _cc(I["L0_ffn_sb1"].reshape(NP, 128).T),
        "b2": I["L0_ffn_b2"], "sb2": I["L0_ffn_sb2"].reshape(1, D),
        "gwt": _cc(I["L0_ffn_gate_w"].T), "gb": I["L0_ffn_gate_b"].reshape(E, 1),
        "wqt": _cc(I["L0_attn_Wq"].T), "wkt": _cc(I["L0_attn_Wk"].T),
        "wvt": _cc(I["L0_attn_Wv"].T), "wot": _cc(I["L0_attn_Wo"].T),
        "wk": I["L1_attn_Wk"], "wv": I["L1_attn_Wv"], "wdn": I["L1_attn_W_down_kv"],
        "wq": I["L1_attn_Wq"], "wupk": I["L1_attn_W_up_k"], "wupv": I["L1_attn_W_up_v"],
        "wo": I["L1_attn_Wo"],
        "wgt": _cc(I["L1_ffn_w_gate"].T), "wut": _cc(I["L1_ffn_w_up"].T),
        "wdt": _cc(I["L1_ffn_w_down"].T),
        "posk": np.arange(S, dtype=np.float32)[None, :],
    }
    im = []
    for c in cores:
        b, qtr = c // 4, c % 4
        cs, ce = qtr * TC, (qtr + 1) * TC
        pm = np.zeros((128, 4), np.float32)
        pm[:, :qtr] = 1.0
        posk = shared["posk"]
        biascol = np.zeros((128, S // 128), np.float32)
        for J in range(S // 128):
            pp = J * 128 + np.arange(128)
            biascol[(pp >= cs) & (pp < ce), J] = -88.0
        m = dict(shared)
        m["x0"] = _cc(I["L0_attn_x"][b, cs:ce, :].T)
        m["pm64"] = pm
        m["posq"] = np.arange(cs, ce, dtype=np.float32)[None, :]
        m["amask"] = (posk < cs).astype(np.float32)
        m["bmask"] = (posk >= ce).astype(np.float32)
        m["biascol"] = biascol
        im.append(m)
    r = run_bass_kernel_spmd(nc, im, core_ids=cores)
    out = np.empty((2, 8192, D), np.float32)
    for c in cores:
        b, qtr = c // 4, c % 4
        out[b, qtr * TC:(qtr + 1) * TC, :] = np.asarray(r.results[c]["x2"]).T
    return out


# version 11
# speedup vs baseline: 1.0150x; 1.0044x over previous
import numpy as np
from contextlib import ExitStack
import concourse.bass as bass
import concourse.mybir as mybir
from concourse.bass_utils import run_bass_kernel_spmd

F32 = mybir.dt.float32
BF16 = mybir.dt.bfloat16
I32 = mybir.dt.int32
AF = mybir.ActivationFunctionType
ALU = mybir.AluOpType
AX = mybir.AxisListType.X

D = 1024
TC = 2048
S = 8192
B = 2
H = 16
DH = 64
E = 8
FF = 4096
HID = 3072
DLAT = 256
NCH = D // 128



def _decl_l0_moe(nc):
    DD = {}
    NP = FF // 128
    DD["xn"] = nc.dram_tensor("xn", [D, TC], BF16, kind="ExternalInput").ap()
    DD["xnf"] = nc.dram_tensor("xnf", [D, TC], F32, kind="ExternalInput").ap()
    DD["w1"] = nc.dram_tensor("w1", [E, D, FF], F32, kind="ExternalInput").ap()
    DD["w2"] = nc.dram_tensor("w2", [E, FF, D], F32, kind="ExternalInput").ap()
    DD["sw1"] = nc.dram_tensor("sw1", [D, FF], F32, kind="ExternalInput").ap()
    DD["sw2"] = nc.dram_tensor("sw2", [FF, D], F32, kind="ExternalInput").ap()
    DD["b1t"] = nc.dram_tensor("b1t", [E, 128, NP], F32, kind="ExternalInput").ap()
    DD["sb1t"] = nc.dram_tensor("sb1t", [128, NP], F32, kind="ExternalInput").ap()
    DD["b2"] = nc.dram_tensor("b2", [E, D], F32, kind="ExternalInput").ap()
    DD["sb2"] = nc.dram_tensor("sb2", [1, D], F32, kind="ExternalInput").ap()
    DD["gwt"] = nc.dram_tensor("gwt", [D, E], F32, kind="ExternalInput").ap()
    DD["gb"] = nc.dram_tensor("gb", [E, 1], F32, kind="ExternalInput").ap()
    DD["g"] = nc.dram_tensor("g", [D, TC], BF16, kind="ExternalOutput").ap()
    return DD


def emit_l0_moe(nc, DD, n_tt=4, n_e=9):
    TT = 512
    NP = FF // 128
    NMISC = n_e + NCH
    xn_d = DD["xn"]
    xnf_d = DD["xnf"]
    w1_d = DD["w1"]
    w2_d = DD["w2"]
    sw1_d = DD["sw1"]
    sw2_d = DD["sw2"]
    b1t_d = DD["b1t"]
    sb1t_d = DD["sb1t"]
    b2_d = DD["b2"]
    sb2_d = DD["sb2"]
    gwt_d = DD["gwt"]
    gb_d = DD["gb"]
    g_d = DD["g"]
    with ExitStack() as ctx:
        ent = ctx.enter_context
        xnsb = ent(nc.sbuf_tensor("xnsb", [128, 2 * NCH * 512], BF16))
        w1b = ent(nc.sbuf_tensor("w1b", [128, 2 * NCH * 512], BF16))
        w2b = ent(nc.sbuf_tensor("w2b", [128, 3 * 16384], BF16))
        hdn = ent(nc.sbuf_tensor("hdn", [128, NP * TT], BF16))
        _rt_off = nc.sbuf_base
        routed = ent(nc.sbuf_tensor("routed", [128, NCH * TT], F32))
        tmpy = ent(nc.sbuf_tensor("tmpy", [128, TT], F32))
        combbc = ent(nc.sbuf_tensor("combbc", [128, 8 * TT], BF16))
        gstg = ent(nc.sbuf_tensor("gstg", [128, TT], BF16))
        comb9 = ent(nc.sbuf_tensor("comb9", [9, TT], F32))
        scf = ent(nc.sbuf_tensor("scf", [8, n_tt * TT], F32))
        xnf2 = nc.alloc_sbuf_tensor_at("xnf2", [128, TC], F32, offset=_rt_off)
        gwtf = ent(nc.sbuf_tensor("gwtf", [128, NCH * E], F32))
        sctm = ent(nc.sbuf_tensor("sctm", [128, 8], F32))
        m1 = ent(nc.sbuf_tensor("m1", [128, 1], F32))
        m2 = ent(nc.sbuf_tensor("m2", [128, 1], F32))
        srec = ent(nc.sbuf_tensor("srec", [128, 1], F32))
        ns1 = ent(nc.sbuf_tensor("ns1", [128, 1], F32))
        ns2 = ent(nc.sbuf_tensor("ns2", [128, 1], F32))
        eq1 = ent(nc.sbuf_tensor("eq1", [128, 8], F32))
        eq2 = ent(nc.sbuf_tensor("eq2", [128, 8], F32))
        msk = ent(nc.sbuf_tensor("msk", [128, 8], F32))
        combtm = ent(nc.sbuf_tensor("combtm", [128, 8], F32))
        b29 = ent(nc.sbuf_tensor("b29", [9, D], F32))
        b1sb = ent(nc.sbuf_tensor("b1sb", [128, 9 * NP], F32))
        gb = ent(nc.sbuf_tensor("gbsb", [E, 1], F32))
        sel = ent(nc.sbuf_tensor("sel", [9, 9 * 128], F32))
        idf = ent(nc.sbuf_tensor("idf", [128, 128], F32))
        ps = ent(nc.psum_tensor("ps", [128, 4096], F32))
        if "w1scr" in DD:
            w1scr, w2scr = DD["w1scr"], DD["w2scr"]
        else:
            w1scr = nc.dram_tensor("moe_w1scr", [n_e * 8, 128, NCH * 512], BF16,
                                   kind="Internal").ap()
            w2scr = nc.dram_tensor("moe_w2scr", [n_e, 2, 128, 32 * 512], BF16,
                                   kind="Internal").ap()
        s_init = ent(nc.semaphore("s_init"))
        s_w1wr = ent(nc.semaphore("s_w1wr"))
        s_w2wr = ent(nc.semaphore("s_w2wr"))
        s_w1s = [ent(nc.semaphore(f"s_w1s{i}")) for i in range(2)]
        s_w1f = ent(nc.semaphore("s_w1f"))
        s_w2sl = [ent(nc.semaphore(f"s_w2sl{i}")) for i in range(3)]
        s_w2f = ent(nc.semaphore("s_w2f"))
        s_sig = ent(nc.semaphore("s_sig"))
        s_tr = ent(nc.semaphore("s_tr"))
        s_trf = ent(nc.semaphore("s_trf"))
        s_comb = ent(nc.semaphore("s_comb"))
        s_misc = ent(nc.semaphore("s_misc"))
        s_miscf = ent(nc.semaphore("s_miscf"))
        s_hps = ent(nc.semaphore("s_hps"))
        s_gelu = ent(nc.semaphore("s_gelu"))
        s_y = ent(nc.semaphore("s_y"))
        s_yf_hd = ent(nc.semaphore("s_yf_hd"))
        s_yf = ent(nc.semaphore("s_yf"))
        s_silu = ent(nc.semaphore("s_silu"))
        s_gsts = [ent(nc.semaphore(f"s_gsts{i}")) for i in range(2)]
        s_vfl = ent(nc.semaphore("s_vfl"))
        s_gate = ent(nc.semaphore("s_gate"))
        s_xnfs = [ent(nc.semaphore(f"s_xnfs{i}")) for i in range(2)]
        s_xbs = [ent(nc.semaphore(f"s_xbs{i}")) for i in range(2)]
        s_xnff = ent(nc.semaphore("s_xnff"))
        s_gfl = ent(nc.semaphore("s_gfl"))
        block = ent(nc.Block())

        Y = lambda oc: ps[:, oc * 512:(oc + 1) * 512]
        HSL = lambda sl: ps[:, 3072 + sl * 512: 3072 + (sl + 1) * 512] if sl < 2 \
            else ps[:, (sl - 2) * 512: (sl - 1) * 512]
        GATE = ps[0:8, 0:TT]
        TRA = ps[:, 512:520]
        TRB = ps[0:8, 1024:1152]
        MISC = lambda r: ps[:, 1536 + (r % 3) * 512: 1536 + (r % 3 + 1) * 512]

        w2thr = {}
        _cum = [0, 0, 0]
        for _h in range(2 * n_tt * n_e):
            _cum[_h % 3] += 16 if _h < 2 * n_e else 64
            w2thr[_h] = _cum[_h % 3]

        INIT_TOT = 16 * 6 + 4

        @block.gpsimd
        def _(g_):
            g_.memset(idf[:, :], 1.0).then_inc(s_gfl, 1)
            g_.wait_ge(s_gfl, 1)
            g_.affine_select(idf[:, :], idf[:, :], [[-1, 128]], ALU.is_equal, 0.0,
                             channel_multiplier=1).then_inc(s_init, 1)
            g_.memset(sel[:, :], 1.0).then_inc(s_gfl, 1)
            g_.wait_ge(s_gfl, 2)
            g_.affine_select(sel[:, :], sel[:, :], [[1, 9 * 128]], ALU.is_ge, 0.0,
                             channel_multiplier=-128).then_inc(s_gfl, 1)
            g_.wait_ge(s_gfl, 3)
            g_.affine_select(sel[:, :], sel[:, :], [[-1, 9 * 128]], ALU.is_ge, 0.0,
                             base=127, channel_multiplier=128).then_inc(s_init, 1)
            g_.nop().then_inc(s_init, 1)
            g_.memset(comb9[:, :], 0.1).then_inc(s_init, 1)

            for e in range(n_e):
                eidx = e
                for hc in range(8):
                    idx = eidx * 8 + hc
                    slot = idx % 2
                    if idx >= 2:
                        g_.wait_ge(s_w1f, idx - 1)
                        g_.wait_ge(s_w1wr, 16 * idx)
                    dst = w1b[:, slot * NCH * 512:(slot + 1) * NCH * 512] \
                        .rearrange("p (c f) -> p c f", f=512)
                    dst2 = w1b[:, slot * NCH * 512:(slot + 1) * NCH * 512]
                    src = w1_d[e, :, hc * 512:(hc + 1) * 512] if e < E \
                        else sw1_d[:, hc * 512:(hc + 1) * 512]
                    g_.dma_start(out=dst, in_=src.rearrange("(c p) f -> p c f", p=128)
                                 ).then_inc(s_w1s[slot], 16)
                    g_.wait_ge(s_w1s[slot], 16 * (idx // 2 + 1))
                    g_.dma_start(out=w1scr[idx], in_=dst2).then_inc(s_w1wr, 16)
                if eidx >= 1:
                    g_.wait_ge(s_w2wr, 16 * 2 * eidx)
                w2src = w2_d[e] if e < E else sw2_d
                for oh in range(2):
                    h = 2 * eidx + oh
                    sl = h % 3
                    if h >= 3:
                        g_.wait_ge(s_w2f, eidx - 1 + oh)
                    g_.dma_start(
                        out=w2b[:, sl * 16384:(sl + 1) * 16384]
                        .rearrange("p (c f) -> p c f", f=512),
                        in_=w2src.rearrange("(c p) o -> p c o", p=128)
                        [:, :, oh * 512:(oh + 1) * 512]).then_inc(s_w2sl[sl], 16)
                for oh in range(2):
                    h = 2 * eidx + oh
                    sl = h % 3
                    g_.wait_ge(s_w2sl[sl], w2thr[h])
                    g_.dma_start(out=w2scr[e, oh],
                                 in_=w2b[:, sl * 16384:(sl + 1) * 16384]
                                 ).then_inc(s_w2wr, 16)
            g_.wait_ge(s_w1wr, 16 * n_e * 8)
            g_.wait_ge(s_w2wr, 16 * 2 * n_e)
            for tt in range(1, n_tt):
                if 2 <= tt + 1 < n_tt:
                    nb = tt + 1
                    g_.wait_ge(s_w2f, (nb - 1) * n_e)
                    g_.dma_start(out=xnsb[:, (nb % 2) * 4096:(nb % 2 + 1) * 4096]
                                 .rearrange("p (c t) -> p c t", t=512),
                                 in_=xn_d.rearrange("(c p) t -> p c t", p=128)
                                 [:, :, nb * TT:(nb + 1) * TT]).then_inc(s_xbs[nb % 2], 16)
                for e in range(n_e):
                    eidx = tt * n_e + e
                    for hc in range(8):
                        idx = eidx * 8 + hc
                        slot = idx % 2
                        g_.wait_ge(s_w1f, idx - 1)
                        g_.dma_start(out=w1b[:, slot * NCH * 512:(slot + 1) * NCH * 512],
                                     in_=w1scr[e * 8 + hc]).then_inc(s_w1s[slot], 16)
                        oh, q = hc // 4, hc % 4
                        h = 2 * eidx + oh
                        sl = h % 3
                        if q == 0:
                            g_.wait_ge(s_w2f, eidx - 1 + oh)
                        g_.dma_start(
                            out=w2b[:, sl * 16384 + q * 4096: sl * 16384 + (q + 1) * 4096],
                            in_=w2scr[e, oh][:, q * 4096:(q + 1) * 4096]
                            ).then_inc(s_w2sl[sl], 16)

        @block.tensor
        def _(t):
            t.wait_ge(s_init, INIT_TOT)
            for c in range(NCH):
                t.wait_ge(s_xnfs[0], 16 * (c + 1))
                for tt in range(n_tt):
                    mm = t.matmul(ps[0:8, tt * TT:(tt + 1) * TT],
                                  gwtf[:, c * E:(c + 1) * E],
                                  xnf2[:, tt * TT: tt * TT + TT],
                                  start=(c == 0), stop=(c == NCH - 1))
                    if c == NCH - 1:
                        mm.then_inc(s_gate, 1)
                t.drain().then_inc(s_xnff, 1)
            nmisc = 0
            for tt in range(n_tt):
                t0 = tt * TT
                t.wait_ge(s_xbs[tt % 2], 16 * (tt // 2 + 1))
                if tt > 0:
                    t.wait_ge(s_yf, tt * n_e)
                for st in range(4):
                    t.wait_ge(s_sig, n_tt)
                    if tt > 0 or st > 0:
                        t.wait_ge(s_trf, tt * 4 + st)
                    t.transpose(TRA, scf[:, tt * TT + st * 128: tt * TT + (st + 1) * 128], idf[0:8, 0:8]) \
                        .then_inc(s_tr, 1)
                    t.wait_ge(s_comb, tt * 4 + st + 1)
                    t.transpose(TRB, combtm[:, :], idf[:, :]).then_inc(s_tr, 1)
                t.wait_ge(s_trf, (tt + 1) * 4)
                for e in range(n_e):
                    if nmisc >= 3:
                        t.wait_ge(s_miscf, nmisc - 2)
                    t.matmul(MISC(nmisc), sel[:, e * 128:(e + 1) * 128],
                             comb9[:, :], start=True, stop=True).then_inc(s_misc, 1)
                    nmisc += 1
                for oc in range(NCH):
                    if nmisc >= 3:
                        t.wait_ge(s_miscf, nmisc - 2)
                    t.matmul(MISC(nmisc), b29[:, oc * 128:(oc + 1) * 128],
                             comb9[:, :], start=True, stop=True).then_inc(s_misc, 1)
                    nmisc += 1
                for e in range(n_e):
                    eidx = tt * n_e + e
                    for p in range(NP):
                        hc, pl = p // 4, p % 4
                        w1idx = eidx * 8 + hc
                        gidx = eidx * NP + p
                        if pl == 0:
                            t.wait_ge(s_w1s[w1idx % 2], 16 * (w1idx // 2 + 1))
                        if gidx >= 4:
                            t.wait_ge(s_gelu, gidx - 3)
                        if p == 0 and eidx >= 1:
                            t.wait_ge(s_yf_hd, 2 * eidx - 1)
                        if p == 2 and eidx >= 1:
                            t.wait_ge(s_yf_hd, 2 * eidx)
                        if p == 2 and e == 0:
                            t.wait_ge(s_trf, (tt + 1) * 4)
                        base = (w1idx % 2) * NCH * 512 + pl * 128
                        for c in range(NCH):
                            mm = t.matmul(HSL(gidx % 4), w1b[:, base + c * 512: base + c * 512 + 128],
                                          xnsb[:, (tt % 2) * 4096 + c * TT:
                                               (tt % 2) * 4096 + c * TT + TT],
                                          start=(c == 0), stop=(c == NCH - 1))
                        mm.then_inc(s_hps, 1)
                        if pl == 3:
                            t.drain().then_inc(s_w1f, 1)
                    t.wait_ge(s_gelu, eidx * NP + NP)
                    t.wait_ge(s_w2sl[(2 * eidx) % 3], w2thr[2 * eidx])
                    if e == 0:
                        t.wait_ge(s_miscf, (tt + 1) * (n_e + NCH))
                    if eidx >= 1:
                        t.wait_ge(s_yf, eidx)
                    for oc in range(NCH):
                        if oc == 4:
                            hB = 2 * eidx + 1
                            t.wait_ge(s_w2sl[hB % 3], w2thr[hB])
                        sb = ((2 * eidx + oc // 4) % 3) * 16384 + (oc % 4) * 128
                        for p in range(NP):
                            mm = t.matmul(Y(oc), w2b[:, sb + p * 512: sb + p * 512 + 128],
                                          hdn[:, p * TT:(p + 1) * TT],
                                          start=(p == 0), stop=(p == NP - 1))
                        if oc == NCH - 1:
                            mm.then_inc(s_y, 1)
                            t.drain().then_inc(s_w2f, 1)

        @block.scalar
        def _(s):
            s.wait_ge(s_init, INIT_TOT)
            for tt in range(n_tt):
                s.wait_ge(s_gate, tt + 1)
                s.activation(scf[:, tt * TT:(tt + 1) * TT], ps[0:8, tt * TT:(tt + 1) * TT],
                             AF.Sigmoid, bias=gb[:, :]).then_inc(s_sig, 1)
            for tt in range(n_tt):
                for e in range(n_e):
                    eidx = tt * n_e + e
                    for p in range(NP):
                        gidx = eidx * NP + p
                        s.wait_ge(s_hps, gidx + 1)
                        s.activation(hdn[:, p * TT:(p + 1) * TT], HSL(gidx % 4),
                                     AF.Gelu, bias=b1sb[:, e * NP + p: e * NP + p + 1]) \
                            .then_inc(s_gelu, 1)
                s.wait_ge(s_yf, (tt + 1) * n_e)
                for oc in range(NCH):
                    gi = tt * NCH + oc
                    if gi >= 1:
                        s.wait_ge(s_gsts[0], 16 * gi)
                    s.activation(gstg[:, 0:TT],
                                 routed[:, oc * TT:(oc + 1) * TT], AF.Silu) \
                        .then_inc(s_silu, 1)

        @block.vector
        def _(v):
            v.wait_ge(s_init, INIT_TOT)
            nmisc_v = 0
            vfl = [0]
            def VF(inst):
                vfl[0] += 1
                inst.then_inc(s_vfl, 1)
                v.wait_ge(s_vfl, vfl[0])
                return inst
            for tt in range(n_tt):
                for st in range(4):
                    v.wait_ge(s_tr, tt * 8 + 2 * st + 1)
                    VF(v.tensor_copy(sctm[:, :], TRA))
                    VF(v.reduce_max(m1[:, :], sctm[:, :], axis=AX))
                    VF(v.tensor_scalar(eq1[:, :], sctm[:, :], m1[:, :], None, ALU.is_equal))
                    VF(v.scalar_tensor_tensor(msk[:, :], eq1[:, :], -2.0, sctm[:, :],
                                              ALU.mult, ALU.add))
                    VF(v.reduce_max(m2[:, :], msk[:, :], axis=AX))
                    VF(v.tensor_scalar(eq2[:, :], msk[:, :], m2[:, :], None, ALU.is_equal))
                    VF(v.tensor_tensor(srec[:, :], m1[:, :], m2[:, :], ALU.add))
                    VF(v.tensor_scalar(srec[:, :], srec[:, :], 1e-6, None, ALU.add))
                    VF(v.reciprocal(srec[:, :], srec[:, :]))
                    VF(v.tensor_tensor(ns1[:, :], m1[:, :], srec[:, :], ALU.mult))
                    VF(v.tensor_tensor(ns2[:, :], m2[:, :], srec[:, :], ALU.mult))
                    VF(v.tensor_scalar(eq1[:, :], eq1[:, :], ns1[:, :], None, ALU.mult))
                    VF(v.tensor_scalar(eq2[:, :], eq2[:, :], ns2[:, :], None, ALU.mult))
                    v.tensor_tensor(combtm[:, :], eq1[:, :], eq2[:, :], ALU.add) \
                        .then_inc(s_comb, 1)
                    v.wait_ge(s_tr, tt * 8 + 2 * st + 2)
                    v.tensor_copy(comb9[0:8, st * 128:(st + 1) * 128], TRB) \
                        .then_inc(s_trf, 1)
                v.wait_ge(s_silu, NCH * tt)
                for e in range(n_e):
                    v.wait_ge(s_misc, tt * NMISC + e + 1)
                    if e < E:
                        v.tensor_copy(combbc[:, e * TT:(e + 1) * TT], MISC(nmisc_v)) \
                            .then_inc(s_miscf, 1)
                    else:
                        v.nop().then_inc(s_miscf, 1)
                    nmisc_v += 1
                for oc in range(NCH):
                    v.wait_ge(s_misc, tt * NMISC + n_e + oc + 1)
                    v.tensor_copy(routed[:, oc * TT:(oc + 1) * TT], MISC(nmisc_v)) \
                        .then_inc(s_miscf, 1)
                    nmisc_v += 1
                for e in range(n_e):
                    eidx = tt * n_e + e
                    v.wait_ge(s_y, eidx + 1)
                    for k, oc in enumerate([6, 7, 0, 1, 2, 3, 4, 5]):
                        if e < E:
                            VF(v.tensor_tensor(tmpy[:, :], Y(oc),
                                               combbc[:, e * TT:(e + 1) * TT], ALU.mult))
                        else:
                            VF(v.tensor_scalar(tmpy[:, :], Y(oc), 0.1, None, ALU.mult))
                        inst = v.tensor_tensor(routed[:, oc * TT:(oc + 1) * TT],
                                               routed[:, oc * TT:(oc + 1) * TT],
                                               tmpy[:, :], ALU.add)
                        if k in (1, 3):
                            inst.then_inc(s_yf_hd, 1)
                    inst.then_inc(s_yf, 1)

        @block.sync
        def _(sy):
            sy.dma_start(out=b29[0:8, :], in_=b2_d[:, :]).then_inc(s_init, 16)
            sy.dma_start(out=b29[8:9, :], in_=sb2_d[:, :]).then_inc(s_init, 16)
            sy.dma_start(out=b1sb[:, 0:8 * NP],
                         in_=b1t_d.rearrange("e p f -> p e f")).then_inc(s_init, 16)
            sy.dma_start(out=b1sb[:, 8 * NP:9 * NP], in_=sb1t_d[:, :]).then_inc(s_init, 16)
            sy.dma_start(out=gwtf[:, :],
                         in_=gwt_d.rearrange("(c p) e -> p c e", p=128)).then_inc(s_init, 16)
            sy.dma_start(out=gb[:, :], in_=gb_d[:, :]).then_inc(s_init, 16)
            for tt in range(min(2, n_tt)):
                sy.dma_start(out=xnsb[:, tt * 4096:(tt + 1) * 4096]
                             .rearrange("p (c t) -> p c t", t=512),
                             in_=xn_d.rearrange("(c p) t -> p c t", p=128)
                             [:, :, tt * TT:(tt + 1) * TT]).then_inc(s_xbs[tt % 2], 16)
            for c in range(NCH):
                if c >= 1:
                    sy.wait_ge(s_xnff, c)
                sy.dma_start(out=xnf2[:, :],
                             in_=xnf_d[c * 128:(c + 1) * 128, :]).then_inc(s_xnfs[0], 16)
            for tt in range(n_tt):
                for oc in range(NCH):
                    gi = tt * NCH + oc
                    sy.wait_ge(s_silu, gi + 1)
                    sy.dma_start(out=g_d.rearrange("(c p) t -> p c t", p=128)
                                 [:, oc, tt * TT:(tt + 1) * TT],
                                 in_=gstg[:, 0:TT]).then_inc(s_gsts[0], 16)
            sy.wait_ge(s_gsts[0], 16 * n_tt * NCH)


def build_l0_moe(n_tt=4, n_e=9):
    nc = bass.Bass("TRN2", target_bir_lowering=False, debug=False)
    DD = _decl_l0_moe(nc)
    emit_l0_moe(nc, DD, n_tt, n_e)
    return nc


def _host_moe_inputs(xn_fm_bf16, W):
    NP = FF // 128
    return {
        "xn": xn_fm_bf16,
        "w1": W["w1"], "w2": W["w2"], "sw1": W["sw1"], "sw2": W["sw2"],
        "b1t": np.ascontiguousarray(W["b1"].reshape(E, NP, 128).transpose(0, 2, 1)),
        "sb1t": np.ascontiguousarray(W["sb1"].reshape(NP, 128).T),
        "b2": W["b2"], "sb2": W["sb2"].reshape(1, D),
        "gwt": np.ascontiguousarray(W["gate_w"].T), "gb": W["gate_b"].reshape(E, 1),
    }





def _decl_ln(nc):
    DD = {}
    DD["x"] = nc.dram_tensor("x", [D, TC], F32, kind="ExternalInput").ap()
    DD["xn"] = nc.dram_tensor("xn", [D, TC], BF16, kind="ExternalOutput").ap()
    DD["xnf"] = nc.dram_tensor("xnf", [D, TC], F32, kind="ExternalOutput").ap()
    return DD

def emit_ln(nc, DD):
    TT = 512
    NT = TC // TT
    x_d = DD["x"]
    xn_d = DD["xn"]
    xnf_d = DD["xnf"]
    with ExitStack() as ctx:
        ent = ctx.enter_context
        xt = ent(nc.sbuf_tensor("xt", [128, 2 * NCH * TT], F32))
        xsq = ent(nc.sbuf_tensor("xsq", [128, NCH * TT], F32))
        ones = ent(nc.sbuf_tensor("ones", [128, 128], F32))
        mrow = ent(nc.sbuf_tensor("mrow", [1, TT], F32))
        e2row = ent(nc.sbuf_tensor("e2row", [1, TT], F32))
        vrow = ent(nc.sbuf_tensor("vrow", [1, TT], F32))
        srow = ent(nc.sbuf_tensor("srow", [1, TT], F32))
        rrow = ent(nc.sbuf_tensor("rrow", [1, TT], F32))
        mrrow = ent(nc.sbuf_tensor("mrrow", [1, TT], F32))
        xnf_s = ent(nc.sbuf_tensor("xnf_s", [128, 2 * NCH * TT], F32))
        xn_s = ent(nc.sbuf_tensor("xn_s", [128, 2 * NCH * TT], BF16))
        ps = ent(nc.psum_tensor("lnps", [128, 4096], F32))
        s_xs = [ent(nc.semaphore(f"s_xs{i}")) for i in range(2)]
        s_xf = ent(nc.semaphore("s_xf"))
        s_init = ent(nc.semaphore("s_init"))
        s_sq = ent(nc.semaphore("s_sq"))
        s_sum = ent(nc.semaphore("s_sum"))
        s_st = ent(nc.semaphore("s_st"))
        s_sqr = ent(nc.semaphore("s_sqr"))
        s_rr = ent(nc.semaphore("s_rr"))
        s_bc = ent(nc.semaphore("s_bc"))
        s_xn = ent(nc.semaphore("s_xn"))
        s_os = [ent(nc.semaphore(f"s_os{i}")) for i in range(2)]
        s_vfl = ent(nc.semaphore("s_vfl"))
        block = ent(nc.Block())

        def S1(p): return ps[0:1, p * 2048: p * 2048 + TT]
        def S2(p): return ps[0:1, p * 2048 + TT: p * 2048 + 2 * TT]
        def RB(p): return ps[:, p * 2048 + 1024: p * 2048 + 1024 + TT]
        def MRB(p): return ps[:, p * 2048 + 1536: p * 2048 + 1536 + TT]

        @block.gpsimd
        def _(g_):
            g_.memset(ones[:, :], 1.0).then_inc(s_init, 1)

        @block.sync
        def _(sy):
            for i in range(NT):
                if i >= 2:
                    sy.wait_ge(s_xf, i - 1)
                sy.dma_start(out=xt.rearrange("p (s c t) -> p s c t", s=2, t=TT)[:, i % 2],
                             in_=x_d.rearrange("(c p) t -> p c t", p=128)[:, :, i * TT:(i + 1) * TT])                     .then_inc(s_xs[i % 2], 16)
            for i in range(NT):
                for half, sem in ((0, s_os[0]), (1, s_os[1])):
                    sy.wait_ge(s_xn, 2 * i + half + 1)
                    sl = slice((i % 2) * NCH * TT + half * 4 * TT,
                               (i % 2) * NCH * TT + (half + 1) * 4 * TT)
                    off = half * 4
                    sy.dma_start(out=xnf_d.rearrange("(c p) t -> p c t", p=128)
                                 [:, off:off + 4, i * TT:(i + 1) * TT],
                                 in_=xnf_s.rearrange("p (s c t) -> p s c t", s=2, t=TT)
                                 [:, i % 2, off:off + 4]).then_inc(sem, 16)
                    sy.dma_start(out=xn_d.rearrange("(c p) t -> p c t", p=128)
                                 [:, off:off + 4, i * TT:(i + 1) * TT],
                                 in_=xn_s.rearrange("p (s c t) -> p s c t", s=2, t=TT)
                                 [:, i % 2, off:off + 4]).then_inc(sem, 16)

        @block.tensor
        def _(t):
            t.wait_ge(s_init, 1)
            for i in range(NT):
                p = i % 2
                t.wait_ge(s_xs[p], 16 * (i // 2 + 1))
                if i >= 2:
                    t.wait_ge(s_xn, 2 * (i - 1))
                for c in range(NCH):
                    t.matmul(S1(p), ones[:, 0:1],
                             xt[:, p * NCH * TT + c * TT:(p * NCH * TT) + (c + 1) * TT],
                             start=(c == 0), stop=(c == NCH - 1))
                t.wait_ge(s_sq, i * NCH + NCH)
                for c in range(NCH):
                    mm = t.matmul(S2(p), ones[:, 0:1], xsq[:, c * TT:(c + 1) * TT],
                                  start=(c == 0), stop=(c == NCH - 1))
                mm.then_inc(s_sum, 1)
                t.wait_ge(s_rr, i + 1)
                t.matmul(RB(p), ones[0:1, :], rrow[:, :], start=True, stop=True)
                t.matmul(MRB(p), ones[0:1, :], mrrow[:, :], start=True, stop=True)                     .then_inc(s_bc, 1)

        @block.scalar
        def _(s):
            for i in range(NT):
                s.wait_ge(s_st, i + 1)
                s.activation(srow[:, :], vrow[:, :], AF.Sqrt).then_inc(s_sqr, 1)

        @block.vector
        def _(v):
            vfl = [0]
            def VF(inst):
                vfl[0] += 1
                inst.then_inc(s_vfl, 1)
                v.wait_ge(s_vfl, vfl[0])
                return inst
            for i in range(NT):
                p = i % 2
                v.wait_ge(s_xs[p], 16 * (i // 2 + 1))
                for c in range(NCH):
                    src = xt[:, p * NCH * TT + c * TT: p * NCH * TT + (c + 1) * TT]
                    v.tensor_tensor(xsq[:, c * TT:(c + 1) * TT], src, src, ALU.mult)                         .then_inc(s_sq, 1)
                v.wait_ge(s_sum, i + 1)
                VF(v.tensor_scalar(mrow[:, :], S1(p), 1.0 / D, None, ALU.mult))
                VF(v.tensor_scalar(e2row[:, :], S2(p), 1.0 / D, None, ALU.mult))
                VF(v.tensor_tensor(vrow[:, :], mrow[:, :], mrow[:, :], ALU.mult))
                VF(v.tensor_tensor(vrow[:, :], e2row[:, :], vrow[:, :], ALU.subtract))
                v.tensor_scalar(vrow[:, :], vrow[:, :], 1e-5, None, ALU.add) \
                    .then_inc(s_st, 1)
                v.wait_ge(s_sqr, i + 1)
                VF(v.reciprocal(rrow[:, :], srow[:, :]))
                v.tensor_tensor(mrrow[:, :], mrow[:, :], rrow[:, :], ALU.mult) \
                    .then_inc(s_rr, 1)
                v.wait_ge(s_bc, i + 1)
                v.wait_ge(s_os[0], 32 * (i // 2))
                v.wait_ge(s_os[1], 32 * (i // 2))
                for c in range(NCH):
                    src = xt[:, p * NCH * TT + c * TT: p * NCH * TT + (c + 1) * TT]
                    dstf = xnf_s[:, p * NCH * TT + c * TT: p * NCH * TT + (c + 1) * TT]
                    dstb = xn_s[:, p * NCH * TT + c * TT: p * NCH * TT + (c + 1) * TT]
                    VF(v.tensor_tensor(dstf, src, RB(p), ALU.mult))
                    VF(v.tensor_tensor(dstf, dstf, MRB(p), ALU.subtract))
                    inst = v.tensor_copy(dstb, dstf)
                    if c == 3 or c == NCH - 1:
                        inst.then_inc(s_xn, 1)
                v.drain().then_inc(s_xf, 1)

def build_ln():
    nc = bass.Bass("TRN2", target_bir_lowering=False, debug=False)
    DD = _decl_ln(nc)
    emit_ln(nc, DD)
    return nc




def _decl_l0_qkv(nc):
    DD = {}
    DD["xn"] = nc.dram_tensor("xn", [D, TC], BF16, kind="ExternalInput").ap()
    DD["wqt"] = nc.dram_tensor("wqt", [D, D], F32, kind="ExternalInput").ap()
    DD["wkt"] = nc.dram_tensor("wkt", [D, D], F32, kind="ExternalInput").ap()
    DD["wvt"] = nc.dram_tensor("wvt", [D, D], F32, kind="ExternalInput").ap()
    DD["qf"] = nc.dram_tensor("qf", [D, TC], BF16, kind="ExternalOutput").ap()
    DD["kf"] = nc.dram_tensor("kf", [D, TC], BF16, kind="ExternalOutput").ap()
    DD["ktm"] = nc.dram_tensor("ktm", [TC, D], BF16, kind="ExternalOutput").ap()
    DD["vtm"] = nc.dram_tensor("vtm", [TC, H * 65], BF16, kind="ExternalOutput").ap()
    DD["stot"] = nc.dram_tensor("stot", [H * 64, 65], F32, kind="ExternalOutput").ap()

    return DD

def emit_l0_qkv(nc, DD):
    NCT = TC // 128
    xn_d = DD["xn"]
    wqt_d = DD["wqt"]
    wkt_d = DD["wkt"]
    wvt_d = DD["wvt"]
    qf_d = DD["qf"]
    kf_d = DD["kf"]
    ktm_d = DD["ktm"]
    vtm_d = DD["vtm"]
    stot_d = DD["stot"]
    with ExitStack() as ctx:
        ent = ctx.enter_context
        xnsb = ent(nc.sbuf_tensor("xnsb", [128, NCH * TC], BF16))
        wq = ent(nc.sbuf_tensor("wq", [128, NCH * D], BF16))
        wk = ent(nc.sbuf_tensor("wk", [128, NCH * D], BF16))
        wv = ent(nc.sbuf_tensor("wv", [128, NCH * D], BF16))
        kf = ent(nc.sbuf_tensor("kfsb", [128, NCH * TC], BF16))
        el1 = ent(nc.sbuf_tensor("el1", [128, 2 * 512], F32))
        el3 = ent(nc.sbuf_tensor("el3", [128, 2 * 512], F32))
        qstg = ent(nc.sbuf_tensor("qstg", [128, 2 * 512], BF16))
        vstg = ent(nc.sbuf_tensor("vstg", [128, 2 * 1040], BF16))
        kts = ent(nc.sbuf_tensor("kts", [128, 2 * 1024], BF16))
        sts = ent(nc.sbuf_tensor("sts", [64, 16 * 65], F32))
        identb = ent(nc.sbuf_tensor("identb", [128, 128], BF16))
        ps = ent(nc.psum_tensor("ps", [128, 3584], F32))
        psb = ent(nc.psum_tensor("psb", [128, 256], BF16))
        s_init = ent(nc.semaphore("s_init"))
        s_proj = ent(nc.semaphore("s_proj"))
        s_emn = ent(nc.semaphore("s_emn"))
        s_eex = ent(nc.semaphore("s_eex"))
        s_eev = ent(nc.semaphore("s_eev"))
        s_ktr = ent(nc.semaphore("s_ktr"))
        s_ktv = ent(nc.semaphore("s_ktv"))
        s_stm = ent(nc.semaphore("s_stm"))
        s_stv = ent(nc.semaphore("s_stv"))
        s_vfl = ent(nc.semaphore("s_vfl"))
        s_gfl = ent(nc.semaphore("s_gfl"))
        s_qst = [ent(nc.semaphore(f"s_qst{i}")) for i in range(2)]
        s_kst = ent(nc.semaphore("s_kst"))
        s_tst = [ent(nc.semaphore(f"s_tst{i}")) for i in range(2)]
        s_vst = [ent(nc.semaphore(f"s_vst{i}")) for i in range(2)]
        s_sst = ent(nc.semaphore("s_sst"))
        s_cc = ent(nc.semaphore("s_cc"))
        block = ent(nc.Block())

        PRJ = lambda r: ps[:, (r % 3) * 512:(r % 3) * 512 + 512]
        VPS = lambda r: ps[:, (r % 2) * 1024:(r % 2) * 1024 + 1024]
        KTR = psb[:, 0:128]
        SKV = lambda h: ps[0:64, (5 + h // 8) * 512 + (h % 8) * 64:
                           (5 + h // 8) * 512 + (h % 8) * 64 + 64]
        SKS = lambda h: ps[0:64, 4 * 512 + h: 4 * 512 + h + 1]

        @block.gpsimd
        def _(g_):
            for w_sb, w_dd in ((wq, wqt_d), (wk, wkt_d), (wv, wvt_d)):
                g_.dma_start(out=w_sb.rearrange("p (c f) -> p c f", f=D),
                             in_=w_dd.rearrange("(c p) f -> p c f", p=128)).then_inc(s_init, 16)
            g_.memset(identb[:, :], 1.0).then_inc(s_gfl, 1)
            g_.wait_ge(s_gfl, 1)
            g_.affine_select(identb[:, :], identb[:, :], [[-1, 128]], ALU.is_equal, 0.0,
                             channel_multiplier=1).then_inc(s_init, 1)
            if "stall" in DD:
                g_.wait_ge(s_sst, 16)
                g_.collective_compute("AllGather", ALU.bypass, [[0, 1, 2, 3], [4, 5, 6, 7]],
                                      [stot_d.tensor.ap().opt()],
                                      [DD["stall"].tensor.ap().opt()]).then_inc(s_cc, 1)
                g_.wait_ge(s_cc, 1)

        @block.sync
        def _(sy):
            sy.dma_start(out=xnsb.rearrange("p (c t) -> p c t", t=TC),
                         in_=xn_d.rearrange("(c p) t -> p c t", p=128)).then_inc(s_init, 16)
            for i in range(32):
                oc, ti = i // 4, i % 4
                sy.wait_ge(s_eev, i + 1)
                sy.dma_start(out=qf_d[oc * 128:(oc + 1) * 128, ti * 512:(ti + 1) * 512],
                             in_=qstg[:, (i % 2) * 512:(i % 2) * 512 + 512]) \
                    .then_inc(s_qst[i % 2], 16)
            for i in range(32):
                oc, ti = i // 4, i % 4
                sy.wait_ge(s_eev, 33 + i)
                sy.dma_start(out=kf_d[oc * 128:(oc + 1) * 128, ti * 512:(ti + 1) * 512],
                             in_=kf[:, oc * TC + ti * 512: oc * TC + ti * 512 + 512]) \
                    .then_inc(s_kst, 16)
            for ct in range(NCT):
                sy.wait_ge(s_ktv, ct * 8 + 8)
                sy.dma_start(out=ktm_d[ct * 128:(ct + 1) * 128, :],
                             in_=kts[:, (ct % 2) * 1024:(ct % 2) * 1024 + 1024]) \
                    .then_inc(s_tst[ct % 2], 16)
                sy.wait_ge(s_eev, 64 + ct + 1)
                sy.dma_start(out=vtm_d[ct * 128:(ct + 1) * 128, :],
                             in_=vstg[:, (ct % 2) * 1040:(ct % 2) * 1040 + 1040]) \
                    .then_inc(s_vst[ct % 2], 16)
            sy.wait_ge(s_stv, 1)
            sy.dma_start(out=stot_d.rearrange("(h p) e -> p h e", p=64),
                         in_=sts.rearrange("p (h e) -> p h e", e=65)).then_inc(s_sst, 16)
            sy.wait_ge(s_qst[0], 16 * 16)
            sy.wait_ge(s_qst[1], 16 * 16)
            sy.wait_ge(s_kst, 16 * 32)
            sy.wait_ge(s_tst[0], 16 * 8)
            sy.wait_ge(s_tst[1], 16 * 8)
            sy.wait_ge(s_vst[0], 16 * 8)
            sy.wait_ge(s_vst[1], 16 * 8)
            sy.wait_ge(s_sst, 16)

        @block.tensor
        def _(t):
            t.wait_ge(s_init, 65)
            nprj = 0
            for w_sb in (wq, wk):
                for oc in range(NCH):
                    for ti in range(4):
                        if nprj >= 3:
                            t.wait_ge(s_eev, nprj - 2)
                        for ic in range(NCH):
                            mm = t.matmul(PRJ(nprj),
                                          w_sb[:, ic * D + oc * 128: ic * D + oc * 128 + 128],
                                          xnsb[:, ic * TC + ti * 512: ic * TC + ti * 512 + 512],
                                          start=(ic == 0), stop=(ic == NCH - 1))
                        mm.then_inc(s_proj, 1)
                        nprj += 1
            t.wait_ge(s_eev, 64)
            nktr = 0
            for ct in range(NCT):
                if ct >= 2:
                    t.wait_ge(s_eev, 64 + ct - 1)
                for ic in range(NCH):
                    t.matmul(VPS(ct)[:, 0:512],
                             xnsb[:, ic * TC + ct * 128: ic * TC + ct * 128 + 128],
                             wv[:, ic * D: ic * D + 512],
                             start=(ic == 0), stop=(ic == NCH - 1))
                    mm = t.matmul(VPS(ct)[:, 512:1024],
                                  xnsb[:, ic * TC + ct * 128: ic * TC + ct * 128 + 128],
                                  wv[:, ic * D + 512: ic * D + 1024],
                                  start=(ic == 0), stop=(ic == NCH - 1))
                mm.then_inc(s_proj, 1)
                for fc in range(NCH):
                    if nktr >= 1:
                        t.wait_ge(s_ktv, nktr)
                    t.transpose(KTR, kf[:, fc * TC + ct * 128: fc * TC + ct * 128 + 128],
                                identb[:, :]).then_inc(s_ktr, 1)
                    nktr += 1
                t.wait_ge(s_ktv, ct * 8 + 8)
                t.wait_ge(s_eev, 64 + ct + 1)
                for h in range(H):
                    off = (h // 2) * 128 + (h % 2) * 64
                    kslice = kts[:, (ct % 2) * 1024 + off: (ct % 2) * 1024 + off + 64]
                    t.matmul(SKV(h), kslice,
                             vstg[:, (ct % 2) * 1040 + h * 65: (ct % 2) * 1040 + h * 65 + 64],
                             start=(ct == 0 and h % 8 == 0), stop=(ct == NCT - 1))
                    mm = t.matmul(SKS(h), kslice,
                                  vstg[:, (ct % 2) * 1040 + h * 65 + 64:
                                       (ct % 2) * 1040 + h * 65 + 65],
                                  start=(ct == 0 and h == 0), stop=(ct == NCT - 1))
                if ct == NCT - 1:
                    mm.then_inc(s_stm, 1)

        @block.scalar
        def _(s):
            for i in range(64):
                s.wait_ge(s_emn, i + 1)
                s.activation(el1[:, (i % 2) * 512:(i % 2) * 512 + 512],
                             el1[:, (i % 2) * 512:(i % 2) * 512 + 512], AF.Exp) \
                    .then_inc(s_eex, 1)

        @block.vector
        def _(v):
            vfl = [0]
            def VF(inst):
                vfl[0] += 1
                inst.then_inc(s_vfl, 1)
                v.wait_ge(s_vfl, vfl[0])
                return inst
            v.memset(vstg[:, :], 1.0)
            for phase in range(2):
                for i in range(32):
                    sl = i % 2
                    gidx = phase * 32 + i
                    v.wait_ge(s_proj, gidx + 1)
                    v.tensor_scalar_min(el1[:, sl * 512:(sl + 1) * 512], PRJ(gidx), 0.0) \
                        .then_inc(s_emn, 1)
                    VF(v.tensor_scalar_max(el3[:, sl * 512:(sl + 1) * 512], PRJ(gidx), 0.0))
                    v.wait_ge(s_eex, gidx + 1)
                    oc, ti = i // 4, i % 4
                    if phase == 0:
                        if i >= 2:
                            v.wait_ge(s_qst[sl], 16 * (i // 2))
                        dst = qstg[:, sl * 512:(sl + 1) * 512]
                    else:
                        dst = kf[:, oc * TC + ti * 512: oc * TC + ti * 512 + 512]
                    v.tensor_tensor(dst, el1[:, sl * 512:(sl + 1) * 512],
                                    el3[:, sl * 512:(sl + 1) * 512], ALU.add).then_inc(s_eev, 1)
            nktr = 0
            for ct in range(NCT):
                sl = ct % 2
                for fc in range(NCH):
                    v.wait_ge(s_ktr, nktr + 1)
                    if fc == 0 and ct >= 2:
                        v.wait_ge(s_tst[sl], 16 * (ct // 2))
                    v.tensor_copy(kts[:, sl * 1024 + fc * 128: sl * 1024 + fc * 128 + 128],
                                  KTR).then_inc(s_ktv, 1)
                    nktr += 1
                v.wait_ge(s_proj, 64 + ct + 1)
                if ct >= 2:
                    v.wait_ge(s_vst[sl], 16 * (ct // 2))
                v.tensor_copy(vstg.rearrange("p (s h e) -> p s h e", s=2, e=65)[:, sl, :, 0:64],
                              VPS(ct).rearrange("p (h e) -> p h e", e=64)) \
                    .then_inc(s_eev, 1)
            v.wait_ge(s_stm, 1)
            for h in range(H):
                v.tensor_copy(sts[:, h * 65: h * 65 + 64], SKV(h))
                v.tensor_copy(sts[:, h * 65 + 64: h * 65 + 65], SKS(h))
            v.drain().then_inc(s_stv, 1)

def build_l0_qkv():
    nc = bass.Bass("TRN2", target_bir_lowering=False, debug=False)
    DD = _decl_l0_qkv(nc)
    emit_l0_qkv(nc, DD)
    return nc




def _decl_l0_scan(nc):
    DD = {}
    DD["qf"] = nc.dram_tensor("qf", [D, TC], BF16, kind="ExternalInput").ap()
    DD["kf"] = nc.dram_tensor("kf", [D, TC], BF16, kind="ExternalInput").ap()
    DD["ktm"] = nc.dram_tensor("ktm", [TC, D], BF16, kind="ExternalInput").ap()
    DD["vtm"] = nc.dram_tensor("vtm", [TC, H * 65], BF16, kind="ExternalInput").ap()
    DD["stall"] = nc.dram_tensor("stall", [4 * H * 64, 65], F32, kind="ExternalInput").ap()
    DD["pm64"] = nc.dram_tensor("pm64", [128, 4], F32, kind="ExternalInput").ap()
    DD["g"] = nc.dram_tensor("g", [D, TC], BF16, kind="ExternalInput").ap()
    DD["x"] = nc.dram_tensor("x", [D, TC], F32, kind="ExternalInput").ap()
    DD["wot"] = nc.dram_tensor("wot", [D, D], F32, kind="ExternalInput").ap()
    DD["x1"] = nc.dram_tensor("x1", [D, TC], F32, kind="ExternalOutput").ap()
    DD["s_out"] = nc.dram_tensor("s_out", [D, TC], BF16, kind="ExternalOutput").ap()

    return DD

def emit_l0_scan(nc, DD):
    NCT = TC // 128
    HPH = 8
    qf_d = DD["qf"]
    kf_d = DD["kf"]
    ktm_d = DD["ktm"]
    vtm_d = DD["vtm"]
    stall_d = DD["stall"]
    pm_d = DD["pm64"]
    g_d = DD["g"]
    x_d = DD["x"]
    wot_d = DD["wot"]
    x1_d = DD["x1"]
    sout_d = DD["s_out"]
    with ExitStack() as ctx:
        ent = ctx.enter_context
        s_sb = ent(nc.sbuf_tensor("s_sb", [128, NCH * TC], BF16))
        trim = ent(nc.sbuf_tensor("trim", [128, 128], F32))
        pm = ent(nc.sbuf_tensor("pm", [128, 4], F32))
        onesf = ent(nc.sbuf_tensor("onesf", [128, 128], F32))
        wot_sb = ent(nc.sbuf_tensor("wot_sb", [128, NCH * D], BF16))
        s_init = ent(nc.semaphore("s_init"))
        s_ld = [ent(nc.semaphore(f"s_ld{i}")) for i in range(2)]
        s_int = ent(nc.semaphore("s_int"))
        s_pt = ent(nc.semaphore("s_pt"))
        s_o = ent(nc.semaphore("s_o"))
        s_up = ent(nc.semaphore("s_up"))
        s_rd = ent(nc.semaphore("s_rd"))
        s_bdA = [ent(nc.semaphore(f"s_bdA{i}")) for i in range(4)]
        s_bdB = [ent(nc.semaphore(f"s_bdB{i}")) for i in range(4)]
        s_sv = ent(nc.semaphore("s_sv"))
        s_stb = ent(nc.semaphore("s_stb"))
        s_bas = ent(nc.semaphore("s_bas"))
        s_vfl = ent(nc.semaphore("s_vfl"))
        s_gfl = ent(nc.semaphore("s_gfl"))
        s_wo = ent(nc.semaphore("s_wo"))
        s_x1 = ent(nc.semaphore("s_x1"))
        s_gx = [ent(nc.semaphore(f"s_gx{i}")) for i in range(2)]
        s_ost = [ent(nc.semaphore(f"s_ost{i}")) for i in range(2)]
        s_sst = ent(nc.semaphore("s_sst"))

        vfl = [0]
        with ExitStack() as ctx1:
            en1 = ctx1.enter_context
            qfh = en1(nc.sbuf_tensor("qfh", [128, 4 * TC], BF16))
            kfh = en1(nc.sbuf_tensor("kfh", [128, 4 * TC], BF16))
            ktmh = en1(nc.sbuf_tensor("ktmh", [128, NCT * 512], BF16))
            vtmh = en1(nc.sbuf_tensor("vtmh", [128, NCT * 520], BF16))
            sgth = en1(nc.sbuf_tensor("sgth", [128, 4 * (HPH // 2) * 65], F32))
            stF = en1(nc.sbuf_tensor("stF", [128, (HPH // 2) * 65], F32))
            stB = en1(nc.sbuf_tensor("stB", [128, (HPH // 2) * 65], BF16))
            ptile = en1(nc.sbuf_tensor("ptile", [128, 4 * 128], BF16))
            drow = en1(nc.sbuf_tensor("drow", [1, 4 * 128], F32))
            bc64 = en1(nc.sbuf_tensor("bc64", [128, 4 * 128], F32))
            ps = en1(nc.psum_tensor("ps", [128, 4096], F32))
            blk = en1(nc.Block())

            ST = lambda r: ps[:, (r % 2) * 512:(r % 2) * 512 + 128]
            OO = lambda r, b: ps[b:b + 64, (2 + r % 6) * 512:(2 + r % 6) * 512 + 128]
            DEN = lambda r: ps[0:1, (2 + r % 6) * 512 + 128:(2 + r % 6) * 512 + 256]
            UP = lambda r, b: ps[b:b + 64, (2 + r % 6) * 512 + 256:(2 + r % 6) * 512 + 321]
            drow_dr = nc.dram_tensor("scan_drow_dr", [4, 128], F32, kind="Internal").ap()

            @blk.gpsimd
            def _(g_):
                g_.memset(trim[:, :], 1.0).then_inc(s_gfl, 1)
                g_.wait_ge(s_gfl, 1)
                g_.affine_select(trim[:, :], trim[:, :], [[1, 128]], ALU.is_ge, 0.0,
                                 channel_multiplier=-1).then_inc(s_init, 1)
                g_.memset(onesf[:, :], 1.0).then_inc(s_init, 1)
                g_.dma_start(out=wot_sb.rearrange("p (c f) -> p c f", f=D),
                             in_=wot_d.rearrange("(c p) f -> p c f", p=128)).then_inc(s_init, 16)

            @blk.sync
            def _(sy):
                sy.dma_start(out=pm[:, :], in_=pm_d[:, :]).then_inc(s_init, 16)
                for hf in range(2):
                    if hf == 1:
                        sy.wait_ge(s_stb, 136)
                        sy.wait_ge(s_sv, 128)
                    r0 = hf * 512
                    sy.dma_start(out=qfh.rearrange("p (c t) -> p c t", t=TC),
                                 in_=qf_d.rearrange("(c p) t -> p c t", p=128)[:, 4 * hf:4 * hf + 4]) \
                        .then_inc(s_ld[hf], 16)
                    sy.dma_start(out=kfh.rearrange("p (c t) -> p c t", t=TC),
                                 in_=kf_d.rearrange("(c p) t -> p c t", p=128)[:, 4 * hf:4 * hf + 4]) \
                        .then_inc(s_ld[hf], 16)
                    sy.dma_start(out=ktmh.rearrange("p (c t) -> p c t", t=512),
                                 in_=ktm_d[:, r0:r0 + 512].rearrange("(c p) t -> p c t", p=128)) \
                        .then_inc(s_ld[hf], 16)
                    sy.dma_start(out=vtmh.rearrange("p (c t) -> p c t", t=520),
                                 in_=vtm_d[:, hf * 520:(hf + 1) * 520]
                                 .rearrange("(c p) t -> p c t", p=128)).then_inc(s_ld[hf], 16)
                    for r in range(4):
                        for par in range(2):
                            sy.dma_start(
                                out=sgth.rearrange("p (r c e) -> p r c e", r=4, e=65)
                                [par * 64:(par + 1) * 64, r],
                                in_=stall_d.rearrange("(r h p) e -> r p h e", h=H, p=64)
                                [r, :, 8 * hf + par:8 * hf + 8:2, :]).then_inc(s_ld[hf], 16)
                    for lit in range(128):
                        it = hf * 128 + lit
                        sy.wait_ge(s_rd, it + 1)
                        if it >= 4:
                            sy.wait_ge(s_bdB[it % 4], 16 * (it // 4))
                        sy.dma_start(out=drow_dr[it % 4:it % 4 + 1, :],
                                     in_=drow[:, (it % 4) * 128:(it % 4) * 128 + 128]) \
                            .then_inc(s_bdA[it % 4], 16)
                        sy.wait_ge(s_bdA[it % 4], 16 * (it // 4 + 1))
                        if it >= 4:
                            sy.wait_ge(s_sv, it - 3)
                        b = (lit % 8 % 2) * 64
                        sy.dma_start(out=bc64[b:b + 64, (it % 4) * 128:(it % 4) * 128 + 128],
                                     in_=bass.AP(drow_dr.tensor, (it % 4) * 128,
                                                 [[0, 64], [1, 128]])) \
                            .then_inc(s_bdB[it % 4], 16)
                sy.wait_ge(s_sv, 256)
                sy.dma_start(out=sout_d.rearrange("(c p) t -> p c t", p=128),
                             in_=s_sb.rearrange("p (c t) -> p c t", t=TC)).then_inc(s_sst, 16)
                sy.wait_ge(s_sst, 16)

            @blk.tensor
            def _(t):
                t.wait_ge(s_init, 18)
                for hf in range(2):
                    t.wait_ge(s_ld[hf], 16 * 9)
                    t.wait_ge(s_bas, hf + 1)

                    def hb(lit):
                        it = hf * 128 + lit
                        ct, h = lit // 8, lit % 8
                        return (it, ct, h, (h % 2) * 64, (h // 2) * 65,
                                (h // 2) * TC + ct * 128,
                                slice((h % 2) * 64, (h % 2) * 64 + 64),
                                (h // 2) * 128 + (h % 2) * 64)

                    def em_st(lit):
                        it, ct, h, b, cg, hq, rw, off = hb(lit)
                        if lit >= 2:
                            t.wait_ge(s_pt, it - 1)
                        t.matmul(ST(it), kfh[rw, hq:hq + 128], qfh[rw, hq:hq + 128],
                                 start=True, stop=True).then_inc(s_int, 1)

                    def em_mid(lit):
                        it, ct, h, b, cg, hq, rw, off = hb(lit)
                        t.wait_ge(s_pt, it + 1)
                        t.wait_ge(s_stb, hf * 136 + lit + 1 if lit < 8 else hf * 136 + 8 + lit - 7)
                        if lit >= 6:
                            t.wait_ge(s_sv, it - 5)
                            t.wait_ge(s_stb, hf * 136 + 8 + lit - 5)
                        t.matmul(OO(it, b), vtmh[:, ct * 520 + h * 65: ct * 520 + h * 65 + 64],
                                 ptile[:, (it % 4) * 128:(it % 4) * 128 + 128],
                                 start=True, stop=False, tile_position=(0, b))
                        t.matmul(OO(it, b), stB[b:b + 64, cg:cg + 64], qfh[rw, hq:hq + 128],
                                 start=False, stop=True, tile_position=(b, b))
                        t.matmul(DEN(it), vtmh[:, ct * 520 + h * 65 + 64: ct * 520 + h * 65 + 65],
                                 ptile[:, (it % 4) * 128:(it % 4) * 128 + 128],
                                 start=True, stop=False, tile_position=(0, 0))
                        t.matmul(DEN(it), stB[b:b + 64, cg + 64:cg + 65], qfh[rw, hq:hq + 128],
                                 start=False, stop=True, tile_position=(b, 0)).then_inc(s_o, 1)
                        t.matmul(UP(it, b), ktmh[:, ct * 512 + off: ct * 512 + off + 64],
                                 vtmh[:, ct * 520 + h * 65: ct * 520 + (h + 1) * 65],
                                 start=True, stop=True, tile_position=(0, b)).then_inc(s_up, 1)

                    for lit in range(130):
                        if lit < 128:
                            em_st(lit)
                        if lit >= 2:
                            em_mid(lit - 2)

            @blk.vector
            def _(v):
                def VF(inst):
                    vfl[0] += 1
                    inst.then_inc(s_vfl, 1)
                    v.wait_ge(s_vfl, vfl[0])
                    return inst
                for hf in range(2):
                    v.wait_ge(s_ld[hf], 16 * 9)
                    VF(v.memset(stF[:, :], 0.0))
                    for r in range(4):
                        VF(v.scalar_tensor_tensor(
                            stF[:, :],
                            sgth[:, r * (HPH // 2) * 65:(r + 1) * (HPH // 2) * 65],
                            pm[:, r:r + 1], stF[:, :], ALU.mult, ALU.add))
                    for h in range(HPH):
                        b = (h % 2) * 64
                        cg = (h // 2) * 65
                        v.tensor_copy(stB[b:b + 64, cg:cg + 65],
                                      stF[b:b + 64, cg:cg + 65]).then_inc(s_stb, 1)
                    v.drain().then_inc(s_bas, 1)

                    def vb(lit):
                        it = hf * 128 + lit
                        ct, h = lit // 8, lit % 8
                        return it, ct, h, (h % 2) * 64, (h // 2) * 65

                    def vm_pt(lit):
                        it, ct, h, b, cg = vb(lit)
                        v.wait_ge(s_int, it + 1)
                        v.tensor_tensor(ptile[:, (it % 4) * 128:(it % 4) * 128 + 128],
                                        ST(it), trim[:, :], ALU.mult).then_inc(s_pt, 1)

                    def vm_ep(lit):
                        it, ct, h, b, cg = vb(lit)
                        v.wait_ge(s_o, it + 1)
                        if it >= 4:
                            v.wait_ge(s_bdA[it % 4], 16 * (it // 4))
                        VF(v.tensor_scalar(drow[:, (it % 4) * 128:(it % 4) * 128 + 128],
                                           DEN(it), 1e-8, None, ALU.add))
                        v.reciprocal(drow[:, (it % 4) * 128:(it % 4) * 128 + 128],
                                     drow[:, (it % 4) * 128:(it % 4) * 128 + 128]) \
                            .then_inc(s_rd, 1)
                        v.wait_ge(s_up, it + 1)
                        VF(v.tensor_tensor(stF[b:b + 64, cg:cg + 65],
                                           stF[b:b + 64, cg:cg + 65], UP(it, b), ALU.add))
                        v.tensor_copy(stB[b:b + 64, cg:cg + 65],
                                      stF[b:b + 64, cg:cg + 65]).then_inc(s_stb, 1)

                    def vm_so(lit):
                        it, ct, h, b, cg = vb(lit)
                        v.wait_ge(s_bdB[it % 4], 16 * (it // 4 + 1))
                        v.tensor_tensor(s_sb[b:b + 64, (4 * hf + h // 2) * TC + ct * 128:
                                             (4 * hf + h // 2) * TC + ct * 128 + 128],
                                        OO(it, b),
                                        bc64[b:b + 64, (it % 4) * 128:(it % 4) * 128 + 128],
                                        ALU.mult).then_inc(s_sv, 1)

                    for lit in range(134):
                        if lit < 128:
                            vm_pt(lit)
                        if 2 <= lit < 130:
                            vm_ep(lit - 2)
                        if lit >= 6:
                            vm_so(lit - 6)

        with ExitStack() as ctx2:
            en2 = ctx2.enter_context
            gtile = en2(nc.sbuf_tensor("gtile", [128, 2 * NCH * 512], BF16))
            xtile = en2(nc.sbuf_tensor("xtile", [128, 2 * NCH * 512], F32))
            x1stg = en2(nc.sbuf_tensor("x1stg", [128, 2 * NCH * 512], F32))
            t1s = en2(nc.sbuf_tensor("t1s", [128, 512], F32))
            ps2 = en2(nc.psum_tensor("ps2", [128, 2048], F32))
            blk2 = en2(nc.Block())
            WOPS = lambda r: ps2[:, (r % 4) * 512:(r % 4) * 512 + 512]

            @blk2.sync
            def _(sy):
                for tt in range(4):
                    sl = tt % 2
                    if tt >= 2:
                        sy.wait_ge(s_x1, 8 * (tt - 1))
                    sy.dma_start(out=xtile.rearrange("p (s c t) -> p s c t", s=2, t=512)[:, sl],
                                 in_=x_d.rearrange("(c p) t -> p c t", p=128)
                                 [:, :, tt * 512:(tt + 1) * 512]).then_inc(s_gx[sl], 16)
                for tt in range(4):
                    sl = tt % 2
                    sy.wait_ge(s_x1, 8 * (tt + 1))
                    sy.dma_start(out=x1_d.rearrange("(c p) t -> p c t", p=128)
                                 [:, :, tt * 512:(tt + 1) * 512],
                                 in_=x1stg.rearrange("p (s c t) -> p s c t", s=2, t=512)[:, sl]) \
                        .then_inc(s_ost[sl], 16)
                sy.wait_ge(s_ost[0], 32)
                sy.wait_ge(s_ost[1], 32)

            @blk2.gpsimd
            def _(g_):
                for tt in range(4):
                    sl = tt % 2
                    if tt >= 2:
                        g_.wait_ge(s_x1, 8 * (tt - 1))
                    g_.dma_start(out=gtile.rearrange("p (s c t) -> p s c t", s=2, t=512)[:, sl],
                                 in_=g_d.rearrange("(c p) t -> p c t", p=128)
                                 [:, :, tt * 512:(tt + 1) * 512]).then_inc(s_gx[sl], 16)

            @blk2.tensor
            def _(t):
                nwo = 0
                for tt in range(4):
                    sl = tt % 2
                    t.wait_ge(s_gx[sl], 32 * (tt // 2 + 1))
                    for oc in range(NCH):
                        if nwo >= 4:
                            t.wait_ge(s_x1, nwo - 3)
                        for ic in range(NCH):
                            mm = t.matmul(WOPS(nwo),
                                          wot_sb[:, ic * D + oc * 128: ic * D + oc * 128 + 128],
                                          s_sb[:, ic * TC + tt * 512: ic * TC + tt * 512 + 512],
                                          start=(ic == 0), stop=(ic == NCH - 1))
                        mm.then_inc(s_wo, 1)
                        nwo += 1

            @blk2.vector
            def _(v):
                def VF(inst):
                    vfl[0] += 1
                    inst.then_inc(s_vfl, 1)
                    v.wait_ge(s_vfl, vfl[0])
                    return inst
                nwo = 0
                for tt in range(4):
                    sl = tt % 2
                    v.wait_ge(s_gx[sl], 32 * (tt // 2 + 1))
                    if tt >= 2:
                        v.wait_ge(s_ost[sl], 16 * (tt // 2))
                    for oc in range(NCH):
                        v.wait_ge(s_wo, nwo + 1)
                        VF(v.tensor_tensor(t1s[:, :], WOPS(nwo),
                                           gtile[:, sl * NCH * 512 + oc * 512:
                                                 sl * NCH * 512 + (oc + 1) * 512], ALU.mult))
                        v.tensor_tensor(x1stg[:, sl * NCH * 512 + oc * 512:
                                              sl * NCH * 512 + (oc + 1) * 512],
                                        xtile[:, sl * NCH * 512 + oc * 512:
                                              sl * NCH * 512 + (oc + 1) * 512],
                                        t1s[:, :], ALU.add).then_inc(s_x1, 1)
                        nwo += 1

def build_l0_scan():
    nc = bass.Bass("TRN2", target_bir_lowering=False, debug=False)
    DD = _decl_l0_scan(nc)
    emit_l0_scan(nc, DD)
    return nc




def _decl_l1_lat(nc):
    DD = {}
    DD["xn1"] = nc.dram_tensor("xn1", [D, TC], BF16, kind="ExternalInput").ap()
    DD["wk"] = nc.dram_tensor("wk", [D, D], F32, kind="ExternalInput").ap()
    DD["wv"] = nc.dram_tensor("wv", [D, D], F32, kind="ExternalInput").ap()
    DD["wdn"] = nc.dram_tensor("wdn", [D, DLAT], F32, kind="ExternalInput").ap()
    DD["lat"] = nc.dram_tensor("lat", [DLAT, TC], BF16, kind="ExternalOutput").ap()
    return DD

def emit_l1_lat(nc, DD):
    xn_d = DD["xn1"]
    wk_d = DD["wk"]
    wv_d = DD["wv"]
    wdn_d = DD["wdn"]
    lat_d = DD["lat"]
    with ExitStack() as ctx:
        ent = ctx.enter_context
        xnsb = ent(nc.sbuf_tensor("xnsb", [128, NCH * TC], BF16))
        wk = ent(nc.sbuf_tensor("wk_sb", [128, NCH * D], BF16))
        wv = ent(nc.sbuf_tensor("wv_sb", [128, NCH * D], BF16))
        wdn = ent(nc.sbuf_tensor("wdn_sb", [128, NCH * DLAT], BF16))
        kvf = ent(nc.sbuf_tensor("kvf", [128, NCH * TC], BF16))
        lstg = ent(nc.sbuf_tensor("lstg", [128, 2 * 2 * 512], BF16))
        ps = ent(nc.psum_tensor("ps", [128, 4096], F32))
        s_init = ent(nc.semaphore("s_init"))
        s_kv = ent(nc.semaphore("s_kv"))
        s_kvv = ent(nc.semaphore("s_kvv"))
        s_lt = ent(nc.semaphore("s_lt"))
        s_ltv = ent(nc.semaphore("s_ltv"))
        s_ost = [ent(nc.semaphore(f"s_ost{i}")) for i in range(2)]
        s_cc = ent(nc.semaphore("s_cc"))
        block = ent(nc.Block())
        PRJ = lambda r: ps[:, (r % 4) * 512:(r % 4) * 512 + 512]
        LPS = lambda r: ps[:, 2048 + (r % 2) * 1024: 2048 + (r % 2) * 1024 + 1024]

        @block.gpsimd
        def _(g_):
            g_.dma_start(out=wk.rearrange("p (c f) -> p c f", f=D),
                         in_=wk_d.rearrange("(c p) f -> p c f", p=128)).then_inc(s_init, 16)
            g_.dma_start(out=wv.rearrange("p (c f) -> p c f", f=D),
                         in_=wv_d.rearrange("(c p) f -> p c f", p=128)).then_inc(s_init, 16)
            g_.dma_start(out=wdn.rearrange("p (c f) -> p c f", f=DLAT),
                         in_=wdn_d.rearrange("(c p) f -> p c f", p=128)).then_inc(s_init, 16)
            if "latg" in DD:
                g_.wait_ge(s_ost[0], 32)
                g_.wait_ge(s_ost[1], 32)
                g_.collective_compute("AllGather", ALU.bypass, [[0, 1, 2, 3], [4, 5, 6, 7]],
                                      [lat_d.tensor.ap().opt()],
                                      [DD["latg"].tensor.ap().opt()]).then_inc(s_cc, 1)
                g_.wait_ge(s_cc, 1)

        @block.sync
        def _(sy):
            sy.dma_start(out=xnsb.rearrange("p (c t) -> p c t", t=TC),
                         in_=xn_d.rearrange("(c p) t -> p c t", p=128)).then_inc(s_init, 16)
            for i in range(4):
                sy.wait_ge(s_ltv, i + 1)
                sy.dma_start(out=lat_d.rearrange("(c p) t -> p c t", p=128)
                             [:, :, i * 512:(i + 1) * 512],
                             in_=lstg.rearrange("p (s c t) -> p s c t", s=2, t=512)[:, i % 2]) \
                    .then_inc(s_ost[i % 2], 16)
            sy.wait_ge(s_ost[0], 32)
            sy.wait_ge(s_ost[1], 32)

        @block.tensor
        def _(t):
            t.wait_ge(s_init, 64)
            nkv = 0
            for oc in range(NCH):
                for ti in range(4):
                    if nkv >= 4:
                        t.wait_ge(s_kvv, nkv - 3)
                    for ic in range(NCH):
                        t.matmul(PRJ(nkv), wk[:, ic * D + oc * 128: ic * D + oc * 128 + 128],
                                 xnsb[:, ic * TC + ti * 512: ic * TC + ti * 512 + 512],
                                 start=(ic == 0), stop=False)
                        mm = t.matmul(PRJ(nkv), wv[:, ic * D + oc * 128: ic * D + oc * 128 + 128],
                                      xnsb[:, ic * TC + ti * 512: ic * TC + ti * 512 + 512],
                                      start=False, stop=(ic == NCH - 1))
                    mm.then_inc(s_kv, 1)
                    nkv += 1
            for ti in range(4):
                if ti >= 2:
                    t.wait_ge(s_ltv, ti - 1)
                t.wait_ge(s_kvv, 32)
                for lc in range(2):
                    for ic in range(NCH):
                        mm = t.matmul(LPS(ti)[:, lc * 512:(lc + 1) * 512],
                                      wdn[:, ic * DLAT + lc * 128: ic * DLAT + lc * 128 + 128],
                                      kvf[:, ic * TC + ti * 512: ic * TC + ti * 512 + 512],
                                      start=(ic == 0), stop=(ic == NCH - 1))
                mm.then_inc(s_lt, 1)

        @block.vector
        def _(v):
            nkv = 0
            for oc in range(NCH):
                for ti in range(4):
                    v.wait_ge(s_kv, nkv + 1)
                    v.tensor_copy(kvf[:, oc * TC + ti * 512: oc * TC + ti * 512 + 512],
                                  PRJ(nkv)).then_inc(s_kvv, 1)
                    nkv += 1
            for ti in range(4):
                v.wait_ge(s_lt, ti + 1)
                if ti >= 2:
                    v.wait_ge(s_ost[ti % 2], 16 * (ti // 2))
                v.tensor_copy(lstg[:, (ti % 2) * 1024:(ti % 2) * 1024 + 1024],
                              LPS(ti)).then_inc(s_ltv, 1)

def build_l1_lat():
    nc = bass.Bass("TRN2", target_bir_lowering=False, debug=False)
    DD = _decl_l1_lat(nc)
    emit_l1_lat(nc, DD)
    return nc







def _decl_l1_attn(nc):
    DD = {}
    NJT = S // 128
    DD["xn1"] = nc.dram_tensor("xn1", [D, TC], BF16, kind="ExternalInput").ap()
    DD["latf"] = nc.dram_tensor("latf", [DLAT, S], BF16, kind="ExternalInput").ap()
    DD["lato"] = nc.dram_tensor("lato", [DLAT, TC], BF16, kind="ExternalInput").ap()
    DD["wq"] = nc.dram_tensor("wq", [D, D], F32, kind="ExternalInput").ap()
    DD["wupk"] = nc.dram_tensor("wupk", [DLAT, D], F32, kind="ExternalInput").ap()
    DD["wupv"] = nc.dram_tensor("wupv", [DLAT, D], F32, kind="ExternalInput").ap()
    DD["posq"] = nc.dram_tensor("posq", [1, TC], F32, kind="ExternalInput").ap()
    DD["posk"] = nc.dram_tensor("posk", [1, S], F32, kind="ExternalInput").ap()
    DD["amask"] = nc.dram_tensor("amask", [1, S], F32, kind="ExternalInput").ap()
    DD["bmask"] = nc.dram_tensor("bmask", [1, S], F32, kind="ExternalInput").ap()
    DD["biascol"] = nc.dram_tensor("biascol", [128, NJT], F32, kind="ExternalInput").ap()
    DD["s1"] = nc.dram_tensor("s1", [D, TC], BF16, kind="ExternalOutput").ap()

    return DD

def emit_l1_attn(nc, DD, gathered=False):
    CDEC = 1e-4
    SCL = float(1.0 / (np.sqrt(np.float32(1024.0)) + 1e-8))
    NJT = S // 128
    NJO = TC // 128
    xn_d = DD["xn1"]
    latf_d = DD["latf"]
    lato_d = DD["lato"]
    wq_d = DD["wq"]
    wupk_d = DD["wupk"]
    wupv_d = DD["wupv"]
    posq_d = DD["posq"]
    posk_d = DD["posk"]
    am_d = DD["amask"]
    bm_d = DD["bmask"]
    bc_d = DD["biascol"]
    s1_d = DD["s1"]
    with ExitStack() as ctx:
        ent = ctx.enter_context
        xnsb = ent(nc.sbuf_tensor("xnsb", [128, NCH * TC], BF16))
        latf = ent(nc.sbuf_tensor("latf_sb", [128, 2 * S], BF16))
        lato = ent(nc.sbuf_tensor("lato_sb", [128, 2 * TC], BF16))
        wq = ent(nc.sbuf_tensor("wq_sb", [128, NCH * D], BF16))
        wupk = ent(nc.sbuf_tensor("wupk_sb", [128, 2 * D], BF16))
        wupv = ent(nc.sbuf_tensor("wupv_sb", [128, 2 * D], BF16))
        rowbuf = ent(nc.sbuf_tensor("rowbuf", [1, 5 * 512], F32))
        qfac = ent(nc.sbuf_tensor("qfac", [128, TC], F32))
        kfsel = ent(nc.sbuf_tensor("kfsel", [128, S], BF16))
        kfown = ent(nc.sbuf_tensor("kfown", [128, TC], BF16))
        corr = ent(nc.sbuf_tensor("corr", [128, 4 * 512], F32))
        bcol = ent(nc.sbuf_tensor("bcol", [128, NJT], F32))
        zcol = ent(nc.sbuf_tensor("zcol", [128, 1], F32))
        onesf = ent(nc.sbuf_tensor("onesf", [128, 128], F32))
        qh = ent(nc.sbuf_tensor("qh", [128, TC], BF16))
        khg = ent(nc.sbuf_tensor("khg", [128, S], BF16))
        kho = ent(nc.sbuf_tensor("kho", [128, TC], BF16))
        vstg = ent(nc.sbuf_tensor("vstg", [128, NJT * 65], BF16))
        vsto = ent(nc.sbuf_tensor("vsto", [128, NJO * 65], BF16))
        ptile = ent(nc.sbuf_tensor("ptile", [128, 4 * 1024], BF16))
        stile = ent(nc.sbuf_tensor("stile", [128, 2 * 512], F32))
        drow = ent(nc.sbuf_tensor("drow", [1, 2 * 512], F32))
        bc64a = ent(nc.sbuf_tensor("bc64a", [128, 2 * 512], F32))
        s1t = ent(nc.sbuf_tensor("s1t", [64, 2 * 512], BF16))
        ps = ent(nc.psum_tensor("ps", [128, 4096], F32))
        s_init = ent(nc.semaphore("s_init"))
        s_gfl = ent(nc.semaphore("s_gfl"))
        s_vfl = ent(nc.semaphore("s_vfl"))
        s_rw = ent(nc.semaphore("s_rw"))
        s_fd = ent(nc.semaphore("s_fd"))
        s_fexp = ent(nc.semaphore("s_fexp"))
        s_fmul = ent(nc.semaphore("s_fmul"))
        s_fb = ent(nc.semaphore("s_fb"))
        s_fv = ent(nc.semaphore("s_fv"))
        s_qp = ent(nc.semaphore("s_qp"))
        s_qv = ent(nc.semaphore("s_qv"))
        s_st = ent(nc.semaphore("s_st"))
        s_pt = ent(nc.semaphore("s_pt"))
        s_cr = ent(nc.semaphore("s_cr"))
        s_pv = ent(nc.semaphore("s_pv"))
        s_pp = ent(nc.semaphore("s_pp"))
        s_rd = ent(nc.semaphore("s_rd"))
        s_bd1 = [ent(nc.semaphore(f"s_bd1{i}")) for i in range(2)]
        s_bd2 = [ent(nc.semaphore(f"s_bd2{i}")) for i in range(2)]
        s_sv = ent(nc.semaphore("s_sv"))
        s_ost = [ent(nc.semaphore(f"s_ost{i}")) for i in range(2)]
        block = ent(nc.Block())

        STp = lambda r: ps[:, (r % 3) * 1024:(r % 3) * 1024 + 1024]
        PV = lambda r: ps[0:65, 3072 + (r % 2) * 512: 3072 + (r % 2) * 512 + 512]
        PRD = lambda r: ps[:, (r % 2) * 512:(r % 2) * 512 + 512]
        drow_dr = nc.dram_tensor("attn_drow_dr", [2, 512], F32, kind="Internal").ap()

        INIT_TOT = 16 * (14 if gathered else 7) + 4
        FPLAN = [("q", i) for i in range(4)] + [("k", i) for i in range(16)] \
            + [("o", i) for i in range(4)]
        RB = lambda sl: rowbuf[:, sl * 512:(sl + 1) * 512]

        @block.gpsimd
        def _(g_):
            g_.dma_start(out=wq.rearrange("p (c f) -> p c f", f=D),
                         in_=wq_d.rearrange("(c p) f -> p c f", p=128)).then_inc(s_init, 16)
            g_.dma_start(out=wupk.rearrange("p (c f) -> p c f", f=D),
                         in_=wupk_d.rearrange("(c p) f -> p c f", p=128)).then_inc(s_init, 16)
            g_.dma_start(out=wupv.rearrange("p (c f) -> p c f", f=D),
                         in_=wupv_d.rearrange("(c p) f -> p c f", p=128)).then_inc(s_init, 16)
            for r in range(4):
                g_.iota(corr[:, r * 512:(r + 1) * 512], [[-1, 512]], base=r * 128,
                        channel_multiplier=1,
                        allow_small_or_imprecise_dtypes=True).then_inc(s_gfl, 1)
            g_.wait_ge(s_gfl, 4)
            g_.tensor_scalar_max(corr[:, :], corr[:, :], 0.0).then_inc(s_gfl, 1)
            g_.memset(onesf[:, :], 1.0).then_inc(s_init, 1)
            g_.memset(vstg[:, :], 1.0).then_inc(s_init, 1)
            g_.memset(vsto[:, :], 1.0).then_inc(s_init, 1)
            g_.memset(zcol[:, :], 0.0).then_inc(s_init, 1)

        @block.sync
        def _(sy):
            sy.dma_start(out=xnsb.rearrange("p (c t) -> p c t", t=TC),
                         in_=xn_d.rearrange("(c p) t -> p c t", p=128)).then_inc(s_init, 16)
            if gathered:
                for rr in range(4):
                    for lc in range(2):
                        sy.dma_start(out=latf[:, lc * S + rr * TC: lc * S + (rr + 1) * TC],
                                     in_=latf_d[rr * 256 + lc * 128: rr * 256 + lc * 128 + 128, :]) \
                            .then_inc(s_init, 16)
            else:
                sy.dma_start(out=latf.rearrange("p (c t) -> p c t", t=S),
                             in_=latf_d.rearrange("(c p) t -> p c t", p=128)).then_inc(s_init, 16)
            sy.dma_start(out=lato.rearrange("p (c t) -> p c t", t=TC),
                         in_=lato_d.rearrange("(c p) t -> p c t", p=128)).then_inc(s_init, 16)
            sy.dma_start(out=bcol[:, :], in_=bc_d[:, :]).then_inc(s_init, 16)
            nfd = 0
            for p, (kind, idx) in enumerate(FPLAN):
                if p >= 1:
                    sy.wait_ge(s_fv, p)
                src = posk_d if kind == "k" else posq_d
                sy.dma_start(out=RB(0), in_=src[:, idx * 512:(idx + 1) * 512]) \
                    .then_inc(s_fd, 16)
                nfd += 16
                if kind == "k":
                    sy.dma_start(out=RB(1), in_=am_d[:, idx * 512:(idx + 1) * 512]) \
                        .then_inc(s_fd, 16)
                    sy.dma_start(out=RB(2), in_=bm_d[:, idx * 512:(idx + 1) * 512]) \
                        .then_inc(s_fd, 16)
                    nfd += 32
            for h in range(H):
                for it in range(4):
                    n = h * 4 + it
                    sy.wait_ge(s_rd, n + 1)
                    if n >= 2:
                        sy.wait_ge(s_bd2[n % 2], 16 * (n // 2))
                    sy.dma_start(out=drow_dr[n % 2:n % 2 + 1, :],
                                 in_=drow[:, (n % 2) * 512:(n % 2) * 512 + 512]) \
                        .then_inc(s_bd1[n % 2], 16)
                    sy.wait_ge(s_bd1[n % 2], 16 * (n // 2 + 1))
                    if n >= 2:
                        sy.wait_ge(s_sv, n - 1)
                    sy.dma_start(out=bc64a[0:64, (n % 2) * 512:(n % 2) * 512 + 512],
                                 in_=bass.AP(drow_dr.tensor, (n % 2) * 512,
                                             [[0, 64], [1, 512]])) \
                        .then_inc(s_bd2[n % 2], 16)
                    sy.wait_ge(s_sv, n + 1)
                    sy.dma_start(out=s1_d[h * 64:(h + 1) * 64, it * 512:(it + 1) * 512],
                                 in_=s1t[:, (n % 2) * 512:(n % 2) * 512 + 512]) \
                        .then_inc(s_ost[n % 2], 16)
            sy.wait_ge(s_ost[0], 16 * 32)
            sy.wait_ge(s_ost[1], 16 * 32)

        @block.scalar
        def _(s):
            s.wait_ge(s_init, INIT_TOT)
            s.wait_ge(s_gfl, 5)
            s.activation(corr[:, :], corr[:, :], AF.Exp, scale=-2.0 * CDEC).then_inc(s_rw, 1)
            nfd = 0
            for p, (kind, idx) in enumerate(FPLAN):
                nfd += 48 if kind == "k" else 16
                s.wait_ge(s_fd, nfd)
                sc0 = -CDEC if kind == "q" else CDEC
                s.activation(RB(3), RB(0), AF.Exp, scale=sc0)
                s.activation(RB(4), RB(0), AF.Exp, scale=-sc0).then_inc(s_fexp, 1)
            for h in range(H):
                for it in range(4):
                    hit = h * 4 + it
                    for Jp in range(40):
                        nst = hit * 40 + Jp
                        sl = (nst % 4) * 1024
                        s.wait_ge(s_st, nst + 1)
                        if nst >= 4:
                            s.wait_ge(s_pp, nst - 3)
                        if Jp < 32:
                            s.activation(ptile[:, sl:sl + 1024], STp(nst),
                                         AF.Exp, bias=bcol[:, 2 * Jp:2 * Jp + 1]) \
                                .then_inc(s_pt, 1)
                        else:
                            Jo0 = 2 * (Jp - 32)
                            d0, d1 = Jo0 - it * 4, Jo0 + 1 - it * 4
                            diag0, diag1 = 0 <= d0 < 4, 0 <= d1 < 4
                            if not diag0 and not diag1:
                                s.activation(ptile[:, sl:sl + 1024], STp(nst),
                                             AF.Exp).then_inc(s_pt, 1)
                            else:
                                insts = []
                                for half, dd, dg in ((0, d0, diag0), (1, d1, diag1)):
                                    dst = ptile[:, sl + half * 512: sl + (half + 1) * 512]
                                    if dg:
                                        s.wait_ge(s_cr, hit * 4 + dd + 1)
                                        insts.append(s.activation(
                                            dst, stile[:, (dd % 2) * 512:(dd % 2) * 512 + 512],
                                            AF.Exp))
                                    else:
                                        insts.append(s.activation(
                                            dst, STp(nst)[:, half * 512:(half + 1) * 512],
                                            AF.Exp))
                                insts[-1].then_inc(s_pt, 1)

        @block.tensor
        def _(t):
            t.wait_ge(s_init, INIT_TOT)
            for p, (kind, idx) in enumerate(FPLAN):
                t.wait_ge(s_fmul, p + 1)
                for ver in range(2):
                    mm = t.matmul(PRD(0)[ver * 64:(ver + 1) * 64, :], onesf[0:1, 0:64],
                                  RB(3 + ver), start=True, stop=True,
                                  tile_position=(0, ver * 64))
                mm.then_inc(s_fb, 1)
            nprd = [48]
            nst = 0
            def _emit_pv(t, hit, Jp):
                it = hit % 4
                npv = hit * 40 + Jp
                t.wait_ge(s_pt, npv + 1)
                for half in range(2):
                    if Jp < 32:
                        J = 2 * Jp + half
                        vsl = vstg[:, J * 65:(J + 1) * 65]
                    else:
                        Jo = 2 * (Jp - 32) + half
                        vsl = vsto[:, Jo * 65:(Jo + 1) * 65]
                    mm = t.matmul(PV(it), vsl,
                                  ptile[:, (npv % 4) * 1024 + half * 512:
                                        (npv % 4) * 1024 + (half + 1) * 512],
                                  start=(Jp == 0 and half == 0),
                                  stop=(Jp == 39 and half == 1))
                mm.then_inc(s_pp, 1)
                return mm
            t.wait_ge(s_fv, 24)
            def pguard():
                pi = nprd[0] - 48
                if pi >= 2:
                    t.wait_ge(s_qv, pi - 1)
            for h in range(H):
                hw = h * 64
                if h >= 1:
                    t.wait_ge(s_pt, 160 * h)
                for tt in range(4):
                    pguard()
                    for ver in range(2):
                        for ic in range(NCH):
                            mm = t.matmul(PRD(nprd[0])[ver * 64:(ver + 1) * 64, :],
                                          wq[:, ic * D + hw: ic * D + hw + 64],
                                          xnsb[:, ic * TC + tt * 512: ic * TC + (tt + 1) * 512],
                                          start=(ic == 0), stop=(ic == NCH - 1),
                                          tile_position=(0, ver * 64))
                    mm.then_inc(s_qp, 1)
                    nprd[0] += 1
                for jt in range(16):
                    pguard()
                    for ver in range(2):
                        for lc in range(2):
                            mm = t.matmul(PRD(nprd[0])[ver * 64:(ver + 1) * 64, :],
                                          wupk[:, lc * D + hw: lc * D + hw + 64],
                                          latf[:, lc * S + jt * 512: lc * S + (jt + 1) * 512],
                                          start=(lc == 0), stop=(lc == 1),
                                          tile_position=(0, ver * 64))
                    mm.then_inc(s_qp, 1)
                    nprd[0] += 1
                for jt in range(4):
                    pguard()
                    for ver in range(2):
                        for lc in range(2):
                            mm = t.matmul(PRD(nprd[0])[ver * 64:(ver + 1) * 64, :],
                                          wupk[:, lc * D + hw: lc * D + hw + 64],
                                          lato[:, lc * TC + jt * 512: lc * TC + (jt + 1) * 512],
                                          start=(lc == 0), stop=(lc == 1),
                                          tile_position=(0, ver * 64))
                    mm.then_inc(s_qp, 1)
                    nprd[0] += 1
                for pk in range(8):
                    pguard()
                    for j8 in range(8):
                        jt = pk * 8 + j8
                        for lc in range(2):
                            mm = t.matmul(PRD(nprd[0])[:, j8 * 64:(j8 + 1) * 64],
                                          latf[:, lc * S + jt * 128: lc * S + jt * 128 + 128],
                                          wupv[:, lc * D + hw: lc * D + hw + 64],
                                          start=(lc == 0), stop=(lc == 1))
                    mm.then_inc(s_qp, 1)
                    nprd[0] += 1
                for pk in range(2):
                    pguard()
                    for j8 in range(8):
                        jt = pk * 8 + j8
                        for lc in range(2):
                            mm = t.matmul(PRD(nprd[0])[:, j8 * 64:(j8 + 1) * 64],
                                          lato[:, lc * TC + jt * 128: lc * TC + jt * 128 + 128],
                                          wupv[:, lc * D + hw: lc * D + hw + 64],
                                          start=(lc == 0), stop=(lc == 1))
                    mm.then_inc(s_qp, 1)
                    nprd[0] += 1
                t.wait_ge(s_qv, (h + 1) * 34)
                for it in range(4):
                    hit = h * 4 + it
                    if hit >= 2:
                        t.wait_ge(s_sv, hit - 1)
                    for Jp in range(40):
                        if nst >= 3:
                            t.wait_ge(s_pt, nst - 2)
                        sl = STp(nst)
                        if Jp >= 2:
                            _emit_pv(t, hit, Jp - 2)
                        for half in range(2):
                            if Jp < 32:
                                J = 2 * Jp + half
                                mm = t.matmul(sl[:, half * 512:(half + 1) * 512],
                                              khg[:, J * 128:(J + 1) * 128],
                                              qh[:, it * 512:(it + 1) * 512],
                                              start=True, stop=True)
                            else:
                                Jo = 2 * (Jp - 32) + half
                                d = Jo - it * 4
                                if d >= 4:
                                    mm = t.matmul(sl[:, half * 512:(half + 1) * 512],
                                                  kho[64:128, Jo * 128:(Jo + 1) * 128],
                                                  qh[64:128, it * 512:(it + 1) * 512],
                                                  start=True, stop=True,
                                                  tile_position=(64, 0))
                                else:
                                    mm = t.matmul(sl[:, half * 512:(half + 1) * 512],
                                                  kho[0:64, Jo * 128:(Jo + 1) * 128],
                                                  qh[0:64, it * 512:(it + 1) * 512],
                                                  start=True, stop=True)
                        mm.then_inc(s_st, 1)
                        nst += 1
                    _emit_pv(t, hit, 38)
                    _emit_pv(t, hit, 39)
                    t.drain().then_inc(s_pv, 1)

        @block.vector
        def _(v):
            vfl = [0]
            def VF(inst):
                vfl[0] += 1
                inst.then_inc(s_vfl, 1)
                v.wait_ge(s_vfl, vfl[0])
                return inst
            v.wait_ge(s_init, INIT_TOT)
            for p, (kind, idx) in enumerate(FPLAN):
                v.wait_ge(s_fexp, p + 1)
                if kind == "q":
                    VF(v.tensor_scalar(RB(3), RB(3), SCL, None, ALU.mult))
                    v.tensor_scalar(RB(4), RB(4), SCL, None, ALU.mult).then_inc(s_fmul, 1)
                elif kind == "k":
                    VF(v.tensor_tensor(RB(3), RB(3), RB(1), ALU.mult))
                    v.tensor_tensor(RB(4), RB(4), RB(2), ALU.mult).then_inc(s_fmul, 1)
                else:
                    VF(v.tensor_scalar(RB(3), RB(3), 1.0, None, ALU.mult))
                    v.tensor_scalar(RB(4), RB(4), 1.0, None, ALU.mult).then_inc(s_fmul, 1)
                v.wait_ge(s_fb, p + 1)
                dst = {"q": qfac, "k": kfsel, "o": kfown}[kind]
                v.tensor_copy(dst[:, idx * 512:(idx + 1) * 512], PRD(0)).then_inc(s_fv, 1)
            for h in range(H):
                nq = 0
                for tt in range(4):
                    v.wait_ge(s_qp, h * 34 + nq + 1)
                    v.tensor_tensor(qh[:, tt * 512:(tt + 1) * 512], PRD(48 + h * 34 + nq),
                                    qfac[:, tt * 512:(tt + 1) * 512], ALU.mult) \
                        .then_inc(s_qv, 1)
                    nq += 1
                for jt in range(16):
                    v.wait_ge(s_qp, h * 34 + nq + 1)
                    v.tensor_tensor(khg[:, jt * 512:(jt + 1) * 512], PRD(48 + h * 34 + nq),
                                    kfsel[:, jt * 512:(jt + 1) * 512], ALU.mult) \
                        .then_inc(s_qv, 1)
                    nq += 1
                for jt in range(4):
                    v.wait_ge(s_qp, h * 34 + nq + 1)
                    v.tensor_tensor(kho[:, jt * 512:(jt + 1) * 512], PRD(48 + h * 34 + nq),
                                    kfown[:, jt * 512:(jt + 1) * 512], ALU.mult) \
                        .then_inc(s_qv, 1)
                    nq += 1
                for pk in range(8):
                    v.wait_ge(s_qp, h * 34 + nq + 1)
                    v.tensor_copy(vstg.rearrange("p (j e) -> p j e", e=65)
                                  [:, pk * 8:(pk + 1) * 8, 0:64],
                                  PRD(48 + h * 34 + nq).rearrange("p (j e) -> p j e", e=64)) \
                        .then_inc(s_qv, 1)
                    nq += 1
                for pk in range(2):
                    v.wait_ge(s_qp, h * 34 + nq + 1)
                    v.tensor_copy(vsto.rearrange("p (j e) -> p j e", e=65)
                                  [:, pk * 8:(pk + 1) * 8, 0:64],
                                  PRD(48 + h * 34 + nq).rearrange("p (j e) -> p j e", e=64)) \
                        .then_inc(s_qv, 1)
                    nq += 1
                for it in range(4):
                    hit = h * 4 + it
                    for d in range(4):
                        Jo = it * 4 + d
                        nst_d = hit * 40 + 32 + Jo // 2
                        v.wait_ge(s_st, nst_d + 1)
                        if hit >= 1 and d < 2:
                            v.wait_ge(s_pt, hit * 40)
                        if d >= 2:
                            v.wait_ge(s_pt, hit * 40 + 32 + (it * 4 + d - 2) // 2 + 1)
                        v.tensor_tensor(stile[:, (d % 2) * 512:(d % 2) * 512 + 512],
                                        STp(nst_d)[:, (Jo % 2) * 512:(Jo % 2) * 512 + 512],
                                        corr[:, d * 512:(d + 1) * 512], ALU.mult) \
                            .then_inc(s_cr, 1)
                    v.wait_ge(s_pv, hit + 1)
                    if hit >= 2:
                        v.wait_ge(s_bd1[hit % 2], 16 * (hit // 2))
                    VF(v.tensor_copy(drow[:, (hit % 2) * 512:(hit % 2) * 512 + 512],
                                     PV(it)[64:65, :]))
                    v.reciprocal(drow[:, (hit % 2) * 512:(hit % 2) * 512 + 512],
                                 drow[:, (hit % 2) * 512:(hit % 2) * 512 + 512]) \
                        .then_inc(s_rd, 1)
                    v.wait_ge(s_bd2[hit % 2], 16 * (hit // 2 + 1))
                    if hit >= 2:
                        v.wait_ge(s_ost[hit % 2], 16 * (hit // 2))
                    v.tensor_tensor(s1t[:, (hit % 2) * 512:(hit % 2) * 512 + 512],
                                    PV(it)[0:64, :],
                                    bc64a[0:64, (hit % 2) * 512:(hit % 2) * 512 + 512],
                                    ALU.mult).then_inc(s_sv, 1)

def build_l1_attn():
    nc = bass.Bass("TRN2", target_bir_lowering=False, debug=False)
    DD = _decl_l1_attn(nc)
    emit_l1_attn(nc, DD)
    return nc



def _decl_l1_ffn(nc):
    DD = {}
    DD["xn1"] = nc.dram_tensor("xn1", [D, TC], BF16, kind="ExternalInput").ap()
    DD["wgt"] = nc.dram_tensor("wgt", [D, HID], I32, kind="ExternalInput").ap()
    DD["wut"] = nc.dram_tensor("wut", [D, HID], I32, kind="ExternalInput").ap()
    DD["wdt"] = nc.dram_tensor("wdt", [HID, D], I32, kind="ExternalInput").ap()
    DD["s1"] = nc.dram_tensor("s1", [D, TC], BF16, kind="ExternalInput").ap()
    DD["wo"] = nc.dram_tensor("wo", [D, D], F32, kind="ExternalInput").ap()
    DD["x1"] = nc.dram_tensor("x1", [D, TC], F32, kind="ExternalInput").ap()
    DD["x2"] = nc.dram_tensor("x2", [D, TC], F32, kind="ExternalOutput").ap()
    DD["h_scr"] = nc.dram_tensor("h_scr", [HID, TC], BF16)

    return DD

def emit_l1_ffn(nc, DD):
    NHC = HID // 128
    QS = float(1.0 / 127.0)
    xn_d = DD["xn1"]
    wgt_d = DD["wgt"]
    wut_d = DD["wut"]
    wdt_d = DD["wdt"]
    s1_d = DD["s1"]
    wo_d = DD["wo"]
    x1_d = DD["x1"]
    x2_d = DD["x2"]
    h_d = DD["h_scr"]
    with ExitStack() as ctx:
        ent = ctx.enter_context
        xnsb = ent(nc.sbuf_tensor("xnsb", [128, NCH * TC], BF16))
        wo_sb = ent(nc.sbuf_tensor("wo_sb", [128, NCH * D], BF16))
        wgi = ent(nc.sbuf_tensor("wgi", [128, 3 * NCH * 128], I32))
        wui = ent(nc.sbuf_tensor("wui", [128, 3 * NCH * 128], I32))
        wgb = ent(nc.sbuf_tensor("wgb", [128, 3 * NCH * 128], BF16))
        wub = ent(nc.sbuf_tensor("wub", [128, 3 * NCH * 128], BF16))
        _gs_off = nc.sbuf_base
        gsil = ent(nc.sbuf_tensor("gsil", [128, 2 * 1024], BF16))
        hstg = ent(nc.sbuf_tensor("hstg", [128, 1024], BF16))
        wdi = ent(nc.sbuf_tensor("wdi", [128, 2 * NHC * 128], I32))
        wdb = ent(nc.sbuf_tensor("wdb", [128, 2 * NHC * 128], BF16))
        htl = ent(nc.sbuf_tensor("htl", [128, 2 * NHC * 512], BF16))
        s1t = ent(nc.sbuf_tensor("s1t", [128, 2 * NCH * 512], BF16))
        x1t = ent(nc.sbuf_tensor("x1t", [128, NCH * 512], F32))
        g1t = ent(nc.sbuf_tensor("g1t", [128, 512], BF16))
        t1s = nc.alloc_sbuf_tensor_at("t1s", [128, 512], F32, offset=_gs_off)
        ps = ent(nc.psum_tensor("ps", [128, 4096], F32))
        s_init = ent(nc.semaphore("s_init"))
        s_vfl = ent(nc.semaphore("s_vfl"))
        s_wg = [ent(nc.semaphore(f"s_wg{i}")) for i in range(3)]
        s_wdq = ent(nc.semaphore("s_wdq"))
        s_gu = ent(nc.semaphore("s_gu"))
        s_sil = ent(nc.semaphore("s_sil"))
        s_hv = ent(nc.semaphore("s_hv"))
        s_hst = [ent(nc.semaphore(f"s_hst{i}")) for i in range(2)]
        s_hld = [ent(nc.semaphore(f"s_hld{i}")) for i in range(2)]
        s_sx = [ent(nc.semaphore(f"s_sx{i}")) for i in range(2)]
        s_wd = [ent(nc.semaphore(f"s_wd{i}")) for i in range(2)]
        s_wdb = ent(nc.semaphore("s_wdb"))
        s_dn = ent(nc.semaphore("s_dn"))
        s_wo = ent(nc.semaphore("s_wo"))
        s_g1 = ent(nc.semaphore("s_g1"))
        s_x2 = ent(nc.semaphore("s_x2"))
        s_ost = [ent(nc.semaphore(f"s_ost{i}")) for i in range(2)]
        block = ent(nc.Block())

        GPS = lambda r: ps[:, (r % 2) * 1024:(r % 2) * 1024 + 1024]
        UPS = lambda r: ps[:, 2048 + (r % 2) * 1024: 2048 + (r % 2) * 1024 + 1024]
        DPS = lambda r: ps[:, (r % 2) * 512:(r % 2) * 512 + 512]
        WPS = lambda r: ps[:, 1024 + (r % 2) * 512: 1024 + (r % 2) * 512 + 512]

        @block.gpsimd
        def _(g_):
            g_.dma_start(out=wo_sb.rearrange("p (c f) -> p c f", f=D),
                         in_=wo_d.rearrange("(c p) f -> p c f", p=128)).then_inc(s_init, 16)
            for tt in range(4):
                sl = tt % 2
                if tt >= 2:
                    g_.wait_ge(s_x2, 8 * (tt - 1))
                g_.wait_ge(s_hst[0], 16 * 2 * NHC)
                g_.dma_start(out=htl[:, sl * NHC * 512:(sl + 1) * NHC * 512]
                             .rearrange("p (c t) -> p c t", t=512),
                             in_=h_d.rearrange("(c p) t -> p c t", p=128)
                             [:, :, tt * 512:(tt + 1) * 512]).then_inc(s_hld[sl], 16)

        @block.sync
        def _(sy):
            sy.dma_start(out=xnsb.rearrange("p (c t) -> p c t", t=TC),
                         in_=xn_d.rearrange("(c p) t -> p c t", p=128)).then_inc(s_init, 16)
            for k in range(2 * NHC):
                th, hc = k // NHC, k % NHC
                sl = k % 3
                if k >= 3:
                    sy.wait_ge(s_wdq, 2 * (k - 2))
                sy.dma_start(out=wgi[:, sl * 1024:(sl + 1) * 1024]
                             .rearrange("p (c f) -> p c f", f=128),
                             in_=wgt_d[:, hc * 128:(hc + 1) * 128]
                             .rearrange("(c p) f -> p c f", p=128)).then_inc(s_wg[sl], 16)
                sy.dma_start(out=wui[:, sl * 1024:(sl + 1) * 1024]
                             .rearrange("p (c f) -> p c f", f=128),
                             in_=wut_d[:, hc * 128:(hc + 1) * 128]
                             .rearrange("(c p) f -> p c f", p=128)).then_inc(s_wg[sl], 16)
                sy.wait_ge(s_hv, 2 * k + 2)
                sy.dma_start(out=h_d[hc * 128:(hc + 1) * 128, th * 1024:(th + 1) * 1024],
                             in_=hstg[:, :]).then_inc(s_hst[0], 16)
            for tt in range(4):
                sl = tt % 2
                if tt >= 2:
                    sy.wait_ge(s_x2, 8 * (tt - 1))
                sy.dma_start(out=s1t[:, sl * NCH * 512:(sl + 1) * NCH * 512]
                             .rearrange("p (c t) -> p c t", t=512),
                             in_=s1_d.rearrange("(c p) t -> p c t", p=128)
                             [:, :, tt * 512:(tt + 1) * 512]).then_inc(s_sx[sl], 16)
                if tt >= 1:
                    sy.wait_ge(s_ost[0], 16 * tt)
                sy.dma_start(out=x1t.rearrange("p (c t) -> p c t", t=512),
                             in_=x1_d.rearrange("(c p) t -> p c t", p=128)
                             [:, :, tt * 512:(tt + 1) * 512]).then_inc(s_sx[sl], 16)
                for oc in range(NCH):
                    n = tt * NCH + oc
                    so = n % 2
                    if n >= 2:
                        sy.wait_ge(s_wdb, n - 1)
                    sy.dma_start(out=wdi[:, so * NHC * 128:(so + 1) * NHC * 128]
                                 .rearrange("p (c f) -> p c f", f=128),
                                 in_=wdt_d[:, oc * 128:(oc + 1) * 128]
                                 .rearrange("(c p) f -> p c f", p=128)).then_inc(s_wd[so], 16)
                sy.wait_ge(s_x2, 8 * (tt + 1))
                sy.dma_start(out=x2_d.rearrange("(c p) t -> p c t", p=128)
                             [:, :, tt * 512:(tt + 1) * 512],
                             in_=x1t.rearrange("p (c t) -> p c t", t=512)) \
                    .then_inc(s_ost[0], 16)
            sy.wait_ge(s_ost[0], 64)

        @block.tensor
        def _(t):
            t.wait_ge(s_init, 32)
            for k in range(2 * NHC):
                th, hc = k // NHC, k % NHC
                sl = k % 3
                t.wait_ge(s_wdq, 2 * k + 2)
                if k >= 2:
                    t.wait_ge(s_hv, 2 * (k - 1))
                for half, PSL in ((0, GPS), (1, UPS)):
                    wsb = wgb if half == 0 else wub
                    for tt2 in range(2):
                        t0 = th * 1024 + tt2 * 512
                        for ic in range(NCH):
                            mm = t.matmul(PSL(k)[:, tt2 * 512:(tt2 + 1) * 512],
                                          wsb[:, sl * NCH * 128 + ic * 128: sl * NCH * 128 + (ic + 1) * 128],
                                          xnsb[:, ic * TC + t0: ic * TC + t0 + 512],
                                          start=(ic == 0), stop=(ic == NCH - 1))
                mm.then_inc(s_gu, 1)
            for tt in range(4):
                sl = tt % 2
                t.wait_ge(s_hld[sl], 16 * (tt // 2 + 1))
                t.wait_ge(s_sx[sl], 32 * (tt // 2 + 1))
                for oc in range(NCH):
                    n = tt * NCH + oc
                    t.wait_ge(s_wdb, n + 1)
                    if n >= 2:
                        t.wait_ge(s_x2, n - 1)
                    for hcc in range(NHC):
                        mm = t.matmul(DPS(n), wdb[:, (n % 2) * NHC * 128 + hcc * 128:
                                                  (n % 2) * NHC * 128 + (hcc + 1) * 128],
                                      htl[:, sl * NHC * 512 + hcc * 512:
                                          sl * NHC * 512 + (hcc + 1) * 512],
                                      start=(hcc == 0), stop=(hcc == NHC - 1))
                    mm.then_inc(s_dn, 1)
                    for ic in range(NCH):
                        mm = t.matmul(WPS(n), wo_sb[:, ic * D + oc * 128: ic * D + oc * 128 + 128],
                                      s1t[:, sl * NCH * 512 + ic * 512: sl * NCH * 512 + (ic + 1) * 512],
                                      start=(ic == 0), stop=(ic == NCH - 1))
                    mm.then_inc(s_wo, 1)

        @block.scalar
        def _(s):
            for k in range(2 * NHC):
                s.wait_ge(s_gu, k + 1)
                s.activation(gsil[:, (k % 2) * 1024:(k % 2) * 1024 + 1024], GPS(k),
                             AF.Silu).then_inc(s_sil, 1)
            for n in range(32):
                s.wait_ge(s_dn, n + 1)
                if n >= 1:
                    s.wait_ge(s_x2, n)
                s.activation(g1t[:, :], DPS(n), AF.Silu).then_inc(s_g1, 1)

        @block.vector
        def _(v):
            vfl = [0]
            def VF(inst):
                vfl[0] += 1
                inst.then_inc(s_vfl, 1)
                v.wait_ge(s_vfl, vfl[0])
                return inst
            for k in range(2 * NHC):
                sl = k % 3
                v.wait_ge(s_wg[sl], 32 * (k // 3 + 1))
                v.tensor_scalar(wgb[:, sl * 1024:(sl + 1) * 1024],
                                wgi[:, sl * 1024:(sl + 1) * 1024],
                                QS, None, ALU.mult).then_inc(s_wdq, 1)
                v.tensor_scalar(wub[:, sl * 1024:(sl + 1) * 1024],
                                wui[:, sl * 1024:(sl + 1) * 1024],
                                QS, None, ALU.mult).then_inc(s_wdq, 1)
                v.wait_ge(s_sil, k + 1)
                if k >= 1:
                    v.wait_ge(s_hst[0], 16 * k)
                v.tensor_tensor(hstg[:, :], UPS(k),
                                gsil[:, (k % 2) * 1024:(k % 2) * 1024 + 1024],
                                ALU.mult).then_inc(s_hv, 2)
            for tt in range(4):
                sl = tt % 2
                for oc in range(NCH):
                    n = tt * NCH + oc
                    so = n % 2
                    v.wait_ge(s_wd[so], 16 * (n // 2 + 1))
                    v.tensor_scalar(wdb[:, so * NHC * 128:(so + 1) * NHC * 128],
                                    wdi[:, so * NHC * 128:(so + 1) * NHC * 128],
                                    QS, None, ALU.mult).then_inc(s_wdb, 1)
                    v.wait_ge(s_g1, n + 1)
                    v.wait_ge(s_wo, n + 1)
                    VF(v.tensor_tensor(t1s[:, :], WPS(n), g1t[:, :], ALU.mult))
                    v.tensor_tensor(x1t[:, oc * 512:(oc + 1) * 512],
                                    x1t[:, oc * 512:(oc + 1) * 512],
                                    t1s[:, :], ALU.add).then_inc(s_x2, 1)

def build_l1_ffn():
    nc = bass.Bass("TRN2", target_bir_lowering=False, debug=False)
    DD = _decl_l1_ffn(nc)
    emit_l1_ffn(nc, DD)
    return nc




class _NCP:

    def __init__(self, nc, pfx):
        self._nc = nc
        self._p = pfx

    def sbuf_tensor(self, name, shape, dtype, **kw):
        return self._nc.sbuf_tensor(self._p + name, shape, dtype, **kw)

    def psum_tensor(self, name, shape, dtype, **kw):
        return self._nc.psum_tensor(self._p + name, shape, dtype, **kw)

    def semaphore(self, name=None, num=None):
        return self._nc.semaphore(self._p + (name or "s"), num)

    def __getattr__(self, a):
        return getattr(self._nc, a)


def _emit_gather(nc, src_ap, dst_ap, tag):
    RG = [[0, 1, 2, 3], [4, 5, 6, 7]]
    with nc.semaphore(f"s_cc_{tag}") as s_c, nc.Block() as blk:
        @blk.gpsimd
        def _(g_):
            g_.collective_compute("AllGather", ALU.bypass, RG,
                                  [src_ap.tensor.ap().opt()],
                                  [dst_ap.tensor.ap().opt()]).then_inc(s_c, 1)
            g_.wait_ge(s_c, 1)


def build_fused(nstages=10):
    nc = bass.Bass("TRN2", target_bir_lowering=False, debug=False)
    EI, INT = "ExternalInput", "Internal"
    dt = nc.dram_tensor
    x0 = dt("x0", [D, TC], F32, kind=EI).ap()
    w1 = dt("w1", [E, D, FF], F32, kind=EI).ap()
    w2 = dt("w2", [E, FF, D], F32, kind=EI).ap()
    sw1 = dt("sw1", [D, FF], F32, kind=EI).ap()
    sw2 = dt("sw2", [FF, D], F32, kind=EI).ap()
    b1t = dt("b1t", [E, 128, FF // 128], F32, kind=EI).ap()
    sb1t = dt("sb1t", [128, FF // 128], F32, kind=EI).ap()
    b2 = dt("b2", [E, D], F32, kind=EI).ap()
    sb2 = dt("sb2", [1, D], F32, kind=EI).ap()
    gwt = dt("gwt", [D, E], F32, kind=EI).ap()
    gb = dt("gb", [E, 1], F32, kind=EI).ap()
    wqt = dt("wqt", [D, D], F32, kind=EI).ap()
    wkt = dt("wkt", [D, D], F32, kind=EI).ap()
    wvt = dt("wvt", [D, D], F32, kind=EI).ap()
    wot = dt("wot", [D, D], F32, kind=EI).ap()
    pm64 = dt("pm64", [128, 4], F32, kind=EI).ap()
    wk1 = dt("wk", [D, D], F32, kind=EI).ap()
    wv1 = dt("wv", [D, D], F32, kind=EI).ap()
    wdn = dt("wdn", [D, DLAT], F32, kind=EI).ap()
    wq1 = dt("wq", [D, D], F32, kind=EI).ap()
    wupk = dt("wupk", [DLAT, D], F32, kind=EI).ap()
    wupv = dt("wupv", [DLAT, D], F32, kind=EI).ap()
    wo1 = dt("wo", [D, D], F32, kind=EI).ap()
    wgt = dt("wgt", [D, HID], I32, kind=EI).ap()
    wut = dt("wut", [D, HID], I32, kind=EI).ap()
    wdt = dt("wdt", [HID, D], I32, kind=EI).ap()
    posq = dt("posq", [1, TC], F32, kind=EI).ap()
    posk = dt("posk", [1, S], F32, kind=EI).ap()
    amask = dt("amask", [1, S], F32, kind=EI).ap()
    bmask = dt("bmask", [1, S], F32, kind=EI).ap()
    biascol = dt("biascol", [128, S // 128], F32, kind=EI).ap()
    xn = dt("xn", [D, TC], BF16, kind=INT).ap()
    xnf = dt("xnf", [D, TC], F32, kind=INT).ap()
    g0 = dt("g0", [D, TC], BF16, kind=INT).ap()
    qf = dt("qf", [D, TC], BF16, kind=INT).ap()
    kf = dt("kf", [D, TC], BF16, kind=INT).ap()
    ktm = dt("ktm", [TC, D], BF16, kind=INT).ap()
    vtm = dt("vtm", [TC, H * 65], BF16, kind=INT).ap()
    stot = dt("stot", [H * 64, 65], F32, kind=INT).ap()
    stall = dt("stall", [4 * H * 64, 65], F32, kind=INT).ap()
    x1 = dt("x1", [D, TC], F32, kind=INT).ap()
    s_out = dt("s_out", [D, TC], BF16, kind=INT).ap()
    xn1 = dt("xn1", [D, TC], BF16, kind=INT).ap()
    xnf1 = dt("xnf1", [D, TC], F32, kind=INT).ap()
    lat = dt("lat", [DLAT, TC], BF16, kind=INT).ap()
    latg = dt("latg", [4 * DLAT, TC], BF16, kind=INT).ap()
    s1 = dt("s1", [D, TC], BF16, kind=INT).ap()
    h_scr = dt("h_scr", [HID, TC], BF16, kind=INT).ap()
    w1scr = dt("moe_w1scr", [9 * 8, 128, NCH * 512], BF16, kind=INT).ap()
    w2scr = dt("moe_w2scr", [9, 2, 128, 32 * 512], BF16, kind=INT).ap()
    x2 = dt("x2", [D, TC], F32, kind="ExternalOutput").ap()

    _steps = []
    def _st0():
        emit_ln(_NCP(nc, 'g1_'), {"x": x0, "xn": xn, "xnf": xnf})
    _steps.append(_st0)
    def _st1():
        emit_l0_moe(_NCP(nc, 'g2_'), {"xn": xn, "xnf": xnf, "w1": w1, "w2": w2, "sw1": sw1,
                         "sw2": sw2, "b1t": b1t, "sb1t": sb1t, "b2": b2, "sb2": sb2,
                         "gwt": gwt, "gb": gb, "g": g0, "w1scr": w1scr, "w2scr": w2scr})
    _steps.append(_st1)
    def _st2():
        emit_l0_qkv(_NCP(nc, 'g3_'), {"xn": xn, "wqt": wqt, "wkt": wkt, "wvt": wvt, "qf": qf,
                         "kf": kf, "ktm": ktm, "vtm": vtm, "stot": stot, "stall": stall})
    _steps.append(_st2)
    def _st4():
        emit_l0_scan(_NCP(nc, 'g5_'), {"qf": qf, "kf": kf, "ktm": ktm, "vtm": vtm, "stall": stall,
                          "pm64": pm64, "g": g0, "x": x0, "wot": wot, "x1": x1,
                          "s_out": s_out})
    _steps.append(_st4)
    def _st5():
        emit_ln(_NCP(nc, 'g6_'), {"x": x1, "xn": xn1, "xnf": xnf1})
    _steps.append(_st5)
    def _st6():
        emit_l1_lat(_NCP(nc, 'g7_'), {"xn1": xn1, "wk": wk1, "wv": wv1, "wdn": wdn,
                         "lat": lat, "latg": latg})
    _steps.append(_st6)
    def _st8():
        emit_l1_attn(_NCP(nc, 'g9_'), {"xn1": xn1, "latf": latg, "lato": lat, "wq": wq1,
                          "wupk": wupk, "wupv": wupv, "posq": posq, "posk": posk,
                          "amask": amask, "bmask": bmask, "biascol": biascol,
                          "s1": s1}, gathered=True)
    _steps.append(_st8)
    def _st9():
        emit_l1_ffn(_NCP(nc, 'g10_'), {"xn1": xn1, "wgt": wgt, "wut": wut, "wdt": wdt, "s1": s1,
                         "wo": wo1, "x1": x1, "x2": x2, "h_scr": h_scr})
    _steps.append(_st9)
    for k, st in enumerate(_steps[:nstages]):
        if k:
            nc.reset()
        st()
    return nc


def _cc(a):
    return np.ascontiguousarray(a)


def kernel_multilaunch(**inputs):
    I = {k: np.asarray(v) for k, v in inputs.items()}
    import ml_dtypes
    BF = ml_dtypes.bfloat16
    cores = list(range(8))

    def slc(c):
        return c // 4, (c % 4) * TC

    nc_ln = build_ln()
    x_slices = [_cc(I["L0_attn_x"][c // 4, (c % 4) * TC:(c % 4 + 1) * TC, :].T)
                for c in cores]
    r = run_bass_kernel_spmd(nc_ln, [{"x": x_slices[c]} for c in cores], core_ids=cores)
    xn = [np.asarray(r.results[c]["xn"]) for c in cores]
    xnf = [np.asarray(r.results[c]["xnf"]) for c in cores]

    nc_moe = build_l0_moe()
    W0 = {"w1": I["L0_ffn_w1"], "w2": I["L0_ffn_w2"], "sw1": I["L0_ffn_sw1"],
          "sw2": I["L0_ffn_sw2"], "b1": I["L0_ffn_b1"], "sb1": I["L0_ffn_sb1"],
          "b2": I["L0_ffn_b2"], "sb2": I["L0_ffn_sb2"], "gate_w": I["L0_ffn_gate_w"],
          "gate_b": I["L0_ffn_gate_b"]}
    im = []
    for c in cores:
        m = _host_moe_inputs(xn[c], W0)
        m["xnf"] = xnf[c]
        im.append(m)
    r = run_bass_kernel_spmd(nc_moe, im, core_ids=cores)
    g0 = [np.asarray(r.results[c]["g"]) for c in cores]

    nc_qkv = build_l0_qkv()
    wqt, wkt = _cc(I["L0_attn_Wq"].T), _cc(I["L0_attn_Wk"].T)
    wvt, wot = _cc(I["L0_attn_Wv"].T), _cc(I["L0_attn_Wo"].T)
    r = run_bass_kernel_spmd(nc_qkv, [{"xn": xn[c], "wqt": wqt, "wkt": wkt,
                                       "wvt": wvt} for c in cores], core_ids=cores)
    qkv = [{k: np.asarray(r.results[c][k]) for k in ("qf", "kf", "ktm", "vtm", "stot")}
           for c in cores]

    nc_scan = build_l0_scan()
    im = []
    for c in cores:
        b, qtr = c // 4, c % 4
        stall = _cc(np.concatenate([qkv[4 * b + rr]["stot"] for rr in range(4)], axis=0))
        pm = np.zeros((128, 4), np.float32)
        pm[:, :qtr] = 1.0
        im.append({"qf": qkv[c]["qf"], "kf": qkv[c]["kf"], "ktm": qkv[c]["ktm"],
                   "vtm": qkv[c]["vtm"], "stall": stall, "pm64": pm, "g": g0[c],
                   "x": x_slices[c], "wot": wot})
    r = run_bass_kernel_spmd(nc_scan, im, core_ids=cores)
    x1 = [np.asarray(r.results[c]["x1"]) for c in cores]

    r = run_bass_kernel_spmd(nc_ln, [{"x": x1[c]} for c in cores], core_ids=cores)
    xn1 = [np.asarray(r.results[c]["xn"]) for c in cores]

    nc_lat = build_l1_lat()
    r = run_bass_kernel_spmd(nc_lat, [{"xn1": xn1[c], "wk": I["L1_attn_Wk"],
                                       "wv": I["L1_attn_Wv"],
                                       "wdn": I["L1_attn_W_down_kv"]} for c in cores],
                             core_ids=cores)
    lats = [np.asarray(r.results[c]["lat"]) for c in cores]
    latf = [_cc(np.concatenate([lats[4 * b + rr] for rr in range(4)], axis=1))
            for b in range(2)]

    nc_attn = build_l1_attn()
    im = []
    posk = np.arange(S, dtype=np.float32)[None, :]
    for c in cores:
        b, qtr = c // 4, c % 4
        cs, ce = qtr * TC, (qtr + 1) * TC
        biascol = np.zeros((128, 64), np.float32)
        for J in range(64):
            pp = J * 128 + np.arange(128)
            biascol[(pp >= cs) & (pp < ce), J] = -88.0
        im.append({"xn1": xn1[c], "latf": latf[b], "lato": lats[c],
                   "wq": I["L1_attn_Wq"], "wupk": I["L1_attn_W_up_k"],
                   "wupv": I["L1_attn_W_up_v"],
                   "posq": np.arange(cs, ce, dtype=np.float32)[None, :],
                   "posk": posk, "amask": (posk < cs).astype(np.float32),
                   "bmask": (posk >= ce).astype(np.float32), "biascol": biascol})
    r = run_bass_kernel_spmd(nc_attn, im, core_ids=cores)
    s1 = [np.asarray(r.results[c]["s1"]) for c in cores]

    nc_ffn = build_l1_ffn()
    wgt, wut = _cc(I["L1_ffn_w_gate"].T), _cc(I["L1_ffn_w_up"].T)
    wdt = _cc(I["L1_ffn_w_down"].T)
    r = run_bass_kernel_spmd(nc_ffn, [{"xn1": xn1[c], "wgt": wgt, "wut": wut,
                                       "wdt": wdt, "s1": s1[c], "wo": I["L1_attn_Wo"],
                                       "x1": x1[c]} for c in cores], core_ids=cores)
    x2 = [np.asarray(r.results[c]["x2"]) for c in cores]

    out = np.empty((2, 8192, D), np.float32)
    for c in cores:
        b, qtr = c // 4, c % 4
        out[b, qtr * TC:(qtr + 1) * TC, :] = x2[c].T
    return out


def kernel(**inputs):
    I = {k: np.asarray(v) for k, v in inputs.items()}
    cores = list(range(8))
    nc = build_fused()
    NP = FF // 128
    shared = {
        "w1": I["L0_ffn_w1"], "w2": I["L0_ffn_w2"], "sw1": I["L0_ffn_sw1"],
        "sw2": I["L0_ffn_sw2"],
        "b1t": _cc(I["L0_ffn_b1"].reshape(E, NP, 128).transpose(0, 2, 1)),
        "sb1t": _cc(I["L0_ffn_sb1"].reshape(NP, 128).T),
        "b2": I["L0_ffn_b2"], "sb2": I["L0_ffn_sb2"].reshape(1, D),
        "gwt": _cc(I["L0_ffn_gate_w"].T), "gb": I["L0_ffn_gate_b"].reshape(E, 1),
        "wqt": _cc(I["L0_attn_Wq"].T), "wkt": _cc(I["L0_attn_Wk"].T),
        "wvt": _cc(I["L0_attn_Wv"].T), "wot": _cc(I["L0_attn_Wo"].T),
        "wk": I["L1_attn_Wk"], "wv": I["L1_attn_Wv"], "wdn": I["L1_attn_W_down_kv"],
        "wq": I["L1_attn_Wq"], "wupk": I["L1_attn_W_up_k"], "wupv": I["L1_attn_W_up_v"],
        "wo": I["L1_attn_Wo"],
        "wgt": _cc(I["L1_ffn_w_gate"].T), "wut": _cc(I["L1_ffn_w_up"].T),
        "wdt": _cc(I["L1_ffn_w_down"].T),
        "posk": np.arange(S, dtype=np.float32)[None, :],
    }
    im = []
    for c in cores:
        b, qtr = c // 4, c % 4
        cs, ce = qtr * TC, (qtr + 1) * TC
        pm = np.zeros((128, 4), np.float32)
        pm[:, :qtr] = 1.0
        posk = shared["posk"]
        biascol = np.zeros((128, S // 128), np.float32)
        for J in range(S // 128):
            pp = J * 128 + np.arange(128)
            biascol[(pp >= cs) & (pp < ce), J] = -88.0
        m = dict(shared)
        m["x0"] = _cc(I["L0_attn_x"][b, cs:ce, :].T)
        m["pm64"] = pm
        m["posq"] = np.arange(cs, ce, dtype=np.float32)[None, :]
        m["amask"] = (posk < cs).astype(np.float32)
        m["bmask"] = (posk >= ce).astype(np.float32)
        m["biascol"] = biascol
        im.append(m)
    r = run_bass_kernel_spmd(nc, im, core_ids=cores)
    out = np.empty((2, 8192, D), np.float32)
    for c in cores:
        b, qtr = c // 4, c % 4
        out[b, qtr * TC:(qtr + 1) * TC, :] = np.asarray(r.results[c]["x2"]).T
    return out
